# version 12
# speedup vs baseline: 1.2351x; 1.0368x over previous
import numpy as np
import concourse.bass as bass
import concourse.mybir as mybir
from concourse.bass_utils import run_bass_kernel_spmd
from concourse.alu_op_type import AluOpType as ALU

F32 = mybir.dt.float32
AF = mybir.ActivationFunctionType
AX = mybir.AxisListType

B, T, D = 16, 4096, 1024
HID = 2048
NS = 16
NSLOT, TOPK = 64, 8
DQ, DH = 256, 512
NCORE = 8
BPC = B // NCORE
TB = 256
NBLK = T // TB
NFC = HID // 128
SEG = TB + 4


class Prog:
    ENGS = ("sync", "tensor", "vector", "scalar", "gpsimd")

    def __init__(self):
        self.ops = []
        self.deps = []
        self.writer = {}
        self.readers = {}
        self.fence = set()

    def op(self, eng, builder, reads=(), writes=(), dma=False, tag="", qsem=None):
        i = len(self.ops)
        if dma and qsem is None:
            qsem = writes[0] if writes else (tag or "dma")
        rb = [k for k in reads if k.startswith("PSB")]
        if rb:
            reads = [k for k in reads if not k.startswith("PSB")]
            writes = list(writes) + rb
        dep = set(self.fence)
        for k in reads:
            w = self.writer.get(k)
            if w is not None:
                dep.add(w)
        for k in writes:
            for r in self.readers.get(k, ()):
                dep.add(r)
            w = self.writer.get(k)
            if w is not None:
                dep.add(w)
        dep.discard(i)
        self.ops.append((eng, builder, dma, tag, qsem))
        self.deps.append(dep)
        for k in reads:
            self.readers.setdefault(k, []).append(i)
        for k in writes:
            self.writer[k] = i
            self.readers[k] = []
        return i

    def barrier(self):
        last = {}
        for i, (eng, _b, isdma, _t, qs) in enumerate(self.ops):
            last[(eng, isdma, qs)] = i
        self.fence = set(last.values())

    def emit(self, block, mksem, final_eng="sync"):
        if TRUNC_OPS is not None:
            self.ops = self.ops[:TRUNC_OPS]
            self.deps = [set(x for x in d if x < TRUNC_OPS)
                         for d in self.deps[:TRUNC_OPS]]
        n = len(self.ops)
        eng_of = [o[0] for o in self.ops]
        isdma = [o[2] for o in self.ops]
        qsem_of = [o[4] for o in self.ops]
        publish = [False] * n
        for j in range(n):
            for i in self.deps[j]:
                if (FULL_SYNC or eng_of[i] != eng_of[j] or isdma[i] or isdma[j]
                        or eng_of[i] in SYNC_ENGS):
                    publish[i] = True
        for i in range(n):
            if isdma[i]:
                publish[i] = True
        doms = []
        dom = [None] * n
        for i in range(n):
            d = ("D", qsem_of[i]) if isdma[i] else ("E", eng_of[i])
            dom[i] = d
            if d not in doms:
                doms.append(d)
        sems = {d: mksem(f"{d[0]}_{d[1]}"[:28]) for d in doms}
        pubcount = {}
        val_after = [0] * n
        for i in range(n):
            if publish[i]:
                pubcount[dom[i]] = pubcount.get(dom[i], 0) + (16 if isdma[i] else 1)
            val_after[i] = pubcount.get(dom[i], 0)
        finals = dict(pubcount)

        def section(engname):
            def go(eng):
                waited = {}
                for i in range(n):
                    if eng_of[i] != engname:
                        continue
                    need = {}
                    for j in self.deps[i]:
                        if not publish[j]:
                            continue
                        if (not FULL_SYNC and dom[j] == ("E", engname)
                                and engname not in SYNC_ENGS and not isdma[i]):
                            continue
                        v = val_after[j]
                        if need.get(dom[j], 0) < v:
                            need[dom[j]] = v
                    for d, v in sorted(need.items(), key=lambda kv: str(kv[0])):
                        if waited.get(d, 0) < v:
                            eng.wait_ge(sems[d], v)
                            waited[d] = v
                    if ((engname in NOP_ENGS or any(t in self.ops[i][3] for t in NOP_TAGS)
                         or (NOP_TAGS and "UNTAGGED" in NOP_TAGS and engname == "scalar"
                             and not self.ops[i][3])) and not isdma[i]):
                        inst = eng.nop()
                    else:
                        inst = self.ops[i][1](eng)
                    if publish[i]:
                        inst.then_inc(sems[dom[i]], 16 if isdma[i] else 1)
                if engname == final_eng:
                    for d, v in sorted(finals.items(), key=lambda kv: str(kv[0])):
                        if d[0] == "D":
                            eng.wait_ge(sems[d], v)
            return go

        block.sync(section("sync"))
        block.tensor(section("tensor"))
        block.vector(section("vector"))
        block.scalar(section("scalar"))
        block.gpsimd(section("gpsimd"))


class Pool:
    def __init__(self, ap, n):
        self.ap, self.n, self.off, self.marks = ap, n, 0, []

    def alloc(self, cols, parts=128):
        cols_al = (cols + 15) & ~15
        assert self.off + cols_al <= self.n, f"SBUF overflow {self.off}+{cols_al}>{self.n}"
        a = self.ap[0:parts, self.off:self.off + cols]
        self.off += cols_al
        return a

    def push(self):
        self.marks.append(self.off)

    def pop(self):
        self.off = self.marks.pop()

    def raw(self, p0, p1, c0, c1):
        return self.ap[p0:p1, c0:c1]


PHASES = "ABC"
LIMIT_BLOCKS = None
TRUNC_OPS = None
N_CORES_OVERRIDE = None
NOP_ENGS = ()
NOP_TAGS = ()
DBG = {}
DBG_DUMP = None
FULL_SYNC = False
RACE_DETECT = False
SYNC_ENGS = ("gpsimd", "vector", "scalar")


def build_program():
    nc = bass.Bass("TRN2", target_bir_lowering=False, detect_race_conditions=RACE_DETECT)
    dt = nc.dram_tensor

    x_in = dt("x", [BPC, T, D], F32, kind="ExternalInput").ap()
    out_hbm = dt("out", [BPC, T, D], F32, kind="ExternalOutput").ap()
    ascr = dt("ascr", [BPC, T, NSLOT], F32, kind="Internal").ap()

    w_inT = dt("w_inT", [32, 128, 8 * 128], F32, kind="ExternalInput").ap()
    conv_w_c = dt("conv_w_c", [128, NFC * 4], F32, kind="ExternalInput").ap()
    conv_b_c = dt("conv_b_c", [128, NFC], F32, kind="ExternalInput").ap()
    w_gateT = dt("w_gateT", [HID, 96], F32, kind="ExternalInput").ap()
    w_deltaT = dt("w_deltaT", [64, HID], F32, kind="ExternalInput").ap()
    dbias_c = dt("dbias_c", [128, NFC], F32, kind="ExternalInput").ap()
    alog_c = dt("alog_c", [128, NFC * NS], F32, kind="ExternalInput").ap()
    dvec_c = dt("dvec_c", [128, NFC], F32, kind="ExternalInput").ap()
    w_outT = dt("w_outT", [HID, D], F32, kind="ExternalInput").ap()

    slot_init = dt("slot_init", [NSLOT, D], F32, kind="ExternalInput").ap()
    slot_initT = dt("slot_initT", [D, NSLOT], F32, kind="ExternalInput").ap()
    w1a_d = dt("w1a", [D, DH], F32, kind="ExternalInput").ap()
    w1b_d = dt("w1b", [D, DH], F32, kind="ExternalInput").ap()
    b1_c = dt("b1_c", [128, 4], F32, kind="ExternalInput").ap()
    w2_d = dt("w2", [DH, NSLOT], F32, kind="ExternalInput").ap()
    b2_c = dt("b2_c", [NSLOT, 1], F32, kind="ExternalInput").ap()
    tau_d = dt("tau", [1, 1], F32, kind="ExternalInput").ap()
    wihT = dt("wihT", [D, 3 * D], F32, kind="ExternalInput").ap()
    whhT = dt("whhT", [D, 3 * D], F32, kind="ExternalInput").ap()
    bih_c = dt("bih_c", [128, 24], F32, kind="ExternalInput").ap()
    bhh_c = dt("bhh_c", [128, 24], F32, kind="ExternalInput").ap()
    wq_d = dt("wq", [D, DQ], F32, kind="ExternalInput").ap()
    wk_d = dt("wk", [D, DQ], F32, kind="ExternalInput").ap()
    bqk_c = dt("bqk_c", [128, 4], F32, kind="ExternalInput").ap()
    wv_d = dt("wv", [D, D], F32, kind="ExternalInput").ap()
    bv_r = dt("bv_r", [NSLOT, D], F32, kind="ExternalInput").ap()
    lng_r = dt("lng_r", [NSLOT, D], F32, kind="ExternalInput").ap()
    lnb_r = dt("lnb_r", [NSLOT, D], F32, kind="ExternalInput").ap()
    wvp_d = dt("wvp", [D, D], F32, kind="ExternalInput").ap()
    bvp_r = dt("bvp_r", [NSLOT, D], F32, kind="ExternalInput").ap()
    wo_d = dt("wo", [D, D], F32, kind="ExternalInput").ap()
    bo_r = dt("bo_r", [1, D], F32, kind="ExternalInput").ap()

    import contextlib
    st = contextlib.ExitStack()
    with st:
        POOLN = 53200
        pool_t = st.enter_context(nc.sbuf_tensor("pool", [128, POOLN], F32))
        sbp = Pool(pool_t.ap(), POOLN)
        psum_t = st.enter_context(nc.psum_tensor("ppool", [128, 4096], F32))
        ps = psum_t.ap()

        def R(c0, c1, p0=0, p1=128):
            return ps[p0:p1, c0:c1], ["PSB%d" % i for i in range(c0 // 512, (c1 + 511) // 512)]

        def mksem(name):
            return st.enter_context(nc.semaphore(name))

        blk_ctx = st.enter_context(nc.Block())
        P = Prog()

        ident = sbp.alloc(128)
        ones = sbp.alloc(128)
        konst = sbp.alloc(4)
        cwc = sbp.alloc(NFC * 4)
        cbc = sbp.alloc(NFC)
        dbc = sbp.alloc(NFC)
        acol = sbp.alloc(NFC * NS)
        dcolv = sbp.alloc(NFC)
        wgT_b = sbp.alloc(NFC * 96)
        wdT_b = sbp.alloc(HID, parts=64)
        si_sb2 = [sbp.alloc(D, parts=64) for _ in range(BPC)]
        dncol = sbp.alloc(2, parts=64)
        sv0_off = sbp.off
        sv_sb = [sbp.alloc(D, parts=64) for _ in range(BPC)]
        bo_row = sbp.raw(64, 65, sv0_off, sv0_off + D)
        DBG["sv0"] = sv0_off
        carry = [sbp.alloc(NS * NFC) for _ in range(BPC)]
        hist_t = sbp.alloc(NFC * 3)

        P.op("gpsimd", lambda e: e.memset(ident[:], 0.0), writes=["ident"])
        P.op("gpsimd", lambda e: e.affine_select(
            out=ident[:], in_=ident[:], compare_op=ALU.not_equal,
            fill=1.0, base=0, pattern=[[-1, 128]], channel_multiplier=1),
            reads=["ident"], writes=["ident"])
        P.op("gpsimd", lambda e: e.memset(ones[:], 1.0), writes=["ones"])
        P.op("gpsimd", lambda e: e.memset(konst[:, 0:1], 1e-5), writes=["konst"])
        P.op("gpsimd", lambda e: e.memset(konst[:, 1:2], -0.5), writes=["konst"])
        for nm, dst, src in [("cwc", cwc, conv_w_c), ("cbc", cbc, conv_b_c),
                             ("dbc", dbc, dbias_c), ("dcolv", dcolv, dvec_c),
                             ("bo_row", bo_row, bo_r)]:
            P.op("sync", lambda e, d=dst, s=src: e.dma_start(out=d, in_=s),
                 writes=[nm], dma=True)
        P.op("sync", lambda e: e.dma_start(out=acol, in_=alog_c), writes=["acol"], dma=True)
        P.op("scalar", lambda e: e.activation(out=acol, in_=acol, func=AF.Exp),
             reads=["acol"], writes=["acol"])
        P.op("vector", lambda e: e.tensor_scalar(out=acol, in0=acol, scalar1=-1.0,
                                                 scalar2=None, op0=ALU.mult),
             reads=["acol"], writes=["acol"])
        P.op("sync", lambda e: e.dma_start(
            out=wgT_b.rearrange("p (c n) -> p c n", n=96),
            in_=w_gateT.rearrange("(c p) n -> p c n", p=128)), writes=["wgT"], dma=True)
        P.op("sync", lambda e: e.dma_start(out=wdT_b, in_=w_deltaT), writes=["wdT"], dma=True)

        def layernorm(xt, xn_out, cols8, scratch, key, rk, scr_key, xtra=()):
            s_ = cols8
            P.op("vector", lambda e: e.reduce_sum(out=s_[:, 0:1], in_=xt, axis=AX.X),
                 reads=[rk], writes=[key + "s", "lncol" + scr_key])
            P.op("vector", lambda e: e.tensor_tensor(out=scratch, in0=xt, in1=xt,
                                                     op=ALU.mult),
                 reads=[rk], writes=["lnscr" + scr_key] + list(xtra))
            P.op("vector", lambda e: e.reduce_sum(out=s_[:, 1:2], in_=scratch, axis=AX.X),
                 reads=["lnscr" + scr_key], writes=[key + "q"])
            P.op("vector", lambda e: e.tensor_scalar(out=s_[:, 0:1], in0=s_[:, 0:1],
                                                     scalar1=1.0 / D, scalar2=None, op0=ALU.mult),
                 reads=[key + "s"], writes=[key + "m"])
            P.op("vector", lambda e: e.tensor_tensor(out=s_[:, 2:3], in0=s_[:, 0:1],
                                                     in1=s_[:, 0:1], op=ALU.mult),
                 reads=[key + "m"], writes=[key + "m2"])
            P.op("vector", lambda e: e.tensor_scalar(out=s_[:, 1:2], in0=s_[:, 1:2],
                                                     scalar1=1.0 / D, scalar2=None, op0=ALU.mult),
                 reads=[key + "q"], writes=[key + "q2"])
            P.op("vector", lambda e: e.tensor_tensor(out=s_[:, 1:2], in0=s_[:, 1:2],
                                                     in1=s_[:, 2:3], op=ALU.subtract),
                 reads=[key + "q2", key + "m2"], writes=[key + "v"])
            P.op("scalar", lambda e: e.activation(out=s_[:, 1:2], in_=s_[:, 1:2],
                                                  func=AF.Ln, bias=konst[:, 0:1]),
                 reads=[key + "v", "konst"], writes=[key + "l"])
            P.op("scalar", lambda e: e.activation(out=s_[:, 1:2], in_=s_[:, 1:2],
                                                  func=AF.Exp, scale=konst[:, 1:2]),
                 reads=[key + "l", "konst"], writes=[key + "i"])
            P.op("vector", lambda e: e.tensor_tensor(out=s_[:, 2:3], in0=s_[:, 0:1],
                                                     in1=s_[:, 1:2], op=ALU.mult),
                 reads=[key + "m", key + "i"], writes=[key + "mi"])
            P.op("vector", lambda e: e.tensor_scalar(out=xn_out, in0=xt,
                                                     scalar1=s_[:, 1:2], scalar2=s_[:, 2:3],
                                                     op0=ALU.mult, op1=ALU.subtract),
                 reads=[rk, key + "i", key + "mi"], writes=[key + "xn"] + list(xtra))

        sbp.push()
        b1tot = sbp.alloc(4)
        b2col = sbp.alloc(1, parts=64)
        invt64 = sbp.alloc(1, parts=64)
        tau_sb = sbp.alloc(4, parts=1)
        P.op("sync", lambda e: e.dma_start(out=b2col, in_=b2_c), writes=["b2col"], dma=True)
        P.op("sync", lambda e: e.dma_start(out=tau_sb[0:1, 0:1], in_=tau_d),
             writes=["tau"], dma=True)
        P.op("vector", lambda e: e.tensor_scalar(out=tau_sb[0:1, 1:2], in0=tau_sb[0:1, 0:1],
                                                 scalar1=-1.0, scalar2=None, op0=ALU.mult),
             reads=["tau"], writes=["tau"])
        P.op("vector", lambda e: e.tensor_tensor(out=tau_sb[0:1, 2:3], in0=tau_sb[0:1, 0:1],
                                                 in1=tau_sb[0:1, 1:2], op=ALU.max),
             reads=["tau"], writes=["tau"])
        P.op("vector", lambda e: e.tensor_scalar(out=tau_sb[0:1, 2:3], in0=tau_sb[0:1, 2:3],
                                                 scalar1=0.1, scalar2=None, op0=ALU.add),
             reads=["tau"], writes=["tau"])
        P.op("vector", lambda e: e.reciprocal(out=tau_sb[0:1, 3:4], in_=tau_sb[0:1, 2:3]),
             reads=["tau"], writes=["tau"])
        P.op("sync", lambda e: e.dma_start(
            out=invt64, in_=tau_sb[0:1, 3:4].unsqueeze(1).broadcast_to((1, 64, 1))),
            reads=["tau"], writes=["invt64"], dma=True)
        w1a_b = sbp.alloc(8 * DH)
        w1b_b = sbp.alloc(8 * DH)
        w2_b = sbp.alloc(4 * NSLOT)
        si_sp = sbp.alloc(D, parts=64)
        smT = sbp.alloc(8)
        bias2 = sbp.alloc(DH, parts=1)
        P.op("sync", lambda e: e.dma_start(
            out=w1a_b.rearrange("p (k f) -> p k f", f=DH),
            in_=w1a_d.rearrange("(k p) f -> p k f", p=128)), writes=["w1a"], dma=True)
        P.op("sync", lambda e: e.dma_start(
            out=w1b_b.rearrange("p (k f) -> p k f", f=DH),
            in_=w1b_d.rearrange("(k p) f -> p k f", p=128)), writes=["w1b"], dma=True)
        P.op("sync", lambda e: e.dma_start(
            out=w2_b.rearrange("p (k f) -> p k f", f=NSLOT),
            in_=w2_d.rearrange("(k p) f -> p k f", p=128)), writes=["w2"], dma=True)
        P.op("sync", lambda e: e.dma_start(out=si_sp, in_=slot_init), writes=["si_sp"], dma=True)
        P.op("sync", lambda e: e.dma_start(out=b1tot, in_=b1_c), writes=["b1tot"], dma=True)

        psm, psmK = R(3584, 3592)
        for k in range(8):
            P.op("tensor", lambda e, k=k: e.matmul(
                psm[:, k:k + 1], lhsT=si_sp[:, k * 128:(k + 1) * 128],
                rhs=ones[0:64, 0:1], start=True, stop=True, skip_group_check=True),
                reads=["si_sp", "ones"], writes=psmK)
        P.op("vector", lambda e: e.tensor_scalar(out=smT, in0=psm, scalar1=1.0 / 64,
                                                 scalar2=None, op0=ALU.mult),
             reads=psmK, writes=["smT"])
        pb2, pb2K = R(512, 1024, p1=1)
        for k in range(8):
            P.op("tensor", lambda e, k=k: e.matmul(
                pb2, lhsT=smT[:, k:k + 1], rhs=w1b_b[:, k * DH:(k + 1) * DH],
                start=(k == 0), stop=(k == 7), skip_group_check=True),
                reads=["smT", "w1b"], writes=pb2K)
        P.op("vector", lambda e: e.tensor_copy(bias2, pb2), reads=pb2K, writes=["bias2"])
        pbt, pbtK = R(3592, 3596)
        for m in range(4):
            P.op("tensor", lambda e, m=m: e.transpose(
                pbt[:, m:m + 1], bias2[0:1, m * 128:(m + 1) * 128], ident[0:1, 0:1]),
                reads=["bias2", "ident"], writes=pbtK)
        P.op("vector", lambda e: e.tensor_tensor(out=b1tot, in0=b1tot, in1=pbt, op=ALU.add),
             reads=pbtK + ["b1tot"], writes=["b1tot"])

        xA = sbp.alloc(2 * D)
        xnA = sbp.alloc(2 * D)
        lnscA = sbp.alloc(D)
        lncolA = sbp.alloc(8)
        xnTA = sbp.alloc(8 * TB)
        h_sb = sbp.alloc(4 * TB)
        lg_sb = sbp.alloc(TB, parts=64)
        lgT = sbp.alloc(2 * NSLOT)
        wkt = sbp.alloc(NSLOT)
        mcol = sbp.alloc(16)
        amt = sbp.alloc(NSLOT)
        alset = sbp.alloc(NSLOT)
        for _n, _a in [("xA", xA), ("xnA", xnA), ("lnscA", lnscA), ("lncolA", lncolA),
                       ("xnTA", xnTA), ("h_sb", h_sb), ("lg_sb", lg_sb), ("lgT", lgT),
                       ("wkt", wkt), ("mcol", mcol), ("amt", amt), ("alset", alset),
                       ("w1a", w1a_b), ("b1tot", b1tot), ("invt64", invt64), ("tau", tau_sb)]:
            DBG[_n] = _a.offset

        ps_hm = [R(0, 256), R(512, 768), R(256, 512), R(768, 1024)]
        ps_si = [R(1024, 2048, p1=64) for b in range(BPC)]
        ps_dn = [R(3072, 3080, p1=64) for b in range(BPC)]
        ps_lg = R(3584, 3840, p1=64)
        ps_tpA = [R(2048, 2176), R(2560, 2688)]

        def phaseA_block(b, blk):
            t0 = blk * TB
            key = f"A{b}_{blk}"
            P.op("sync", lambda e: e.dma_start(
                out=xA.rearrange("p (s d) -> p s d", d=D),
                in_=x_in[b, t0:t0 + TB, :].rearrange("(s p) d -> p s d", p=128)),
                writes=["xA"], dma=True, tag=key + "xload")
            P.op("scalar", lambda e: e.dma_start(
                out=out_hbm[b, t0:t0 + TB, :].rearrange("(s p) d -> p s d", p=128),
                in_=xA.rearrange("p (s d) -> p s d", d=D)),
                reads=["xA"], dma=True, tag=key + "xstore", qsem="store")
            for sub in range(2):
                xs = xA[:, sub * D:(sub + 1) * D]
                xn = xnA[:, sub * D:(sub + 1) * D]
                layernorm(xs, xn, lncolA, lnscA, key + f"ln{sub}", "xA", "A")
                for k in range(8):
                    tp, tpK = ps_tpA[k % 2]
                    P.op("tensor", lambda e, k=k, xn=xn, tp=tp: e.transpose(
                        tp[:, 0:128], xn[:, k * 128:(k + 1) * 128], ident[:]),
                        reads=[key + f"ln{sub}xn", "ident"], writes=tpK)
                    P.op("vector", lambda e, k=k, sub=sub, tp=tp: e.tensor_copy(
                        xnTA[:, k * TB + sub * 128: k * TB + sub * 128 + 128], tp[:, 0:128]),
                        reads=tpK, writes=[f"xnTA_{k}_{sub}"])
            xnT_keys = [f"xnTA_{k}_{s}" for k in range(8) for s in range(2)]
            for m in range(4):
                pm, pmK = ps_hm[m]
                for k in range(8):
                    P.op("tensor", lambda e, m=m, k=k, pm=pm: e.matmul(
                        pm, lhsT=w1a_b[:, k * DH + m * 128: k * DH + m * 128 + 128],
                        rhs=xnTA[:, k * TB:(k + 1) * TB],
                        start=(k == 0), stop=(k == 7), skip_group_check=True),
                        reads=["w1a"] + xnT_keys, writes=pmK, tag=key + "w1")
                P.op("scalar", lambda e, m=m, pm=pm: e.activation(
                    out=h_sb[:, m * TB:(m + 1) * TB], in_=pm, func=AF.Gelu,
                    bias=b1tot[:, m:m + 1]),
                    reads=pmK + ["b1tot"], writes=[f"h{m}"], tag=key + "gelu")
            plg, plgK = ps_lg
            for k in range(4):
                P.op("tensor", lambda e, k=k, plg=plg: e.matmul(
                    plg, lhsT=w2_b[:, k * NSLOT:(k + 1) * NSLOT],
                    rhs=h_sb[:, k * TB:(k + 1) * TB],
                    start=(k == 0), stop=(k == 3), skip_group_check=True),
                    reads=["w2", f"h{k}"], writes=plgK, tag=key + "w2")
            P.op("vector", lambda e, plg=plg: e.tensor_scalar(
                out=lg_sb, in0=plg, scalar1=b2col, scalar2=invt64,
                op0=ALU.add, op1=ALU.mult),
                reads=plgK + ["b2col", "invt64"], writes=["lg"], tag=key + "lgevac")
            for sub in range(2):
                tp, tpK = ps_tpA[sub]
                P.op("tensor", lambda e, sub=sub, tp=tp: e.transpose(
                    tp[:, 0:NSLOT], lg_sb[:, sub * 128:(sub + 1) * 128], ident[0:64, 0:64]),
                    reads=["lg", "ident"], writes=tpK)
                P.op("vector", lambda e, sub=sub, tp=tp: e.tensor_copy(
                    lgT[:, sub * NSLOT:(sub + 1) * NSLOT], tp[:, 0:NSLOT]),
                    reads=tpK, writes=[f"lgT{sub}"])
            for sub in range(2):
                lgs = lgT[:, sub * NSLOT:(sub + 1) * NSLOT]
                P.op("vector", lambda e, lgs=lgs: e.tensor_copy(wkt, lgs),
                     reads=[f"lgT{sub}"], writes=["wkt"])
                for it in range(TOPK):
                    P.op("vector", lambda e, it=it: e.reduce_max(
                        out=mcol[:, it:it + 1], in_=wkt, axis=AX.X, negate=True),
                        reads=["wkt"], writes=[f"mx{it}", "mcol"])
                    if it < TOPK - 1:
                        P.op("vector", lambda e, it=it: e.tensor_scalar(
                            out=amt, in0=wkt, scalar1=mcol[:, it:it + 1],
                            scalar2=0.0, op0=ALU.add, op1=ALU.is_ge),
                            reads=["wkt", f"mx{it}"], writes=["amt"])
                        P.op("vector", lambda e: e.scalar_tensor_tensor(
                            out=wkt, in0=amt, scalar=-1e38, in1=wkt,
                            op0=ALU.mult, op1=ALU.add),
                            reads=["amt", "wkt"], writes=["wkt"])
                P.op("scalar", lambda e, lgs=lgs: e.activation(
                    out=amt, in_=lgs, func=AF.Exp, bias=mcol[:, 0:1]),
                    reads=[f"lgT{sub}", "mx0"], writes=["amt"], tag=key + "aexp")
                P.op("vector", lambda e, lgs=lgs: e.tensor_scalar(
                    out=wkt, in0=lgs, scalar1=mcol[:, TOPK - 1:TOPK],
                    scalar2=0.0, op0=ALU.add, op1=ALU.is_ge),
                    reads=[f"lgT{sub}", f"mx{TOPK - 1}"], writes=["wkt"])
                P.op("vector", lambda e: e.tensor_tensor(
                    out=alset, in0=amt, in1=wkt, op=ALU.mult),
                    reads=["amt", "wkt"], writes=["alset"])
                P.op("vector", lambda e: e.reduce_sum(out=mcol[:, 8:9], in_=alset, axis=AX.X),
                     reads=["alset"], writes=["aZ", "mcol"])
                P.op("vector", lambda e: e.reciprocal(out=mcol[:, 9:10], in_=mcol[:, 8:9]),
                     reads=["aZ"], writes=["aZr", "mcol"])
                P.op("vector", lambda e: e.tensor_scalar(
                    out=alset, in0=alset, scalar1=mcol[:, 9:10], scalar2=None, op0=ALU.mult),
                    reads=["alset", "aZr"], writes=["alset"], tag=key + "anorm")
                P.op("scalar", lambda e, b=b, t0=t0, sub=sub: e.dma_start(
                    out=ascr[b, t0 + sub * 128: t0 + sub * 128 + 128, :], in_=alset),
                    reads=["alset"], dma=True, tag=key + "astore", qsem="astore")
                first = (blk == 0 and sub == 0)
                last = (blk == (LIMIT_BLOCKS or NBLK) - 1 and sub == 1)
                xn = xnA[:, sub * D:(sub + 1) * D]
                psi, psiK = ps_si[b]
                for half in range(2):
                    P.op("tensor", lambda e, xn=xn, half=half, first=first, last=last, psi=psi: e.matmul(
                        psi[:, half * 512:(half + 1) * 512], lhsT=alset,
                        rhs=xn[:, half * 512:(half + 1) * 512],
                        start=first, stop=last, skip_group_check=True),
                        reads=["alset", key + f"ln{sub}xn"], writes=psiK, tag=key + "simm")
                pdn, pdnK = ps_dn[b]
                P.op("tensor", lambda e, first=first, last=last, pdn=pdn: e.matmul(
                    pdn, lhsT=alset, rhs=ones[:, 0:8],
                    start=first, stop=last, skip_group_check=True),
                    reads=["alset", "ones"], writes=pdnK, tag=key + "dnmm")

        nblk_a = LIMIT_BLOCKS or NBLK
        for b in range(BPC):
            for blk in range(nblk_a):
                phaseA_block(b, blk)
            P.op("vector", lambda e, b=b: e.tensor_copy(dncol[:, b:b + 1], ps_dn[b][0][:, 0:1]),
                 reads=ps_dn[b][1], writes=[f"dncol{b}", "dncol"])
            P.op("vector", lambda e, b=b: e.tensor_copy(si_sb2[b], ps_si[b][0]),
                 reads=ps_si[b][1], writes=[f"siraw{b}"])
        sbp.pop()
        P.barrier()
        if "B" not in PHASES:
            if DBG_DUMP is not None:
                c0, c1 = DBG_DUMP
                dbg_t = dt("dbg", [128, c1 - c0], F32, kind="ExternalOutput").ap()
                P.barrier()
                P.op("sync", lambda e: e.dma_start(out=dbg_t, in_=sbp.raw(0, 128, c0, c1)),
                     dma=True, qsem="dbgdump")
            P.emit(blk_ctx, mksem, final_eng="sync")
            return nc

        sbp.push()
        s0T_b = sbp.alloc(8 * NSLOT)
        bihc = sbp.alloc(24)
        bhhc = sbp.alloc(24)
        bqkc = sbp.alloc(4)
        wm_b = [sbp.alloc(8 * 128) for _ in range(2)]
        wq_b = sbp.alloc(8 * DQ)
        wk_b = sbp.alloc(8 * DQ)
        wv_b = sbp.alloc(8 * D)
        wvp_b = sbp.alloc(8 * D)
        gh_sb = sbp.alloc(24 * NSLOT)
        gi_sb = sbp.alloc(24 * NSLOT)
        r_sb = sbp.alloc(8 * NSLOT)
        z_sb = sbp.alloc(8 * NSLOT)
        n_sb = sbp.alloc(8 * NSLOT)
        snew = sbp.alloc(8 * NSLOT)
        snewT = sbp.alloc(D, parts=64)
        q_sb = sbp.alloc(2 * NSLOT)
        k_sb = sbp.alloc(2 * NSLOT)
        vT_sb = sbp.alloc(D, parts=64)
        sc_sb = sbp.alloc(NSLOT, parts=64)
        at_sb = sbp.alloc(NSLOT, parts=64)
        y_sb = sbp.alloc(D, parts=64)
        yn_sb = sbp.alloc(D, parts=64)
        ynT_sb = sbp.alloc(NSLOT)
        siT = sbp.alloc(8 * NSLOT)
        colB = sbp.alloc(8, parts=64)
        scrB = sbp.alloc(D, parts=64)
        ggb = sbp.alloc(D, parts=64)
        for _n, _a in [("s0T", s0T_b), ("gh", gh_sb), ("gi", gi_sb), ("r", r_sb),
                       ("z", z_sb), ("n", n_sb), ("snew", snew), ("snewT", snewT),
                       ("q", q_sb), ("k", k_sb), ("vT", vT_sb), ("sc", sc_sb),
                       ("at", at_sb), ("y", y_sb), ("yn", yn_sb), ("siT", siT),
                       ("scrB", scrB), ("si0", si_sb2[0]), ("si1", si_sb2[1])]:
            DBG[_n] = _a.offset

        P.op("sync", lambda e: e.dma_start(
            out=s0T_b.rearrange("p (k f) -> p k f", f=NSLOT),
            in_=slot_initT.rearrange("(k p) f -> p k f", p=128)), writes=["s0T"], dma=True)
        P.op("sync", lambda e: e.dma_start(out=bihc, in_=bih_c), writes=["bihc"], dma=True)
        P.op("sync", lambda e: e.dma_start(out=bhhc, in_=bhh_c), writes=["bhhc"], dma=True)
        P.op("sync", lambda e: e.dma_start(out=bqkc, in_=bqk_c), writes=["bqkc"], dma=True)
        P.op("sync", lambda e: e.dma_start(
            out=wq_b.rearrange("p (k f) -> p k f", f=DQ),
            in_=wq_d.rearrange("(k p) f -> p k f", p=128)), writes=["wqb"], dma=True)
        P.op("sync", lambda e: e.dma_start(
            out=wk_b.rearrange("p (k f) -> p k f", f=DQ),
            in_=wk_d.rearrange("(k p) f -> p k f", p=128)), writes=["wkb"], dma=True)
        P.op("sync", lambda e: e.dma_start(
            out=wv_b.rearrange("p (k f) -> p k f", f=D),
            in_=wv_d.rearrange("(k p) f -> p k f", p=128)), writes=["wvb"], dma=True)
        P.op("sync", lambda e: e.dma_start(
            out=wvp_b.rearrange("p (k f) -> p k f", f=D),
            in_=wvp_d.rearrange("(k p) f -> p k f", p=128)), writes=["wvpb"], dma=True)

        for b in range(BPC):
            P.op("vector", lambda e, b=b: e.tensor_scalar(
                out=colB[:, 0:1], in0=dncol[:, b:b + 1], scalar1=1e-8, scalar2=None, op0=ALU.add),
                reads=[f"dncol{b}"], writes=[f"B{b}_dn1", "colB"])
            P.op("vector", lambda e, b=b: e.reciprocal(out=colB[:, 1:2], in_=colB[:, 0:1]),
                 reads=[f"B{b}_dn1"], writes=[f"B{b}_dnr", "colB"])
            P.op("vector", lambda e, b=b: e.tensor_scalar(
                out=si_sb2[b], in0=si_sb2[b], scalar1=colB[:, 1:2], scalar2=None, op0=ALU.mult),
                reads=[f"siraw{b}", f"B{b}_dnr"], writes=[f"B{b}_si"])

        pgh, pghK = R(0, 64)
        pgi, pgiK = R(64, 128)
        pq, pqK = R(128, 192)
        ptB, ptBK = R(256, 384)
        psc, pscK = R(384, 448, p1=64)
        pvT, pvTK = R(1024, 2048, p1=64)
        pcm, pcmK = R(2048, 3072, p1=64)

        for m in range(24):
            wb = wm_b[m % 2]
            P.op("sync", lambda e, m=m, wb=wb: e.dma_start(
                out=wb.rearrange("p (k f) -> p k f", f=128),
                in_=whhT[:, m * 128:(m + 1) * 128].rearrange("(k p) f -> p k f", p=128)),
                writes=[f"wm{m % 2}"], dma=True)
            for k in range(8):
                P.op("tensor", lambda e, k=k, wb=wb: e.matmul(
                    pgh, lhsT=wb[:, k * 128:(k + 1) * 128],
                    rhs=s0T_b[:, k * NSLOT:(k + 1) * NSLOT],
                    start=(k == 0), stop=(k == 7), skip_group_check=True),
                    reads=[f"wm{m % 2}", "s0T"], writes=pghK)
            P.op("vector", lambda e, m=m: e.tensor_copy(
                gh_sb[:, m * NSLOT:(m + 1) * NSLOT], pgh),
                reads=pghK, writes=[f"gh{m}"])

        for b in range(BPC):
            kb = f"B{b}_"
            for k in range(8):
                P.op("tensor", lambda e, k=k, b=b: e.transpose(
                    ptB[:, 0:64], si_sb2[b][:, k * 128:(k + 1) * 128], ident[0:64, 0:64]),
                    reads=[kb + "si", "ident"], writes=ptBK)
                P.op("vector", lambda e, k=k: e.tensor_copy(
                    siT[:, k * NSLOT:(k + 1) * NSLOT], ptB[:, 0:64]),
                    reads=ptBK, writes=[kb + f"siT{k}"])
            siT_keys = [kb + f"siT{k}" for k in range(8)]
            for m in range(24):
                wb = wm_b[m % 2]
                P.op("sync", lambda e, m=m, wb=wb: e.dma_start(
                    out=wb.rearrange("p (k f) -> p k f", f=128),
                    in_=wihT[:, m * 128:(m + 1) * 128].rearrange("(k p) f -> p k f", p=128)),
                    writes=[f"wm{m % 2}"], dma=True)
                for k in range(8):
                    P.op("tensor", lambda e, k=k, wb=wb: e.matmul(
                        pgi, lhsT=wb[:, k * 128:(k + 1) * 128],
                        rhs=siT[:, k * NSLOT:(k + 1) * NSLOT],
                        start=(k == 0), stop=(k == 7), skip_group_check=True),
                        reads=[f"wm{m % 2}"] + siT_keys, writes=pgiK)
                P.op("vector", lambda e, m=m: e.tensor_copy(
                    gi_sb[:, m * NSLOT:(m + 1) * NSLOT], pgi),
                    reads=pgiK, writes=[kb + f"gi{m}"])
            for out_sb, base in ((r_sb, 0), (z_sb, 8)):
                for m8 in range(8):
                    m = base + m8
                    sl = slice(m8 * NSLOT, (m8 + 1) * NSLOT)
                    P.op("vector", lambda e, m=m, sl=sl, o=out_sb: e.scalar_tensor_tensor(
                        out=o[:, sl], in0=gi_sb[:, m * NSLOT:(m + 1) * NSLOT],
                        scalar=bihc[:, m:m + 1], in1=gh_sb[:, m * NSLOT:(m + 1) * NSLOT],
                        op0=ALU.add, op1=ALU.add),
                        reads=[kb + f"gi{m}", f"gh{m}", "bihc"], writes=[kb + f"g1_{m}"])
                    P.op("vector", lambda e, m=m, sl=sl, o=out_sb: e.tensor_scalar(
                        out=o[:, sl], in0=o[:, sl], scalar1=bhhc[:, m:m + 1],
                        scalar2=None, op0=ALU.add),
                        reads=[kb + f"g1_{m}", "bhhc"], writes=[kb + f"g2_{m}"])
                    P.op("scalar", lambda e, sl=sl, o=out_sb: e.activation(
                        out=o[:, sl], in_=o[:, sl], func=AF.Sigmoid),
                        reads=[kb + f"g2_{m}"], writes=[kb + f"gact{m}"])
            for m8 in range(8):
                m = 16 + m8
                sl = slice(m8 * NSLOT, (m8 + 1) * NSLOT)
                P.op("vector", lambda e, m=m, sl=sl: e.tensor_scalar(
                    out=n_sb[:, sl], in0=gh_sb[:, m * NSLOT:(m + 1) * NSLOT],
                    scalar1=bhhc[:, m:m + 1], scalar2=None, op0=ALU.add),
                    reads=[f"gh{m}", "bhhc"], writes=[kb + f"nh{m8}"])
                P.op("vector", lambda e, m8=m8, sl=sl: e.tensor_tensor(
                    out=n_sb[:, sl], in0=r_sb[:, sl], in1=n_sb[:, sl], op=ALU.mult),
                    reads=[kb + f"gact{m8}", kb + f"nh{m8}"], writes=[kb + f"nr{m8}"])
                P.op("vector", lambda e, m=m, sl=sl: e.scalar_tensor_tensor(
                    out=n_sb[:, sl], in0=gi_sb[:, m * NSLOT:(m + 1) * NSLOT],
                    scalar=bihc[:, m:m + 1], in1=n_sb[:, sl], op0=ALU.add, op1=ALU.add),
                    reads=[kb + f"gi{m}", kb + f"nr{m8}", "bihc"], writes=[kb + f"np{m8}"])
                P.op("scalar", lambda e, sl=sl: e.activation(
                    out=n_sb[:, sl], in_=n_sb[:, sl], func=AF.Tanh),
                    reads=[kb + f"np{m8}"], writes=[kb + f"n{m8}"])
                P.op("vector", lambda e, m8=m8, sl=sl: e.tensor_tensor(
                    out=snew[:, sl], in0=s0T_b[:, sl], in1=n_sb[:, sl], op=ALU.subtract),
                    reads=["s0T", kb + f"n{m8}"], writes=[kb + f"sd{m8}"])
                P.op("vector", lambda e, m8=m8, sl=sl: e.tensor_tensor(
                    out=snew[:, sl], in0=z_sb[:, sl], in1=snew[:, sl], op=ALU.mult),
                    reads=[kb + f"gact{8 + m8}", kb + f"sd{m8}"], writes=[kb + f"zs{m8}"])
                P.op("vector", lambda e, m8=m8, sl=sl: e.tensor_tensor(
                    out=snew[:, sl], in0=n_sb[:, sl], in1=snew[:, sl], op=ALU.add),
                    reads=[kb + f"n{m8}", kb + f"zs{m8}"], writes=[kb + f"sn{m8}"])
            snew_keys = [kb + f"sn{m8}" for m8 in range(8)]
            for k in range(8):
                P.op("tensor", lambda e, k=k: e.transpose(
                    ptB[0:64, 0:128], snew[:, k * NSLOT:(k + 1) * NSLOT], ident[:]),
                    reads=snew_keys + ["ident"], writes=ptBK)
                P.op("vector", lambda e, k=k: e.tensor_copy(
                    snewT[:, k * 128:(k + 1) * 128], ptB[0:64, 0:128]),
                    reads=ptBK, writes=[kb + f"snT{k}"])
            snewT_keys = [kb + f"snT{k}" for k in range(8)]
            for wb_, bi, dst, wkey, rkey in ((wq_b, 0, q_sb, "q", "wqb"),
                                             (wk_b, 2, k_sb, "k", "wkb")):
                for c in range(2):
                    for k in range(8):
                        P.op("tensor", lambda e, c=c, k=k, wb_=wb_: e.matmul(
                            pq, lhsT=wb_[:, k * DQ + c * 128: k * DQ + c * 128 + 128],
                            rhs=snew[:, k * NSLOT:(k + 1) * NSLOT],
                            start=(k == 0), stop=(k == 7), skip_group_check=True),
                            reads=[rkey] + snew_keys, writes=pqK)
                    P.op("vector", lambda e, c=c, dst=dst, bi=bi: e.tensor_scalar(
                        out=dst[:, c * NSLOT:(c + 1) * NSLOT], in0=pq,
                        scalar1=bqkc[:, bi + c:bi + c + 1], scalar2=None, op0=ALU.add),
                        reads=pqK + ["bqkc"], writes=[kb + wkey + str(c)])
            for nh in range(2):
                for k in range(8):
                    P.op("tensor", lambda e, nh=nh, k=k: e.matmul(
                        pvT[:, nh * 512:(nh + 1) * 512],
                        lhsT=snew[:, k * NSLOT:(k + 1) * NSLOT],
                        rhs=wv_b[:, k * D + nh * 512: k * D + nh * 512 + 512],
                        start=(k == 0), stop=(k == 7), skip_group_check=True),
                        reads=["wvb"] + snew_keys, writes=pvTK)
            P.op("sync", lambda e: e.dma_start(out=ggb, in_=bv_r), writes=["ggb"], dma=True)
            P.op("vector", lambda e: e.tensor_tensor(out=vT_sb, in0=pvT, in1=ggb, op=ALU.add),
                 reads=pvTK + ["ggb"], writes=[kb + "vT"])
            for c in range(2):
                P.op("tensor", lambda e, c=c: e.matmul(
                    psc, lhsT=q_sb[:, c * NSLOT:(c + 1) * NSLOT],
                    rhs=k_sb[:, c * NSLOT:(c + 1) * NSLOT],
                    start=(c == 0), stop=(c == 1), skip_group_check=True),
                    reads=[kb + "q" + str(c), kb + "k" + str(c)], writes=pscK)
            P.op("vector", lambda e: e.tensor_scalar(
                out=sc_sb, in0=psc, scalar1=DQ ** -0.5, scalar2=None, op0=ALU.mult),
                reads=pscK, writes=[kb + "sc"])
            P.op("vector", lambda e: e.reduce_max(out=colB[:, 2:3], in_=sc_sb,
                                                  axis=AX.X, negate=True),
                 reads=[kb + "sc"], writes=[kb + "scm", "colB"])
            P.op("scalar", lambda e: e.activation(out=sc_sb, in_=sc_sb, func=AF.Exp,
                                                  bias=colB[:, 2:3]),
                 reads=[kb + "sc", kb + "scm"], writes=[kb + "sce"])
            P.op("vector", lambda e: e.reduce_sum(out=colB[:, 3:4], in_=sc_sb, axis=AX.X),
                 reads=[kb + "sce"], writes=[kb + "scs", "colB"])
            P.op("vector", lambda e: e.reciprocal(out=colB[:, 4:5], in_=colB[:, 3:4]),
                 reads=[kb + "scs"], writes=[kb + "scz", "colB"])
            P.op("vector", lambda e: e.tensor_scalar(
                out=sc_sb, in0=sc_sb, scalar1=colB[:, 4:5], scalar2=None, op0=ALU.mult),
                reads=[kb + "sce", kb + "scz"], writes=[kb + "attn"])
            P.op("tensor", lambda e: e.transpose(
                ptB[0:64, 0:64], sc_sb, ident[0:64, 0:64]),
                reads=[kb + "attn", "ident"], writes=ptBK)
            P.op("vector", lambda e: e.tensor_copy(at_sb, ptB[0:64, 0:64]),
                 reads=ptBK, writes=[kb + "attnT"])
            for nh in range(2):
                P.op("tensor", lambda e, nh=nh: e.matmul(
                    pcm[:, nh * 512:(nh + 1) * 512], lhsT=at_sb,
                    rhs=vT_sb[:, nh * 512:(nh + 1) * 512],
                    start=True, stop=True, skip_group_check=True),
                    reads=[kb + "attnT", kb + "vT"], writes=pcmK)
            P.op("vector", lambda e: e.tensor_tensor(out=y_sb, in0=pcm, in1=snewT, op=ALU.add),
                 reads=pcmK + snewT_keys, writes=[kb + "y"])
            P.op("vector", lambda e: e.reduce_sum(out=colB[:, 0:1], in_=y_sb, axis=AX.X),
                 reads=[kb + "y"], writes=[kb + "ys", "colB"])
            P.op("vector", lambda e: e.tensor_tensor(out=scrB, in0=y_sb, in1=y_sb,
                                                     op=ALU.mult),
                 reads=[kb + "y"], writes=["scrB"])
            P.op("vector", lambda e: e.reduce_sum(out=colB[:, 1:2], in_=scrB, axis=AX.X),
                 reads=["scrB"], writes=[kb + "yq", "colB"])
            P.op("vector", lambda e: e.tensor_scalar(out=colB[:, 0:1], in0=colB[:, 0:1],
                                                     scalar1=1.0 / D, scalar2=None, op0=ALU.mult),
                 reads=[kb + "ys"], writes=[kb + "ym", "colB"])
            P.op("vector", lambda e: e.tensor_tensor(out=colB[:, 2:3], in0=colB[:, 0:1],
                                                     in1=colB[:, 0:1], op=ALU.mult),
                 reads=[kb + "ym"], writes=[kb + "ym2", "colB"])
            P.op("vector", lambda e: e.tensor_scalar(out=colB[:, 1:2], in0=colB[:, 1:2],
                                                     scalar1=1.0 / D, scalar2=None, op0=ALU.mult),
                 reads=[kb + "yq"], writes=[kb + "yq2", "colB"])
            P.op("vector", lambda e: e.tensor_tensor(out=colB[:, 1:2], in0=colB[:, 1:2],
                                                     in1=colB[:, 2:3], op=ALU.subtract),
                 reads=[kb + "yq2", kb + "ym2"], writes=[kb + "yv", "colB"])
            P.op("scalar", lambda e: e.activation(out=colB[:, 1:2], in_=colB[:, 1:2],
                                                  func=AF.Ln, bias=konst[0:64, 0:1]),
                 reads=[kb + "yv", "konst"], writes=[kb + "yl", "colB"])
            P.op("scalar", lambda e: e.activation(out=colB[:, 1:2], in_=colB[:, 1:2],
                                                  func=AF.Exp, scale=konst[0:64, 1:2]),
                 reads=[kb + "yl", "konst"], writes=[kb + "yi", "colB"])
            P.op("vector", lambda e: e.tensor_tensor(out=colB[:, 2:3], in0=colB[:, 0:1],
                                                     in1=colB[:, 1:2], op=ALU.mult),
                 reads=[kb + "ym", kb + "yi"], writes=[kb + "ymi", "colB"])
            P.op("vector", lambda e: e.tensor_scalar(
                out=yn_sb, in0=y_sb, scalar1=colB[:, 1:2], scalar2=colB[:, 2:3],
                op0=ALU.mult, op1=ALU.subtract),
                reads=[kb + "y", kb + "yi", kb + "ymi"], writes=[kb + "yn0"])
            P.op("sync", lambda e: e.dma_start(out=ggb, in_=lng_r), writes=["ggb"], dma=True)
            P.op("vector", lambda e: e.tensor_tensor(out=yn_sb, in0=yn_sb, in1=ggb, op=ALU.mult),
                 reads=[kb + "yn0", "ggb"], writes=[kb + "yn1"])
            P.op("sync", lambda e: e.dma_start(out=ggb, in_=lnb_r), writes=["ggb"], dma=True)
            P.op("vector", lambda e: e.tensor_tensor(out=yn_sb, in0=yn_sb, in1=ggb, op=ALU.add),
                 reads=[kb + "yn1", "ggb"], writes=[kb + "yn"])
            for k in range(8):
                P.op("tensor", lambda e, k=k: e.transpose(
                    ptB[:, 0:64], yn_sb[:, k * 128:(k + 1) * 128], ident[0:64, 0:64]),
                    reads=[kb + "yn", "ident"], writes=ptBK)
                P.op("vector", lambda e: e.tensor_copy(ynT_sb, ptB[:, 0:64]),
                     reads=ptBK, writes=[kb + "ynT"])
                for nh in range(2):
                    P.op("tensor", lambda e, k=k, nh=nh: e.matmul(
                        pvT[:, nh * 512:(nh + 1) * 512], lhsT=ynT_sb,
                        rhs=wvp_b[:, k * D + nh * 512: k * D + nh * 512 + 512],
                        start=(k == 0), stop=(k == 7), skip_group_check=True),
                        reads=[kb + "ynT", "wvpb"], writes=pvTK)
            P.op("sync", lambda e: e.dma_start(out=ggb, in_=bvp_r), writes=["ggb"], dma=True)
            P.op("vector", lambda e, b=b: e.tensor_tensor(
                out=sv_sb[b], in0=pvT, in1=ggb, op=ALU.add),
                reads=pvTK + ["ggb"], writes=[f"sv{b}"])
        sbp.pop()
        P.barrier()
        if "C" not in PHASES:
            if DBG_DUMP is not None:
                c0, c1 = DBG_DUMP
                dbg_t = dt("dbg", [128, c1 - c0], F32, kind="ExternalOutput").ap()
                P.barrier()
                P.op("sync", lambda e: e.dma_start(out=dbg_t, in_=sbp.raw(0, 128, c0, c1)),
                     dma=True, qsem="dbgdump")
            P.emit(blk_ctx, mksem, final_eng="sync")
            return nc

        sbp.push()
        xC = sbp.alloc(2 * D)
        lncolC = sbp.alloc(8)
        xnTC = sbp.alloc(8 * TB)
        wing = [sbp.alloc(8 * 128) for _ in range(2)]
        u_raw = sbp.alloc(NFC * SEG)
        uconv = sbp.alloc(NFC * TB)
        go_sl = [sbp.alloc(NFC * TB) for _ in range(2)]
        delta = sbp.alloc(NFC * TB)
        du_sl = sbp.alloc(NFC * TB)
        y_sl = sbp.alloc(NFC * TB)
        g_sb2 = sbp.alloc(TB, parts=96)
        _sio = si_sb2[0].offset
        _sio1 = si_sb2[1].offset
        brep = [sbp.raw(0, 128, _sio + i * TB, _sio + (i + 1) * TB) for i in range(2)]
        crep = [sbp.raw(0, 128, _sio + (2 + i) * TB, _sio + (3 + i) * TB) for i in range(2)]
        fbuf = [sbp.raw(0, 128, _sio1 + i * TB, _sio1 + (i + 1) * TB) for i in range(4)]
        dtmp = fbuf[0]
        gslab = sbp.alloc(NFC * TB)
        hslab = sbp.alloc(8 * TB)
        mslab = sbp.alloc(8 * TB)
        fin_d = mslab
        woutb = None
        wob = None
        aC = sbp.alloc(2 * NSLOT)
        f0T_sb = delta[0:128, 0:2048]
        aT_sb = delta[0:64, 2048:2048 + 2 * 128]

        ps_uo = [R(0, 256), R(512, 768), R(256, 512), R(768, 1024)]
        ps_g = R(1024, 1280, p1=96)
        ps_d = [R(1280, 1536), R(1536, 1792)]
        ps_f0 = [R(1024, 1280), R(1536, 1792), R(1280, 1536), R(1792, 2048)]
        ps_tpx = [R(0, 128), R(512, 640)]
        ps_tpa = [R(1280, 1408, p1=64), R(1792, 1920, p1=64)]
        ps_fin = [[R(2048 + (sub * 2 + nh) * 512, 2048 + (sub * 2 + nh + 1) * 512)
                   for nh in range(2)] for sub in range(2)]

        def seg(slab, fc, w=TB):
            return slab[:, fc * w:(fc + 1) * w]

        u3d = u_raw.rearrange("p (c t) -> p c t", t=SEG)
        wslots = [(wing[0], "wing0"), (wing[1], "wing1"),
                  (xC[:, 0:D], "xCa"), (xC[:, D:2 * D], "xCb")]

        def stage_front(b, blk):
            t0 = blk * TB
            key = f"C{b}_{blk}"
            P.op("sync", lambda e: e.dma_start(
                out=xC.rearrange("p (s d) -> p s d", d=D),
                in_=x_in[b, t0:t0 + TB, :].rearrange("(s p) d -> p s d", p=128)),
                writes=["xCa", "xCb"], dma=True, tag=key + "xload")
            lnscr = u_raw[:, 2048:3072]
            for sub in range(2):
                xs = xC[:, sub * D:(sub + 1) * D]
                xn = u_raw[:, sub * D:(sub + 1) * D]
                layernorm(xs, xn, lncolC, lnscr, key + f"ln{sub}", "xCa" if sub == 0 else "xCb", "C", xtra=("u_raw",))
                for k in range(8):
                    tp, tpK = ps_tpx[k % 2]
                    P.op("tensor", lambda e, k=k, xn=xn, tp=tp: e.transpose(
                        tp, xn[:, k * 128:(k + 1) * 128], ident[:]),
                        reads=[key + f"ln{sub}xn", "u_raw", "ident"], writes=tpK)
                    P.op("vector", lambda e, k=k, sub=sub, tp=tp: e.tensor_copy(
                        xnTC[:, k * TB + sub * 128: k * TB + sub * 128 + 128], tp),
                        reads=tpK, writes=[f"CxnT_{k}_{sub}"])
            if blk == 0 and b == 0:
                P.op("vector", lambda e: e.memset(u3d[:, :, 0:3], 0.0),
                     reads=["u_raw"], writes=["u_raw"], tag=key + "hist0")
            elif blk == 0:
                P.op("vector", lambda e: e.memset(u3d[:, :, 0:3], 0.0),
                     reads=["u_raw"], writes=["u_raw"], tag=key + "hist0")
            else:
                P.op("vector", lambda e: e.tensor_copy(
                    u3d[:, :, 0:3], hist_t.rearrange("p (c t) -> p c t", t=3)),
                    reads=["u_raw", "hist_t"], writes=["u_raw"], tag=key + "hist")

        def stage_mid_grp(b, blk, par, grps):
            key = f"C{b}_{blk}"
            xnT_keys = [f"CxnT_{k}_{s}" for k in range(8) for s in range(2)]
            for grp in grps:
                wb = wing[grp % 2]
                P.op("sync", lambda e, grp=grp, wb=wb: e.dma_start(
                    out=wb, in_=w_inT[grp]),
                    writes=[f"wing{grp % 2}"], dma=True, tag=key + "windma")
                pm, pmK = ps_uo[grp % 4]
                for k in range(8):
                    P.op("tensor", lambda e, k=k, wb=wb, pm=pm: e.matmul(
                        pm, lhsT=wb[:, k * 128:(k + 1) * 128],
                        rhs=xnTC[:, k * TB:(k + 1) * TB],
                        start=(k == 0), stop=(k == 7), skip_group_check=True),
                        reads=[f"wing{grp % 2}"] + xnT_keys, writes=pmK,
                        tag=key + "uomm")
                fc = grp % 16
                if grp < 16:
                    P.op("vector", lambda e, fc=fc, pm=pm: e.tensor_copy(
                        u3d[:, fc, 3:3 + TB], pm),
                        reads=pmK, writes=["u_raw"], tag=key + "uevac")
                else:
                    P.op("scalar", lambda e, fc=fc, pm=pm, par=par: e.activation(
                        out=seg(go_sl[par], fc), in_=pm, func=AF.Silu),
                        reads=pmK, writes=[f"go_{par}_{fc}"], tag=key + "goevac")

        def stage_mid_post(b, blk):
            key = f"C{b}_{blk}"
            for q in range(NFC // 2):
                for j in range(4):
                    for fc in (2 * q, 2 * q + 1):
                        base = fc * SEG
                        srcv = u_raw[:, base + j:base + j + TB]
                        if j == 0:
                            P.op("vector", lambda e, srcv=srcv, fc=fc: e.tensor_scalar(
                                out=seg(uconv, fc), in0=srcv, scalar1=cwc[:, fc * 4:fc * 4 + 1],
                                scalar2=None, op0=ALU.mult),
                                reads=["u_raw", "cwc"], writes=[f"uc_{fc}"], tag=key + "conv")
                        else:
                            P.op("vector", lambda e, srcv=srcv, fc=fc, j=j: e.scalar_tensor_tensor(
                                out=seg(uconv, fc), in0=srcv, scalar=cwc[:, fc * 4 + j:fc * 4 + j + 1],
                                in1=seg(uconv, fc), op0=ALU.mult, op1=ALU.add),
                                reads=["u_raw", "cwc", f"uc_{fc}"], writes=[f"uc_{fc}"],
                                tag=key + "conv")
                for fc in (2 * q, 2 * q + 1):
                    P.op("scalar", lambda e, fc=fc: e.activation(
                        out=seg(uconv, fc), in_=seg(uconv, fc), func=AF.Silu,
                        bias=cbc[:, fc:fc + 1]),
                        reads=[f"uc_{fc}", "cbc"], writes=[f"uc_{fc}"], tag=key + "u2silu")
            P.op("vector", lambda e: e.tensor_copy(
                hist_t.rearrange("p (c t) -> p c t", t=3), u3d[:, :, TB:TB + 3]),
                reads=["u_raw"], writes=["hist_t"], tag=key + "tails")

        def stage_gate(b, blk):
            key = f"C{b}_{blk}"
            uc_keys = [f"uc_{fc}" for fc in range(NFC)]
            pg, pgK = ps_g
            for k in range(NFC):
                P.op("tensor", lambda e, k=k, pg=pg: e.matmul(
                    pg, lhsT=wgT_b[:, k * 96:(k + 1) * 96], rhs=seg(uconv, k),
                    start=(k == 0), stop=(k == NFC - 1), skip_group_check=True),
                    reads=["wgT", f"uc_{k}"], writes=pgK, tag=key + "gmm")
            P.op("vector", lambda e, pg=pg: e.tensor_copy(g_sb2, pg),
                 reads=pgK, writes=["gsb"], tag=key + "gevac")
            for m in range(NFC):
                pd, pdK = ps_d[m % 2]
                P.op("tensor", lambda e, m=m, pd=pd: e.matmul(
                    pd, lhsT=wdT_b[:, m * 128:(m + 1) * 128], rhs=g_sb2[0:64, :],
                    start=True, stop=True, skip_group_check=True),
                    reads=["wdT", "gsb"], writes=pdK, tag=key + "dmm")
                P.op("scalar", lambda e, m=m, pd=pd: e.activation(
                    out=dtmp, in_=pd, func=AF.Exp, bias=dbc[:, m:m + 1]),
                    reads=pdK + ["dbc"], writes=["dtmp", "fb0"], tag=key + "dexp")
                P.op("scalar", lambda e, m=m: e.activation(
                    out=seg(delta, m), in_=dtmp, func=AF.Ln, bias=1.0),
                    reads=["dtmp", "fb0", "dreg"], writes=[f"dl_{m}", "dreg"], tag=key + "dln")
            P.op("vector", lambda e: e.tensor_tensor(out=du_sl, in0=delta, in1=uconv,
                                                     op=ALU.mult),
                 reads=[f"dl_{m}" for m in range(NFC)] + uc_keys + ["dreg"],
                 writes=["du"], tag=key + "du")

        def stage_scan(b, blk, par, midnxt):
            t0s = blk * TB
            key = f"C{b}_{blk}"
            P.op("sync", lambda e, t0s=t0s, b=b: e.dma_start(
                out=aC.rearrange("p (s a) -> p s a", a=NSLOT),
                in_=ascr[b, t0s:t0s + TB, :].rearrange("(s p) a -> p s a", p=128)),
                writes=["aC"], dma=True, tag=key + "aload")

            def embrep(s):
                br, cr = brep[s % 2], crep[s % 2]
                P.op("sync", lambda e, s=s, br=br: e.dma_start(
                    out=br, in_=g_sb2[64 + s:65 + s, :].unsqueeze(1).broadcast_to((1, 128, TB))),
                    reads=["gsb"], writes=[f"brep{s % 2}"], dma=True, tag=key + "brep")
                P.op("sync", lambda e, s=s, cr=cr: e.dma_start(
                    out=cr, in_=g_sb2[80 + s:81 + s, :].unsqueeze(1).broadcast_to((1, 128, TB))),
                    reads=["gsb"], writes=[f"crep{s % 2}"], dma=True, tag=key + "crep")

            def emgbuild(s, gq, eng):
                br = brep[s % 2]
                gsl = gslab[:, gq * 4 * TB:(gq + 1) * 4 * TB]
                P.op(eng, lambda e, br=br, gsl=gsl: e.tensor_tensor(
                    out=gsl.rearrange("p (c t) -> p c t", t=TB),
                    in0=du_sl[:, gq * 4 * TB:(gq + 1) * 4 * TB].rearrange("p (c t) -> p c t", t=TB),
                    in1=br.unsqueeze(1).broadcast_to((128, 4, TB)), op=ALU.mult),
                    reads=["du", f"brep{s % 2}"], writes=[f"gsl{gq}"],
                    tag=key + "gbuild")

            embrep(0)
            for gq in range(4):
                emgbuild(0, gq, "vector")
            for s in range(NS):
                if s + 1 < NS:
                    embrep(s + 1)
                for half in range(2):
                    for fc8 in range(8):
                        fc = half * 8 + fc8
                        fb = fbuf[fc % 4]
                        P.op("scalar", lambda e, s=s, fc=fc, fb=fb: e.activation(
                            out=fb, in_=seg(delta, fc), func=AF.Exp,
                            scale=acol[:, fc * NS + s:fc * NS + s + 1]),
                            reads=[f"dl_{fc}", "acol", "dreg"], writes=[f"fb{fc % 4}"],
                            tag=key + "fexp")
                        P.op("vector", lambda e, s=s, fc=fc, fc8=fc8, fb=fb, b=b: e.tensor_tensor_scan(
                            out=seg(hslab, fc8), data0=fb, data1=seg(gslab, fc),
                            initial=carry[b][:, s * NFC + fc:s * NFC + fc + 1],
                            op0=ALU.mult, op1=ALU.add),
                            reads=[f"fb{fc % 4}", f"gsl{fc // 4}", f"ca_{b}_{s}_{fc}"],
                            writes=[f"h_{fc8}"], tag=key + "scan")
                        if s + 1 < NS and fc8 == 3:
                            emgbuild(s + 1, half * 2, "vector")
                        if s + 1 < NS and fc8 == 7:
                            emgbuild(s + 1, half * 2 + 1, "vector")
                    P.op("vector", lambda e, s=s, b=b, half=half: e.tensor_copy(
                        carry[b][:, s * NFC + half * 8:s * NFC + half * 8 + 8],
                        hslab.rearrange("p (c t) -> p c t", t=TB)[:, :, TB - 1]),
                        reads=[f"h_{j}" for j in range(8)],
                        writes=[f"ca_{b}_{s}_{half * 8 + j}" for j in range(8)],
                        tag=key + "carry")
                    if midnxt is not None:
                        stage_mid_grp(midnxt[0], midnxt[1], midnxt[2],
                                      [s * 2 + half])
                    for yq in range(2):
                        q0 = half * 8 * TB + yq * 4 * TB
                        ysl_q = y_sl[:, q0:q0 + 4 * TB]
                        h_q = hslab[:, yq * 4 * TB:(yq + 1) * 4 * TB]
                        m_q = mslab[:, yq * 4 * TB:(yq + 1) * 4 * TB]
                        hkeys = [f"h_{yq * 4 + j}" for j in range(4)]
                        if s == 0:
                            P.op("vector", lambda e, cr=crep[s % 2], ysl_q=ysl_q, h_q=h_q: e.tensor_tensor(
                                out=ysl_q.rearrange("p (c t) -> p c t", t=TB),
                                in0=h_q.rearrange("p (c t) -> p c t", t=TB),
                                in1=cr.unsqueeze(1).broadcast_to((128, 4, TB)), op=ALU.mult),
                                reads=hkeys + [f"crep{s % 2}"],
                                writes=["y_sl"], tag=key + "ymul")
                        else:
                            P.op("vector", lambda e, cr=crep[s % 2], h_q=h_q, m_q=m_q: e.tensor_tensor(
                                out=m_q.rearrange("p (c t) -> p c t", t=TB),
                                in0=h_q.rearrange("p (c t) -> p c t", t=TB),
                                in1=cr.unsqueeze(1).broadcast_to((128, 4, TB)), op=ALU.mult),
                                reads=hkeys + [f"crep{s % 2}"],
                                writes=["mslab"], tag=key + "ymul")
                            P.op("vector", lambda e, ysl_q=ysl_q, m_q=m_q: e.tensor_tensor(
                                out=ysl_q, in0=ysl_q, in1=m_q, op=ALU.add),
                                reads=["y_sl", "mslab"], writes=["y_sl"], tag=key + "yadd")
            for fc in range(NFC):
                P.op("vector", lambda e, fc=fc: e.scalar_tensor_tensor(
                    out=seg(mslab, fc % 8), in0=seg(uconv, fc), scalar=dcolv[:, fc:fc + 1],
                    in1=seg(y_sl, fc), op0=ALU.mult, op1=ALU.add),
                    reads=[f"uc_{fc}", "y_sl", "dcolv"], writes=[f"t2_{fc}", "mslab"],
                    tag=key + "du2")
                P.op("vector", lambda e, fc=fc, par=par: e.tensor_tensor(
                    out=seg(y_sl, fc), in0=seg(mslab, fc % 8), in1=seg(go_sl[par], fc), op=ALU.mult),
                    reads=[f"t2_{fc}", f"go_{par}_{fc}"], writes=["y_sl", f"y2_{fc}"],
                    tag=key + "y2")

        def stage_tail_a(b, blk):
            t0 = blk * TB
            key = f"C{b}_{blk}"
            for sub in range(2):
                for nh in range(2):
                    pf, pfK = ps_fin[sub][nh]
                    P.op("tensor", lambda e, sub=sub, nh=nh, pf=pf: e.matmul(
                        pf, lhsT=ones[64:65, 0:128],
                        rhs=bo_row[:, nh * 512:(nh + 1) * 512],
                        start=True, stop=False, skip_group_check=True),
                        reads=["ones", "bo_row"], writes=pfK, tag=key + "bomm")
            for sub in range(2):
                pa, paK = ps_tpa[sub]
                P.op("tensor", lambda e, sub=sub, pa=pa: e.transpose(
                    pa, aC[:, sub * NSLOT:(sub + 1) * NSLOT], ident[:]),
                    reads=["aC", "ident"], writes=paK)
                P.op("vector", lambda e, sub=sub, pa=pa: e.tensor_copy(
                    aT_sb[:, sub * 128:(sub + 1) * 128], pa),
                    reads=paK + ["dreg"], writes=[f"aT{sub}", "dreg"])
            for c in range(NFC):
                wb, wbK = wslots[c % 4]
                P.op("sync", lambda e, c=c, wb=wb: e.dma_start(
                    out=wb, in_=w_outT[c * 128:(c + 1) * 128, :]),
                    writes=[wbK], dma=True, tag=key + "woutdma")
                for sub in range(2):
                    for nh in range(2):
                        pf, pfK = ps_fin[sub][nh]
                        P.op("tensor", lambda e, c=c, sub=sub, nh=nh, wb=wb, pf=pf: e.matmul(
                            pf, lhsT=seg(y_sl, c)[:, sub * 128:sub * 128 + 128],
                            rhs=wb[:, nh * 512:(nh + 1) * 512],
                            start=False, stop=False, skip_group_check=True),
                            reads=[wbK, f"y2_{c}", "y_sl"], writes=pfK,
                            tag=key + "woutmm")
        def stage_tail_b(b, blk):
            t0 = blk * TB
            key = f"C{b}_{blk}"
            for m in range(8):
                pf0, pf0K = ps_f0[m % 4]
                P.op("tensor", lambda e, m=m, b=b, pf0=pf0: e.matmul(
                    pf0, lhsT=sv_sb[b][:, m * 128:(m + 1) * 128], rhs=aT_sb,
                    start=True, stop=True, skip_group_check=True),
                    reads=[f"sv{b}", "aT0", "aT1", "dreg"], writes=pf0K, tag=key + "f0mm")
                P.op("vector", lambda e, m=m, pf0=pf0: e.tensor_copy(
                    f0T_sb[:, m * TB:(m + 1) * TB], pf0),
                    reads=pf0K + ["dreg"], writes=[f"f0T{m}", "dreg"])
            for k in range(8):
                wb, wbK = wslots[k % 4]
                P.op("sync", lambda e, k=k, wb=wb: e.dma_start(
                    out=wb, in_=wo_d[k * 128:(k + 1) * 128, :]),
                    writes=[wbK], dma=True, tag=key + "wodma")
                for sub in range(2):
                    for nh in range(2):
                        pf, pfK = ps_fin[sub][nh]
                        P.op("tensor", lambda e, k=k, sub=sub, nh=nh, wb=wb, pf=pf: e.matmul(
                            pf, lhsT=f0T_sb[:, k * TB + sub * 128: k * TB + sub * 128 + 128],
                            rhs=wb[:, nh * 512:(nh + 1) * 512],
                            start=False, stop=(k == 7), skip_group_check=True),
                            reads=[wbK, f"f0T{k}", "dreg"], writes=pfK,
                            tag=key + "womm")
            for sub in range(2):
                for nh in range(2):
                    pf, pfK = ps_fin[sub][nh]
                    idx = sub * 2 + nh
                    dst = fin_d[:, idx * 512:(idx + 1) * 512]
                    P.op("vector", lambda e, pf=pf, dst=dst: e.tensor_copy(dst, pf),
                         reads=pfK, writes=["finP", "mslab"], tag=key + "finevac")
            P.op("gpsimd", lambda e, b=b, t0=t0: e.dma_start(
                out=out_hbm[b, t0:t0 + 128, :],
                in_=fin_d[:, 0:1024], accum_op=ALU.add),
                reads=["finP", "mslab"], dma=True, tag=key + "acc0", qsem="acc0")
            P.op("gpsimd", lambda e, b=b, t0=t0: e.dma_start(
                out=out_hbm[b, t0 + 128:t0 + 256, :],
                in_=fin_d[:, 1024:2048], accum_op=ALU.add),
                reads=["finP", "mslab"], dma=True, tag=key + "acc1", qsem="acc1")

        nblk_c = LIMIT_BLOCKS or NBLK
        for b in range(BPC):
            P.op("vector", lambda e, b=b: e.memset(carry[b][:], 0.0),
                 writes=[f"ca_{b}_{s}_{fc}" for s in range(NS) for fc in range(NFC)])
        seqC = [(b, blk) for b in range(BPC) for blk in range(nblk_c)]
        stage_front(*seqC[0])
        stage_mid_grp(*seqC[0], 0, range(32))
        stage_mid_post(*seqC[0])
        stage_gate(*seqC[0])
        for i, cur in enumerate(seqC):
            nxt = seqC[i + 1] if i + 1 < len(seqC) else None
            mn = (nxt + ((i + 1) % 2,)) if nxt else None
            if nxt:
                stage_front(*nxt)
            stage_scan(*cur, i % 2, mn)
            stage_tail_a(*cur)
            if nxt:
                stage_mid_post(*nxt)
            stage_tail_b(*cur)
            if nxt:
                stage_gate(*nxt)
        sbp.pop()

        if DBG_DUMP is not None:
            c0, c1 = DBG_DUMP
            dbg_t = dt("dbg", [128, c1 - c0], F32, kind="ExternalOutput").ap()
            P.barrier()
            P.op("sync", lambda e: e.dma_start(out=dbg_t, in_=sbp.raw(0, 128, c0, c1)),
                 dma=True, qsem="dbgdump")
        P.emit(blk_ctx, mksem, final_eng="sync")
    return nc


_NC_CACHE = {}


def _get_nc():
    if "nc" not in _NC_CACHE:
        _NC_CACHE["nc"] = build_program()
    return _NC_CACHE["nc"]


def kernel(**inp):
    def c(a):
        return np.ascontiguousarray(np.asarray(a, dtype=np.float32))

    x = c(inp["L0_attn_x"])

    def cols(v, n):
        return np.ascontiguousarray(v.reshape(n, 128).T)

    W_in = c(inp["L0_attn_W_in"]); conv_w = c(inp["L0_attn_conv_w"])
    conv_b = c(inp["L0_attn_conv_b"]); W_gate = c(inp["L0_attn_W_gate"])
    W_delta = c(inp["L0_attn_W_delta"]); delta_bias = c(inp["L0_attn_delta_bias"])
    A_log = c(inp["L0_attn_A_log"]); Dv = c(inp["L0_attn_D"])
    W_out = c(inp["L0_attn_W_out"]); slot_init = c(inp["L0_ffn_slot_init"])
    w1 = c(inp["L0_ffn_w1"]); b1 = c(inp["L0_ffn_b1"])
    w2 = c(inp["L0_ffn_w2"]); b2 = c(inp["L0_ffn_b2"])
    tau = c(inp["L0_ffn_tau"]); Wih = c(inp["L0_ffn_Wih"]); Whh = c(inp["L0_ffn_Whh"])
    bih = c(inp["L0_ffn_bih"]); bhh = c(inp["L0_ffn_bhh"])
    wq = c(inp["L0_ffn_wq"]); bq = c(inp["L0_ffn_bq"])
    wk = c(inp["L0_ffn_wk"]); bk = c(inp["L0_ffn_bk"])
    wv = c(inp["L0_ffn_wv"]); bv = c(inp["L0_ffn_bv"])
    ln_g = c(inp["L0_ffn_ln_g"]); ln_b = c(inp["L0_ffn_ln_b"])
    wvp = c(inp["L0_ffn_wvp"]); bvp = c(inp["L0_ffn_bvp"])
    wo = c(inp["L0_ffn_wo"]); bo = c(inp["L0_ffn_bo"])

    common = {
        "w_inT": np.ascontiguousarray(W_in.T.reshape(8, 128, 32, 128).transpose(2, 1, 0, 3).reshape(32, 128, 1024)),
        "conv_w_c": np.ascontiguousarray(conv_w[:, 0, :].reshape(NFC, 128, 4)
                                         .transpose(1, 0, 2).reshape(128, NFC * 4)),
        "conv_b_c": cols(conv_b, NFC),
        "w_gateT": np.ascontiguousarray(W_gate.T),
        "w_deltaT": np.ascontiguousarray(W_delta.T),
        "dbias_c": cols(delta_bias, NFC),
        "alog_c": np.ascontiguousarray(A_log.reshape(NFC, 128, NS)
                                       .transpose(1, 0, 2).reshape(128, NFC * NS)),
        "dvec_c": cols(Dv, NFC),
        "w_outT": np.ascontiguousarray(W_out.T),
        "slot_init": slot_init,
        "slot_initT": np.ascontiguousarray(slot_init.T),
        "w1a": np.ascontiguousarray(w1[:D]),
        "w1b": np.ascontiguousarray(w1[D:]),
        "b1_c": cols(b1, 4),
        "w2": w2,
        "b2_c": np.ascontiguousarray(b2.reshape(NSLOT, 1)),
        "tau": np.ascontiguousarray(tau.reshape(1, 1)),
        "wihT": np.ascontiguousarray(Wih.T),
        "whhT": np.ascontiguousarray(Whh.T),
        "bih_c": cols(bih, 24),
        "bhh_c": cols(bhh, 24),
        "wq": wq, "wk": wk,
        "bqk_c": np.ascontiguousarray(np.concatenate([cols(bq, 2), cols(bk, 2)], axis=1)),
        "wv": wv,
        "bv_r": np.ascontiguousarray(np.broadcast_to(bv, (NSLOT, D))),
        "lng_r": np.ascontiguousarray(np.broadcast_to(ln_g, (NSLOT, D))),
        "lnb_r": np.ascontiguousarray(np.broadcast_to(ln_b, (NSLOT, D))),
        "wvp": wvp,
        "bvp_r": np.ascontiguousarray(np.broadcast_to(bvp, (NSLOT, D))),
        "wo": wo,
        "bo_r": np.ascontiguousarray(bo.reshape(1, D)),
    }

    nc = _get_nc()
    ncore = NCORE if N_CORES_OVERRIDE is None else N_CORES_OVERRIDE
    in_maps = []
    for core in range(ncore):
        m = dict(common)
        m["x"] = np.ascontiguousarray(x[core * BPC:(core + 1) * BPC])
        in_maps.append(m)
    res = run_bass_kernel_spmd(nc, in_maps, core_ids=list(range(ncore)))
    return np.concatenate([r["out"] for r in res.results], axis=0)


if __name__ == "__main__":
    build_program()
    print("built ok")


# version 13
# speedup vs baseline: 1.2710x; 1.0291x over previous
import numpy as np
import concourse.bass as bass
import concourse.mybir as mybir
from concourse.bass_utils import run_bass_kernel_spmd
from concourse.alu_op_type import AluOpType as ALU

F32 = mybir.dt.float32
AF = mybir.ActivationFunctionType
AX = mybir.AxisListType

B, T, D = 16, 4096, 1024
HID = 2048
NS = 16
NSLOT, TOPK = 64, 8
DQ, DH = 256, 512
NCORE = 8
BPC = B // NCORE
TB = 256
NBLK = T // TB
NFC = HID // 128
SEG = TB + 4


class Prog:
    ENGS = ("sync", "tensor", "vector", "scalar", "gpsimd")

    def __init__(self):
        self.ops = []
        self.deps = []
        self.writer = {}
        self.readers = {}
        self.fence = set()

    def op(self, eng, builder, reads=(), writes=(), dma=False, tag="", qsem=None):
        i = len(self.ops)
        if dma and qsem is None:
            qsem = writes[0] if writes else (tag or "dma")
        rb = [k for k in reads if k.startswith("PSB")]
        if rb:
            reads = [k for k in reads if not k.startswith("PSB")]
            writes = list(writes) + rb
        dep = set(self.fence)
        for k in reads:
            w = self.writer.get(k)
            if w is not None:
                dep.add(w)
        for k in writes:
            for r in self.readers.get(k, ()):
                dep.add(r)
            w = self.writer.get(k)
            if w is not None:
                dep.add(w)
        dep.discard(i)
        self.ops.append((eng, builder, dma, tag, qsem))
        self.deps.append(dep)
        for k in reads:
            self.readers.setdefault(k, []).append(i)
        for k in writes:
            self.writer[k] = i
            self.readers[k] = []
        return i

    def barrier(self):
        last = {}
        for i, (eng, _b, isdma, _t, qs) in enumerate(self.ops):
            last[(eng, isdma, qs)] = i
        self.fence = set(last.values())

    def emit(self, block, mksem, final_eng="sync"):
        if TRUNC_OPS is not None:
            self.ops = self.ops[:TRUNC_OPS]
            self.deps = [set(x for x in d if x < TRUNC_OPS)
                         for d in self.deps[:TRUNC_OPS]]
        n = len(self.ops)
        eng_of = [o[0] for o in self.ops]
        isdma = [o[2] for o in self.ops]
        qsem_of = [o[4] for o in self.ops]
        publish = [False] * n
        for j in range(n):
            for i in self.deps[j]:
                if (FULL_SYNC or eng_of[i] != eng_of[j] or isdma[i] or isdma[j]
                        or eng_of[i] in SYNC_ENGS):
                    publish[i] = True
        for i in range(n):
            if isdma[i]:
                publish[i] = True
        doms = []
        dom = [None] * n
        for i in range(n):
            d = ("D", qsem_of[i]) if isdma[i] else ("E", eng_of[i])
            dom[i] = d
            if d not in doms:
                doms.append(d)
        sems = {d: mksem(f"{d[0]}_{d[1]}"[:28]) for d in doms}
        pubcount = {}
        val_after = [0] * n
        for i in range(n):
            if publish[i]:
                pubcount[dom[i]] = pubcount.get(dom[i], 0) + (16 if isdma[i] else 1)
            val_after[i] = pubcount.get(dom[i], 0)
        finals = dict(pubcount)

        def section(engname):
            def go(eng):
                waited = {}
                for i in range(n):
                    if eng_of[i] != engname:
                        continue
                    need = {}
                    for j in self.deps[i]:
                        if not publish[j]:
                            continue
                        if (not FULL_SYNC and dom[j] == ("E", engname)
                                and engname not in SYNC_ENGS and not isdma[i]):
                            continue
                        v = val_after[j]
                        if need.get(dom[j], 0) < v:
                            need[dom[j]] = v
                    for d, v in sorted(need.items(), key=lambda kv: str(kv[0])):
                        if waited.get(d, 0) < v:
                            eng.wait_ge(sems[d], v)
                            waited[d] = v
                    if ((engname in NOP_ENGS or any(t in self.ops[i][3] for t in NOP_TAGS)
                         or (NOP_TAGS and "UNTAGGED" in NOP_TAGS and engname == "scalar"
                             and not self.ops[i][3])) and not isdma[i]):
                        inst = eng.nop()
                    else:
                        inst = self.ops[i][1](eng)
                    if publish[i]:
                        inst.then_inc(sems[dom[i]], 16 if isdma[i] else 1)
                if engname == final_eng:
                    for d, v in sorted(finals.items(), key=lambda kv: str(kv[0])):
                        if d[0] == "D":
                            eng.wait_ge(sems[d], v)
            return go

        block.sync(section("sync"))
        block.tensor(section("tensor"))
        block.vector(section("vector"))
        block.scalar(section("scalar"))
        block.gpsimd(section("gpsimd"))


class Pool:
    def __init__(self, ap, n):
        self.ap, self.n, self.off, self.marks = ap, n, 0, []

    def alloc(self, cols, parts=128):
        cols_al = (cols + 15) & ~15
        assert self.off + cols_al <= self.n, f"SBUF overflow {self.off}+{cols_al}>{self.n}"
        a = self.ap[0:parts, self.off:self.off + cols]
        self.off += cols_al
        return a

    def push(self):
        self.marks.append(self.off)

    def pop(self):
        self.off = self.marks.pop()

    def raw(self, p0, p1, c0, c1):
        return self.ap[p0:p1, c0:c1]


PHASES = "ABC"
LIMIT_BLOCKS = None
TRUNC_OPS = None
N_CORES_OVERRIDE = None
NOP_ENGS = ()
NOP_TAGS = ()
DBG = {}
DBG_DUMP = None
FULL_SYNC = False
RACE_DETECT = False
SYNC_ENGS = ("gpsimd", "vector", "scalar")


def build_program():
    nc = bass.Bass("TRN2", target_bir_lowering=False, detect_race_conditions=RACE_DETECT)
    dt = nc.dram_tensor

    x_in = dt("x", [BPC, T, D], F32, kind="ExternalInput").ap()
    out_hbm = dt("out", [BPC, T, D], F32, kind="ExternalOutput").ap()
    ascr = dt("ascr", [BPC, T, NSLOT], F32, kind="Internal").ap()

    w_inT = dt("w_inT", [32, 128, 8 * 128], F32, kind="ExternalInput").ap()
    conv_w_c = dt("conv_w_c", [128, NFC * 4], F32, kind="ExternalInput").ap()
    conv_b_c = dt("conv_b_c", [128, NFC], F32, kind="ExternalInput").ap()
    w_gateT = dt("w_gateT", [HID, 96], F32, kind="ExternalInput").ap()
    w_deltaT = dt("w_deltaT", [64, HID], F32, kind="ExternalInput").ap()
    dbias_c = dt("dbias_c", [128, NFC], F32, kind="ExternalInput").ap()
    alog_c = dt("alog_c", [128, NFC * NS], F32, kind="ExternalInput").ap()
    dvec_c = dt("dvec_c", [128, NFC], F32, kind="ExternalInput").ap()
    w_outT = dt("w_outT", [HID, D], F32, kind="ExternalInput").ap()

    slot_init = dt("slot_init", [NSLOT, D], F32, kind="ExternalInput").ap()
    slot_initT = dt("slot_initT", [D, NSLOT], F32, kind="ExternalInput").ap()
    w1a_d = dt("w1a", [D, DH], F32, kind="ExternalInput").ap()
    w1b_d = dt("w1b", [D, DH], F32, kind="ExternalInput").ap()
    b1_c = dt("b1_c", [128, 4], F32, kind="ExternalInput").ap()
    w2_d = dt("w2", [DH, NSLOT], F32, kind="ExternalInput").ap()
    b2_c = dt("b2_c", [NSLOT, 1], F32, kind="ExternalInput").ap()
    tau_d = dt("tau", [1, 1], F32, kind="ExternalInput").ap()
    wihT = dt("wihT", [D, 3 * D], F32, kind="ExternalInput").ap()
    whhT = dt("whhT", [D, 3 * D], F32, kind="ExternalInput").ap()
    bih_c = dt("bih_c", [128, 24], F32, kind="ExternalInput").ap()
    bhh_c = dt("bhh_c", [128, 24], F32, kind="ExternalInput").ap()
    wq_d = dt("wq", [D, DQ], F32, kind="ExternalInput").ap()
    wk_d = dt("wk", [D, DQ], F32, kind="ExternalInput").ap()
    bqk_c = dt("bqk_c", [128, 4], F32, kind="ExternalInput").ap()
    wv_d = dt("wv", [D, D], F32, kind="ExternalInput").ap()
    bv_r = dt("bv_r", [NSLOT, D], F32, kind="ExternalInput").ap()
    lng_r = dt("lng_r", [NSLOT, D], F32, kind="ExternalInput").ap()
    lnb_r = dt("lnb_r", [NSLOT, D], F32, kind="ExternalInput").ap()
    wvp_d = dt("wvp", [D, D], F32, kind="ExternalInput").ap()
    bvp_r = dt("bvp_r", [NSLOT, D], F32, kind="ExternalInput").ap()
    wo_d = dt("wo", [D, D], F32, kind="ExternalInput").ap()
    bo_r = dt("bo_r", [1, D], F32, kind="ExternalInput").ap()

    import contextlib
    st = contextlib.ExitStack()
    with st:
        POOLN = 53200
        pool_t = st.enter_context(nc.sbuf_tensor("pool", [128, POOLN], F32))
        sbp = Pool(pool_t.ap(), POOLN)
        psum_t = st.enter_context(nc.psum_tensor("ppool", [128, 4096], F32))
        ps = psum_t.ap()

        def R(c0, c1, p0=0, p1=128):
            return ps[p0:p1, c0:c1], ["PSB%d" % i for i in range(c0 // 512, (c1 + 511) // 512)]

        def mksem(name):
            return st.enter_context(nc.semaphore(name))

        blk_ctx = st.enter_context(nc.Block())
        P = Prog()

        ident = sbp.alloc(128)
        ones = sbp.alloc(128)
        konst = sbp.alloc(4)
        cwc = sbp.alloc(NFC * 4)
        cbc = sbp.alloc(NFC)
        dbc = sbp.alloc(NFC)
        acol = sbp.alloc(NFC * NS)
        dcolv = sbp.alloc(NFC)
        wgT_b = sbp.alloc(NFC * 96)
        wdT_b = sbp.alloc(HID, parts=64)
        si_sb2 = [sbp.alloc(D, parts=64) for _ in range(BPC)]
        dncol = sbp.alloc(2, parts=64)
        sv0_off = sbp.off
        sv_sb = [sbp.alloc(D, parts=64) for _ in range(BPC)]
        bo_row = sbp.raw(64, 65, sv0_off, sv0_off + D)
        DBG["sv0"] = sv0_off
        carry = [sbp.alloc(NS * NFC) for _ in range(BPC)]
        hist_t = sbp.alloc(NFC * 3)

        P.op("gpsimd", lambda e: e.memset(ident[:], 0.0), writes=["ident"])
        P.op("gpsimd", lambda e: e.affine_select(
            out=ident[:], in_=ident[:], compare_op=ALU.not_equal,
            fill=1.0, base=0, pattern=[[-1, 128]], channel_multiplier=1),
            reads=["ident"], writes=["ident"])
        P.op("gpsimd", lambda e: e.memset(ones[:], 1.0), writes=["ones"])
        P.op("gpsimd", lambda e: e.memset(konst[:, 0:1], 1e-5), writes=["konst"])
        P.op("gpsimd", lambda e: e.memset(konst[:, 1:2], -0.5), writes=["konst"])
        for nm, dst, src in [("cwc", cwc, conv_w_c), ("cbc", cbc, conv_b_c),
                             ("dbc", dbc, dbias_c), ("dcolv", dcolv, dvec_c),
                             ("bo_row", bo_row, bo_r)]:
            P.op("sync", lambda e, d=dst, s=src: e.dma_start(out=d, in_=s),
                 writes=[nm], dma=True)
        P.op("sync", lambda e: e.dma_start(out=acol, in_=alog_c), writes=["acol"], dma=True)
        P.op("scalar", lambda e: e.activation(out=acol, in_=acol, func=AF.Exp),
             reads=["acol"], writes=["acol"])
        P.op("vector", lambda e: e.tensor_scalar(out=acol, in0=acol, scalar1=-1.0,
                                                 scalar2=None, op0=ALU.mult),
             reads=["acol"], writes=["acol"])
        P.op("sync", lambda e: e.dma_start(
            out=wgT_b.rearrange("p (c n) -> p c n", n=96),
            in_=w_gateT.rearrange("(c p) n -> p c n", p=128)), writes=["wgT"], dma=True)
        P.op("sync", lambda e: e.dma_start(out=wdT_b, in_=w_deltaT), writes=["wdT"], dma=True)

        def layernorm(xt, xn_out, cols8, scratch, key, rk, scr_key, xtra=()):
            s_ = cols8
            P.op("vector", lambda e: e.reduce_sum(out=s_[:, 0:1], in_=xt, axis=AX.X),
                 reads=[rk], writes=[key + "s", "lncol" + scr_key])
            P.op("vector", lambda e: e.tensor_tensor(out=scratch, in0=xt, in1=xt,
                                                     op=ALU.mult),
                 reads=[rk], writes=["lnscr" + scr_key] + list(xtra))
            P.op("vector", lambda e: e.reduce_sum(out=s_[:, 1:2], in_=scratch, axis=AX.X),
                 reads=["lnscr" + scr_key], writes=[key + "q"])
            P.op("vector", lambda e: e.tensor_scalar(out=s_[:, 0:1], in0=s_[:, 0:1],
                                                     scalar1=1.0 / D, scalar2=None, op0=ALU.mult),
                 reads=[key + "s"], writes=[key + "m"])
            P.op("vector", lambda e: e.tensor_tensor(out=s_[:, 2:3], in0=s_[:, 0:1],
                                                     in1=s_[:, 0:1], op=ALU.mult),
                 reads=[key + "m"], writes=[key + "m2"])
            P.op("vector", lambda e: e.tensor_scalar(out=s_[:, 1:2], in0=s_[:, 1:2],
                                                     scalar1=1.0 / D, scalar2=None, op0=ALU.mult),
                 reads=[key + "q"], writes=[key + "q2"])
            P.op("vector", lambda e: e.tensor_tensor(out=s_[:, 1:2], in0=s_[:, 1:2],
                                                     in1=s_[:, 2:3], op=ALU.subtract),
                 reads=[key + "q2", key + "m2"], writes=[key + "v"])
            P.op("scalar", lambda e: e.activation(out=s_[:, 1:2], in_=s_[:, 1:2],
                                                  func=AF.Ln, bias=konst[:, 0:1]),
                 reads=[key + "v", "konst"], writes=[key + "l"])
            P.op("scalar", lambda e: e.activation(out=s_[:, 1:2], in_=s_[:, 1:2],
                                                  func=AF.Exp, scale=konst[:, 1:2]),
                 reads=[key + "l", "konst"], writes=[key + "i"])
            P.op("vector", lambda e: e.tensor_tensor(out=s_[:, 2:3], in0=s_[:, 0:1],
                                                     in1=s_[:, 1:2], op=ALU.mult),
                 reads=[key + "m", key + "i"], writes=[key + "mi"])
            P.op("vector", lambda e: e.tensor_scalar(out=xn_out, in0=xt,
                                                     scalar1=s_[:, 1:2], scalar2=s_[:, 2:3],
                                                     op0=ALU.mult, op1=ALU.subtract),
                 reads=[rk, key + "i", key + "mi"], writes=[key + "xn"] + list(xtra))

        sbp.push()
        b1tot = sbp.alloc(4)
        b2col = sbp.alloc(1, parts=64)
        invt64 = sbp.alloc(1, parts=64)
        tau_sb = sbp.alloc(4, parts=1)
        P.op("sync", lambda e: e.dma_start(out=b2col, in_=b2_c), writes=["b2col"], dma=True)
        P.op("sync", lambda e: e.dma_start(out=tau_sb[0:1, 0:1], in_=tau_d),
             writes=["tau"], dma=True)
        P.op("vector", lambda e: e.tensor_scalar(out=tau_sb[0:1, 1:2], in0=tau_sb[0:1, 0:1],
                                                 scalar1=-1.0, scalar2=None, op0=ALU.mult),
             reads=["tau"], writes=["tau"])
        P.op("vector", lambda e: e.tensor_tensor(out=tau_sb[0:1, 2:3], in0=tau_sb[0:1, 0:1],
                                                 in1=tau_sb[0:1, 1:2], op=ALU.max),
             reads=["tau"], writes=["tau"])
        P.op("vector", lambda e: e.tensor_scalar(out=tau_sb[0:1, 2:3], in0=tau_sb[0:1, 2:3],
                                                 scalar1=0.1, scalar2=None, op0=ALU.add),
             reads=["tau"], writes=["tau"])
        P.op("vector", lambda e: e.reciprocal(out=tau_sb[0:1, 3:4], in_=tau_sb[0:1, 2:3]),
             reads=["tau"], writes=["tau"])
        P.op("sync", lambda e: e.dma_start(
            out=invt64, in_=tau_sb[0:1, 3:4].unsqueeze(1).broadcast_to((1, 64, 1))),
            reads=["tau"], writes=["invt64"], dma=True)
        w1a_b = sbp.alloc(8 * DH)
        w1b_b = sbp.alloc(8 * DH)
        w2_b = sbp.alloc(4 * NSLOT)
        si_sp = sbp.alloc(D, parts=64)
        smT = sbp.alloc(8)
        bias2 = sbp.alloc(DH, parts=1)
        P.op("sync", lambda e: e.dma_start(
            out=w1a_b.rearrange("p (k f) -> p k f", f=DH),
            in_=w1a_d.rearrange("(k p) f -> p k f", p=128)), writes=["w1a"], dma=True)
        P.op("sync", lambda e: e.dma_start(
            out=w1b_b.rearrange("p (k f) -> p k f", f=DH),
            in_=w1b_d.rearrange("(k p) f -> p k f", p=128)), writes=["w1b"], dma=True)
        P.op("sync", lambda e: e.dma_start(
            out=w2_b.rearrange("p (k f) -> p k f", f=NSLOT),
            in_=w2_d.rearrange("(k p) f -> p k f", p=128)), writes=["w2"], dma=True)
        P.op("sync", lambda e: e.dma_start(out=si_sp, in_=slot_init), writes=["si_sp"], dma=True)
        P.op("sync", lambda e: e.dma_start(out=b1tot, in_=b1_c), writes=["b1tot"], dma=True)

        psm, psmK = R(3584, 3592)
        for k in range(8):
            P.op("tensor", lambda e, k=k: e.matmul(
                psm[:, k:k + 1], lhsT=si_sp[:, k * 128:(k + 1) * 128],
                rhs=ones[0:64, 0:1], start=True, stop=True, skip_group_check=True),
                reads=["si_sp", "ones"], writes=psmK)
        P.op("vector", lambda e: e.tensor_scalar(out=smT, in0=psm, scalar1=1.0 / 64,
                                                 scalar2=None, op0=ALU.mult),
             reads=psmK, writes=["smT"])
        pb2, pb2K = R(512, 1024, p1=1)
        for k in range(8):
            P.op("tensor", lambda e, k=k: e.matmul(
                pb2, lhsT=smT[:, k:k + 1], rhs=w1b_b[:, k * DH:(k + 1) * DH],
                start=(k == 0), stop=(k == 7), skip_group_check=True),
                reads=["smT", "w1b"], writes=pb2K)
        P.op("vector", lambda e: e.tensor_copy(bias2, pb2), reads=pb2K, writes=["bias2"])
        pbt, pbtK = R(3592, 3596)
        for m in range(4):
            P.op("tensor", lambda e, m=m: e.transpose(
                pbt[:, m:m + 1], bias2[0:1, m * 128:(m + 1) * 128], ident[0:1, 0:1]),
                reads=["bias2", "ident"], writes=pbtK)
        P.op("vector", lambda e: e.tensor_tensor(out=b1tot, in0=b1tot, in1=pbt, op=ALU.add),
             reads=pbtK + ["b1tot"], writes=["b1tot"])

        xA = sbp.alloc(2 * D)
        xnA = sbp.alloc(2 * D)
        lnscA = sbp.alloc(D)
        lncolA = sbp.alloc(8)
        xnTA = sbp.alloc(8 * TB)
        h_sb = sbp.alloc(4 * TB)
        lg_sb = sbp.alloc(TB, parts=64)
        lgT = sbp.alloc(2 * NSLOT)
        wkt = sbp.alloc(NSLOT)
        mcol = sbp.alloc(16)
        amt = sbp.alloc(NSLOT)
        alset = sbp.alloc(NSLOT)
        for _n, _a in [("xA", xA), ("xnA", xnA), ("lnscA", lnscA), ("lncolA", lncolA),
                       ("xnTA", xnTA), ("h_sb", h_sb), ("lg_sb", lg_sb), ("lgT", lgT),
                       ("wkt", wkt), ("mcol", mcol), ("amt", amt), ("alset", alset),
                       ("w1a", w1a_b), ("b1tot", b1tot), ("invt64", invt64), ("tau", tau_sb)]:
            DBG[_n] = _a.offset

        ps_hm = [R(0, 256), R(512, 768), R(256, 512), R(768, 1024)]
        ps_si = [R(1024, 2048, p1=64) for b in range(BPC)]
        ps_dn = [R(3072, 3080, p1=64) for b in range(BPC)]
        ps_lg = R(3584, 3840, p1=64)
        ps_tpA = [R(2048, 2176), R(2560, 2688)]

        def phaseA_block(b, blk):
            t0 = blk * TB
            key = f"A{b}_{blk}"
            P.op("sync", lambda e: e.dma_start(
                out=xA.rearrange("p (s d) -> p s d", d=D),
                in_=x_in[b, t0:t0 + TB, :].rearrange("(s p) d -> p s d", p=128)),
                writes=["xA"], dma=True, tag=key + "xload")
            P.op("scalar", lambda e: e.dma_start(
                out=out_hbm[b, t0:t0 + TB, :].rearrange("(s p) d -> p s d", p=128),
                in_=xA.rearrange("p (s d) -> p s d", d=D)),
                reads=["xA"], dma=True, tag=key + "xstore", qsem="store")
            for sub in range(2):
                xs = xA[:, sub * D:(sub + 1) * D]
                xn = xnA[:, sub * D:(sub + 1) * D]
                layernorm(xs, xn, lncolA, lnscA, key + f"ln{sub}", "xA", "A")
                for k in range(8):
                    tp, tpK = ps_tpA[k % 2]
                    P.op("tensor", lambda e, k=k, xn=xn, tp=tp: e.transpose(
                        tp[:, 0:128], xn[:, k * 128:(k + 1) * 128], ident[:]),
                        reads=[key + f"ln{sub}xn", "ident"], writes=tpK)
                    P.op("vector", lambda e, k=k, sub=sub, tp=tp: e.tensor_copy(
                        xnTA[:, k * TB + sub * 128: k * TB + sub * 128 + 128], tp[:, 0:128]),
                        reads=tpK, writes=[f"xnTA_{k}_{sub}"])
            xnT_keys = [f"xnTA_{k}_{s}" for k in range(8) for s in range(2)]
            for m in range(4):
                pm, pmK = ps_hm[m]
                for k in range(8):
                    P.op("tensor", lambda e, m=m, k=k, pm=pm: e.matmul(
                        pm, lhsT=w1a_b[:, k * DH + m * 128: k * DH + m * 128 + 128],
                        rhs=xnTA[:, k * TB:(k + 1) * TB],
                        start=(k == 0), stop=(k == 7), skip_group_check=True),
                        reads=["w1a"] + xnT_keys, writes=pmK, tag=key + "w1")
                P.op("scalar", lambda e, m=m, pm=pm: e.activation(
                    out=h_sb[:, m * TB:(m + 1) * TB], in_=pm, func=AF.Gelu,
                    bias=b1tot[:, m:m + 1]),
                    reads=pmK + ["b1tot"], writes=[f"h{m}"], tag=key + "gelu")
            plg, plgK = ps_lg
            for k in range(4):
                P.op("tensor", lambda e, k=k, plg=plg: e.matmul(
                    plg, lhsT=w2_b[:, k * NSLOT:(k + 1) * NSLOT],
                    rhs=h_sb[:, k * TB:(k + 1) * TB],
                    start=(k == 0), stop=(k == 3), skip_group_check=True),
                    reads=["w2", f"h{k}"], writes=plgK, tag=key + "w2")
            P.op("vector", lambda e, plg=plg: e.tensor_scalar(
                out=lg_sb, in0=plg, scalar1=b2col, scalar2=invt64,
                op0=ALU.add, op1=ALU.mult),
                reads=plgK + ["b2col", "invt64"], writes=["lg"], tag=key + "lgevac")
            for sub in range(2):
                tp, tpK = ps_tpA[sub]
                P.op("tensor", lambda e, sub=sub, tp=tp: e.transpose(
                    tp[:, 0:NSLOT], lg_sb[:, sub * 128:(sub + 1) * 128], ident[0:64, 0:64]),
                    reads=["lg", "ident"], writes=tpK)
                P.op("vector", lambda e, sub=sub, tp=tp: e.tensor_copy(
                    lgT[:, sub * NSLOT:(sub + 1) * NSLOT], tp[:, 0:NSLOT]),
                    reads=tpK, writes=[f"lgT{sub}"])
            for sub in range(2):
                lgs = lgT[:, sub * NSLOT:(sub + 1) * NSLOT]
                P.op("vector", lambda e, lgs=lgs: e.tensor_copy(wkt, lgs),
                     reads=[f"lgT{sub}"], writes=["wkt"])
                for it in range(TOPK):
                    P.op("vector", lambda e, it=it: e.reduce_max(
                        out=mcol[:, it:it + 1], in_=wkt, axis=AX.X, negate=True),
                        reads=["wkt"], writes=[f"mx{it}", "mcol"])
                    if it < TOPK - 1:
                        P.op("vector", lambda e, it=it: e.tensor_scalar(
                            out=amt, in0=wkt, scalar1=mcol[:, it:it + 1],
                            scalar2=0.0, op0=ALU.add, op1=ALU.is_ge),
                            reads=["wkt", f"mx{it}"], writes=["amt"])
                        P.op("vector", lambda e: e.scalar_tensor_tensor(
                            out=wkt, in0=amt, scalar=-1e38, in1=wkt,
                            op0=ALU.mult, op1=ALU.add),
                            reads=["amt", "wkt"], writes=["wkt"])
                P.op("scalar", lambda e, lgs=lgs: e.activation(
                    out=amt, in_=lgs, func=AF.Exp, bias=mcol[:, 0:1]),
                    reads=[f"lgT{sub}", "mx0"], writes=["amt"], tag=key + "aexp")
                P.op("vector", lambda e, lgs=lgs: e.tensor_scalar(
                    out=wkt, in0=lgs, scalar1=mcol[:, TOPK - 1:TOPK],
                    scalar2=0.0, op0=ALU.add, op1=ALU.is_ge),
                    reads=[f"lgT{sub}", f"mx{TOPK - 1}"], writes=["wkt"])
                P.op("vector", lambda e: e.tensor_tensor(
                    out=alset, in0=amt, in1=wkt, op=ALU.mult),
                    reads=["amt", "wkt"], writes=["alset"])
                P.op("vector", lambda e: e.reduce_sum(out=mcol[:, 8:9], in_=alset, axis=AX.X),
                     reads=["alset"], writes=["aZ", "mcol"])
                P.op("vector", lambda e: e.reciprocal(out=mcol[:, 9:10], in_=mcol[:, 8:9]),
                     reads=["aZ"], writes=["aZr", "mcol"])
                P.op("vector", lambda e: e.tensor_scalar(
                    out=alset, in0=alset, scalar1=mcol[:, 9:10], scalar2=None, op0=ALU.mult),
                    reads=["alset", "aZr"], writes=["alset"], tag=key + "anorm")
                P.op("scalar", lambda e, b=b, t0=t0, sub=sub: e.dma_start(
                    out=ascr[b, t0 + sub * 128: t0 + sub * 128 + 128, :], in_=alset),
                    reads=["alset"], dma=True, tag=key + "astore", qsem="astore")
                first = (blk == 0 and sub == 0)
                last = (blk == (LIMIT_BLOCKS or NBLK) - 1 and sub == 1)
                xn = xnA[:, sub * D:(sub + 1) * D]
                psi, psiK = ps_si[b]
                for half in range(2):
                    P.op("tensor", lambda e, xn=xn, half=half, first=first, last=last, psi=psi: e.matmul(
                        psi[:, half * 512:(half + 1) * 512], lhsT=alset,
                        rhs=xn[:, half * 512:(half + 1) * 512],
                        start=first, stop=last, skip_group_check=True),
                        reads=["alset", key + f"ln{sub}xn"], writes=psiK, tag=key + "simm")
                pdn, pdnK = ps_dn[b]
                P.op("tensor", lambda e, first=first, last=last, pdn=pdn: e.matmul(
                    pdn, lhsT=alset, rhs=ones[:, 0:8],
                    start=first, stop=last, skip_group_check=True),
                    reads=["alset", "ones"], writes=pdnK, tag=key + "dnmm")

        nblk_a = LIMIT_BLOCKS or NBLK
        for b in range(BPC):
            for blk in range(nblk_a):
                phaseA_block(b, blk)
            P.op("vector", lambda e, b=b: e.tensor_copy(dncol[:, b:b + 1], ps_dn[b][0][:, 0:1]),
                 reads=ps_dn[b][1], writes=[f"dncol{b}", "dncol"])
            P.op("vector", lambda e, b=b: e.tensor_copy(si_sb2[b], ps_si[b][0]),
                 reads=ps_si[b][1], writes=[f"siraw{b}"])
        sbp.pop()
        P.barrier()
        if "B" not in PHASES:
            if DBG_DUMP is not None:
                c0, c1 = DBG_DUMP
                dbg_t = dt("dbg", [128, c1 - c0], F32, kind="ExternalOutput").ap()
                P.barrier()
                P.op("sync", lambda e: e.dma_start(out=dbg_t, in_=sbp.raw(0, 128, c0, c1)),
                     dma=True, qsem="dbgdump")
            P.emit(blk_ctx, mksem, final_eng="sync")
            return nc

        sbp.push()
        s0T_b = sbp.alloc(8 * NSLOT)
        bihc = sbp.alloc(24)
        bhhc = sbp.alloc(24)
        bqkc = sbp.alloc(4)
        wm_b = [sbp.alloc(8 * 128) for _ in range(2)]
        wq_b = sbp.alloc(8 * DQ)
        wk_b = sbp.alloc(8 * DQ)
        wv_b = sbp.alloc(8 * D)
        wvp_b = sbp.alloc(8 * D)
        gh_sb = sbp.alloc(24 * NSLOT)
        gi_sb = sbp.alloc(24 * NSLOT)
        r_sb = sbp.alloc(8 * NSLOT)
        z_sb = sbp.alloc(8 * NSLOT)
        n_sb = sbp.alloc(8 * NSLOT)
        snew = sbp.alloc(8 * NSLOT)
        snewT = sbp.alloc(D, parts=64)
        q_sb = sbp.alloc(2 * NSLOT)
        k_sb = sbp.alloc(2 * NSLOT)
        vT_sb = sbp.alloc(D, parts=64)
        sc_sb = sbp.alloc(NSLOT, parts=64)
        at_sb = sbp.alloc(NSLOT, parts=64)
        y_sb = sbp.alloc(D, parts=64)
        yn_sb = sbp.alloc(D, parts=64)
        ynT_sb = sbp.alloc(NSLOT)
        siT = sbp.alloc(8 * NSLOT)
        colB = sbp.alloc(8, parts=64)
        scrB = sbp.alloc(D, parts=64)
        ggb = sbp.alloc(D, parts=64)
        for _n, _a in [("s0T", s0T_b), ("gh", gh_sb), ("gi", gi_sb), ("r", r_sb),
                       ("z", z_sb), ("n", n_sb), ("snew", snew), ("snewT", snewT),
                       ("q", q_sb), ("k", k_sb), ("vT", vT_sb), ("sc", sc_sb),
                       ("at", at_sb), ("y", y_sb), ("yn", yn_sb), ("siT", siT),
                       ("scrB", scrB), ("si0", si_sb2[0]), ("si1", si_sb2[1])]:
            DBG[_n] = _a.offset

        P.op("sync", lambda e: e.dma_start(
            out=s0T_b.rearrange("p (k f) -> p k f", f=NSLOT),
            in_=slot_initT.rearrange("(k p) f -> p k f", p=128)), writes=["s0T"], dma=True)
        P.op("sync", lambda e: e.dma_start(out=bihc, in_=bih_c), writes=["bihc"], dma=True)
        P.op("sync", lambda e: e.dma_start(out=bhhc, in_=bhh_c), writes=["bhhc"], dma=True)
        P.op("sync", lambda e: e.dma_start(out=bqkc, in_=bqk_c), writes=["bqkc"], dma=True)
        P.op("sync", lambda e: e.dma_start(
            out=wq_b.rearrange("p (k f) -> p k f", f=DQ),
            in_=wq_d.rearrange("(k p) f -> p k f", p=128)), writes=["wqb"], dma=True)
        P.op("sync", lambda e: e.dma_start(
            out=wk_b.rearrange("p (k f) -> p k f", f=DQ),
            in_=wk_d.rearrange("(k p) f -> p k f", p=128)), writes=["wkb"], dma=True)
        P.op("sync", lambda e: e.dma_start(
            out=wv_b.rearrange("p (k f) -> p k f", f=D),
            in_=wv_d.rearrange("(k p) f -> p k f", p=128)), writes=["wvb"], dma=True)
        P.op("sync", lambda e: e.dma_start(
            out=wvp_b.rearrange("p (k f) -> p k f", f=D),
            in_=wvp_d.rearrange("(k p) f -> p k f", p=128)), writes=["wvpb"], dma=True)

        for b in range(BPC):
            P.op("vector", lambda e, b=b: e.tensor_scalar(
                out=colB[:, 0:1], in0=dncol[:, b:b + 1], scalar1=1e-8, scalar2=None, op0=ALU.add),
                reads=[f"dncol{b}"], writes=[f"B{b}_dn1", "colB"])
            P.op("vector", lambda e, b=b: e.reciprocal(out=colB[:, 1:2], in_=colB[:, 0:1]),
                 reads=[f"B{b}_dn1"], writes=[f"B{b}_dnr", "colB"])
            P.op("vector", lambda e, b=b: e.tensor_scalar(
                out=si_sb2[b], in0=si_sb2[b], scalar1=colB[:, 1:2], scalar2=None, op0=ALU.mult),
                reads=[f"siraw{b}", f"B{b}_dnr"], writes=[f"B{b}_si"])

        pgh, pghK = R(0, 64)
        pgi, pgiK = R(64, 128)
        pq, pqK = R(128, 192)
        ptB, ptBK = R(256, 384)
        psc, pscK = R(384, 448, p1=64)
        pvT, pvTK = R(1024, 2048, p1=64)
        pcm, pcmK = R(2048, 3072, p1=64)

        for m in range(24):
            wb = wm_b[m % 2]
            P.op("sync", lambda e, m=m, wb=wb: e.dma_start(
                out=wb.rearrange("p (k f) -> p k f", f=128),
                in_=whhT[:, m * 128:(m + 1) * 128].rearrange("(k p) f -> p k f", p=128)),
                writes=[f"wm{m % 2}"], dma=True)
            for k in range(8):
                P.op("tensor", lambda e, k=k, wb=wb: e.matmul(
                    pgh, lhsT=wb[:, k * 128:(k + 1) * 128],
                    rhs=s0T_b[:, k * NSLOT:(k + 1) * NSLOT],
                    start=(k == 0), stop=(k == 7), skip_group_check=True),
                    reads=[f"wm{m % 2}", "s0T"], writes=pghK)
            P.op("vector", lambda e, m=m: e.tensor_copy(
                gh_sb[:, m * NSLOT:(m + 1) * NSLOT], pgh),
                reads=pghK, writes=[f"gh{m}"])

        for b in range(BPC):
            kb = f"B{b}_"
            for k in range(8):
                P.op("tensor", lambda e, k=k, b=b: e.transpose(
                    ptB[:, 0:64], si_sb2[b][:, k * 128:(k + 1) * 128], ident[0:64, 0:64]),
                    reads=[kb + "si", "ident"], writes=ptBK)
                P.op("vector", lambda e, k=k: e.tensor_copy(
                    siT[:, k * NSLOT:(k + 1) * NSLOT], ptB[:, 0:64]),
                    reads=ptBK, writes=[kb + f"siT{k}"])
            siT_keys = [kb + f"siT{k}" for k in range(8)]
            for m in range(24):
                wb = wm_b[m % 2]
                P.op("sync", lambda e, m=m, wb=wb: e.dma_start(
                    out=wb.rearrange("p (k f) -> p k f", f=128),
                    in_=wihT[:, m * 128:(m + 1) * 128].rearrange("(k p) f -> p k f", p=128)),
                    writes=[f"wm{m % 2}"], dma=True)
                for k in range(8):
                    P.op("tensor", lambda e, k=k, wb=wb: e.matmul(
                        pgi, lhsT=wb[:, k * 128:(k + 1) * 128],
                        rhs=siT[:, k * NSLOT:(k + 1) * NSLOT],
                        start=(k == 0), stop=(k == 7), skip_group_check=True),
                        reads=[f"wm{m % 2}"] + siT_keys, writes=pgiK)
                P.op("vector", lambda e, m=m: e.tensor_copy(
                    gi_sb[:, m * NSLOT:(m + 1) * NSLOT], pgi),
                    reads=pgiK, writes=[kb + f"gi{m}"])
            for out_sb, base in ((r_sb, 0), (z_sb, 8)):
                for m8 in range(8):
                    m = base + m8
                    sl = slice(m8 * NSLOT, (m8 + 1) * NSLOT)
                    P.op("vector", lambda e, m=m, sl=sl, o=out_sb: e.scalar_tensor_tensor(
                        out=o[:, sl], in0=gi_sb[:, m * NSLOT:(m + 1) * NSLOT],
                        scalar=bihc[:, m:m + 1], in1=gh_sb[:, m * NSLOT:(m + 1) * NSLOT],
                        op0=ALU.add, op1=ALU.add),
                        reads=[kb + f"gi{m}", f"gh{m}", "bihc"], writes=[kb + f"g1_{m}"])
                    P.op("vector", lambda e, m=m, sl=sl, o=out_sb: e.tensor_scalar(
                        out=o[:, sl], in0=o[:, sl], scalar1=bhhc[:, m:m + 1],
                        scalar2=None, op0=ALU.add),
                        reads=[kb + f"g1_{m}", "bhhc"], writes=[kb + f"g2_{m}"])
                    P.op("scalar", lambda e, sl=sl, o=out_sb: e.activation(
                        out=o[:, sl], in_=o[:, sl], func=AF.Sigmoid),
                        reads=[kb + f"g2_{m}"], writes=[kb + f"gact{m}"])
            for m8 in range(8):
                m = 16 + m8
                sl = slice(m8 * NSLOT, (m8 + 1) * NSLOT)
                P.op("vector", lambda e, m=m, sl=sl: e.tensor_scalar(
                    out=n_sb[:, sl], in0=gh_sb[:, m * NSLOT:(m + 1) * NSLOT],
                    scalar1=bhhc[:, m:m + 1], scalar2=None, op0=ALU.add),
                    reads=[f"gh{m}", "bhhc"], writes=[kb + f"nh{m8}"])
                P.op("vector", lambda e, m8=m8, sl=sl: e.tensor_tensor(
                    out=n_sb[:, sl], in0=r_sb[:, sl], in1=n_sb[:, sl], op=ALU.mult),
                    reads=[kb + f"gact{m8}", kb + f"nh{m8}"], writes=[kb + f"nr{m8}"])
                P.op("vector", lambda e, m=m, sl=sl: e.scalar_tensor_tensor(
                    out=n_sb[:, sl], in0=gi_sb[:, m * NSLOT:(m + 1) * NSLOT],
                    scalar=bihc[:, m:m + 1], in1=n_sb[:, sl], op0=ALU.add, op1=ALU.add),
                    reads=[kb + f"gi{m}", kb + f"nr{m8}", "bihc"], writes=[kb + f"np{m8}"])
                P.op("scalar", lambda e, sl=sl: e.activation(
                    out=n_sb[:, sl], in_=n_sb[:, sl], func=AF.Tanh),
                    reads=[kb + f"np{m8}"], writes=[kb + f"n{m8}"])
                P.op("vector", lambda e, m8=m8, sl=sl: e.tensor_tensor(
                    out=snew[:, sl], in0=s0T_b[:, sl], in1=n_sb[:, sl], op=ALU.subtract),
                    reads=["s0T", kb + f"n{m8}"], writes=[kb + f"sd{m8}"])
                P.op("vector", lambda e, m8=m8, sl=sl: e.tensor_tensor(
                    out=snew[:, sl], in0=z_sb[:, sl], in1=snew[:, sl], op=ALU.mult),
                    reads=[kb + f"gact{8 + m8}", kb + f"sd{m8}"], writes=[kb + f"zs{m8}"])
                P.op("vector", lambda e, m8=m8, sl=sl: e.tensor_tensor(
                    out=snew[:, sl], in0=n_sb[:, sl], in1=snew[:, sl], op=ALU.add),
                    reads=[kb + f"n{m8}", kb + f"zs{m8}"], writes=[kb + f"sn{m8}"])
            snew_keys = [kb + f"sn{m8}" for m8 in range(8)]
            for k in range(8):
                P.op("tensor", lambda e, k=k: e.transpose(
                    ptB[0:64, 0:128], snew[:, k * NSLOT:(k + 1) * NSLOT], ident[:]),
                    reads=snew_keys + ["ident"], writes=ptBK)
                P.op("vector", lambda e, k=k: e.tensor_copy(
                    snewT[:, k * 128:(k + 1) * 128], ptB[0:64, 0:128]),
                    reads=ptBK, writes=[kb + f"snT{k}"])
            snewT_keys = [kb + f"snT{k}" for k in range(8)]
            for wb_, bi, dst, wkey, rkey in ((wq_b, 0, q_sb, "q", "wqb"),
                                             (wk_b, 2, k_sb, "k", "wkb")):
                for c in range(2):
                    for k in range(8):
                        P.op("tensor", lambda e, c=c, k=k, wb_=wb_: e.matmul(
                            pq, lhsT=wb_[:, k * DQ + c * 128: k * DQ + c * 128 + 128],
                            rhs=snew[:, k * NSLOT:(k + 1) * NSLOT],
                            start=(k == 0), stop=(k == 7), skip_group_check=True),
                            reads=[rkey] + snew_keys, writes=pqK)
                    P.op("vector", lambda e, c=c, dst=dst, bi=bi: e.tensor_scalar(
                        out=dst[:, c * NSLOT:(c + 1) * NSLOT], in0=pq,
                        scalar1=bqkc[:, bi + c:bi + c + 1], scalar2=None, op0=ALU.add),
                        reads=pqK + ["bqkc"], writes=[kb + wkey + str(c)])
            for nh in range(2):
                for k in range(8):
                    P.op("tensor", lambda e, nh=nh, k=k: e.matmul(
                        pvT[:, nh * 512:(nh + 1) * 512],
                        lhsT=snew[:, k * NSLOT:(k + 1) * NSLOT],
                        rhs=wv_b[:, k * D + nh * 512: k * D + nh * 512 + 512],
                        start=(k == 0), stop=(k == 7), skip_group_check=True),
                        reads=["wvb"] + snew_keys, writes=pvTK)
            P.op("sync", lambda e: e.dma_start(out=ggb, in_=bv_r), writes=["ggb"], dma=True)
            P.op("vector", lambda e: e.tensor_tensor(out=vT_sb, in0=pvT, in1=ggb, op=ALU.add),
                 reads=pvTK + ["ggb"], writes=[kb + "vT"])
            for c in range(2):
                P.op("tensor", lambda e, c=c: e.matmul(
                    psc, lhsT=q_sb[:, c * NSLOT:(c + 1) * NSLOT],
                    rhs=k_sb[:, c * NSLOT:(c + 1) * NSLOT],
                    start=(c == 0), stop=(c == 1), skip_group_check=True),
                    reads=[kb + "q" + str(c), kb + "k" + str(c)], writes=pscK)
            P.op("vector", lambda e: e.tensor_scalar(
                out=sc_sb, in0=psc, scalar1=DQ ** -0.5, scalar2=None, op0=ALU.mult),
                reads=pscK, writes=[kb + "sc"])
            P.op("vector", lambda e: e.reduce_max(out=colB[:, 2:3], in_=sc_sb,
                                                  axis=AX.X, negate=True),
                 reads=[kb + "sc"], writes=[kb + "scm", "colB"])
            P.op("scalar", lambda e: e.activation(out=sc_sb, in_=sc_sb, func=AF.Exp,
                                                  bias=colB[:, 2:3]),
                 reads=[kb + "sc", kb + "scm"], writes=[kb + "sce"])
            P.op("vector", lambda e: e.reduce_sum(out=colB[:, 3:4], in_=sc_sb, axis=AX.X),
                 reads=[kb + "sce"], writes=[kb + "scs", "colB"])
            P.op("vector", lambda e: e.reciprocal(out=colB[:, 4:5], in_=colB[:, 3:4]),
                 reads=[kb + "scs"], writes=[kb + "scz", "colB"])
            P.op("vector", lambda e: e.tensor_scalar(
                out=sc_sb, in0=sc_sb, scalar1=colB[:, 4:5], scalar2=None, op0=ALU.mult),
                reads=[kb + "sce", kb + "scz"], writes=[kb + "attn"])
            P.op("tensor", lambda e: e.transpose(
                ptB[0:64, 0:64], sc_sb, ident[0:64, 0:64]),
                reads=[kb + "attn", "ident"], writes=ptBK)
            P.op("vector", lambda e: e.tensor_copy(at_sb, ptB[0:64, 0:64]),
                 reads=ptBK, writes=[kb + "attnT"])
            for nh in range(2):
                P.op("tensor", lambda e, nh=nh: e.matmul(
                    pcm[:, nh * 512:(nh + 1) * 512], lhsT=at_sb,
                    rhs=vT_sb[:, nh * 512:(nh + 1) * 512],
                    start=True, stop=True, skip_group_check=True),
                    reads=[kb + "attnT", kb + "vT"], writes=pcmK)
            P.op("vector", lambda e: e.tensor_tensor(out=y_sb, in0=pcm, in1=snewT, op=ALU.add),
                 reads=pcmK + snewT_keys, writes=[kb + "y"])
            P.op("vector", lambda e: e.reduce_sum(out=colB[:, 0:1], in_=y_sb, axis=AX.X),
                 reads=[kb + "y"], writes=[kb + "ys", "colB"])
            P.op("vector", lambda e: e.tensor_tensor(out=scrB, in0=y_sb, in1=y_sb,
                                                     op=ALU.mult),
                 reads=[kb + "y"], writes=["scrB"])
            P.op("vector", lambda e: e.reduce_sum(out=colB[:, 1:2], in_=scrB, axis=AX.X),
                 reads=["scrB"], writes=[kb + "yq", "colB"])
            P.op("vector", lambda e: e.tensor_scalar(out=colB[:, 0:1], in0=colB[:, 0:1],
                                                     scalar1=1.0 / D, scalar2=None, op0=ALU.mult),
                 reads=[kb + "ys"], writes=[kb + "ym", "colB"])
            P.op("vector", lambda e: e.tensor_tensor(out=colB[:, 2:3], in0=colB[:, 0:1],
                                                     in1=colB[:, 0:1], op=ALU.mult),
                 reads=[kb + "ym"], writes=[kb + "ym2", "colB"])
            P.op("vector", lambda e: e.tensor_scalar(out=colB[:, 1:2], in0=colB[:, 1:2],
                                                     scalar1=1.0 / D, scalar2=None, op0=ALU.mult),
                 reads=[kb + "yq"], writes=[kb + "yq2", "colB"])
            P.op("vector", lambda e: e.tensor_tensor(out=colB[:, 1:2], in0=colB[:, 1:2],
                                                     in1=colB[:, 2:3], op=ALU.subtract),
                 reads=[kb + "yq2", kb + "ym2"], writes=[kb + "yv", "colB"])
            P.op("scalar", lambda e: e.activation(out=colB[:, 1:2], in_=colB[:, 1:2],
                                                  func=AF.Ln, bias=konst[0:64, 0:1]),
                 reads=[kb + "yv", "konst"], writes=[kb + "yl", "colB"])
            P.op("scalar", lambda e: e.activation(out=colB[:, 1:2], in_=colB[:, 1:2],
                                                  func=AF.Exp, scale=konst[0:64, 1:2]),
                 reads=[kb + "yl", "konst"], writes=[kb + "yi", "colB"])
            P.op("vector", lambda e: e.tensor_tensor(out=colB[:, 2:3], in0=colB[:, 0:1],
                                                     in1=colB[:, 1:2], op=ALU.mult),
                 reads=[kb + "ym", kb + "yi"], writes=[kb + "ymi", "colB"])
            P.op("vector", lambda e: e.tensor_scalar(
                out=yn_sb, in0=y_sb, scalar1=colB[:, 1:2], scalar2=colB[:, 2:3],
                op0=ALU.mult, op1=ALU.subtract),
                reads=[kb + "y", kb + "yi", kb + "ymi"], writes=[kb + "yn0"])
            P.op("sync", lambda e: e.dma_start(out=ggb, in_=lng_r), writes=["ggb"], dma=True)
            P.op("vector", lambda e: e.tensor_tensor(out=yn_sb, in0=yn_sb, in1=ggb, op=ALU.mult),
                 reads=[kb + "yn0", "ggb"], writes=[kb + "yn1"])
            P.op("sync", lambda e: e.dma_start(out=ggb, in_=lnb_r), writes=["ggb"], dma=True)
            P.op("vector", lambda e: e.tensor_tensor(out=yn_sb, in0=yn_sb, in1=ggb, op=ALU.add),
                 reads=[kb + "yn1", "ggb"], writes=[kb + "yn"])
            for k in range(8):
                P.op("tensor", lambda e, k=k: e.transpose(
                    ptB[:, 0:64], yn_sb[:, k * 128:(k + 1) * 128], ident[0:64, 0:64]),
                    reads=[kb + "yn", "ident"], writes=ptBK)
                P.op("vector", lambda e: e.tensor_copy(ynT_sb, ptB[:, 0:64]),
                     reads=ptBK, writes=[kb + "ynT"])
                for nh in range(2):
                    P.op("tensor", lambda e, k=k, nh=nh: e.matmul(
                        pvT[:, nh * 512:(nh + 1) * 512], lhsT=ynT_sb,
                        rhs=wvp_b[:, k * D + nh * 512: k * D + nh * 512 + 512],
                        start=(k == 0), stop=(k == 7), skip_group_check=True),
                        reads=[kb + "ynT", "wvpb"], writes=pvTK)
            P.op("sync", lambda e: e.dma_start(out=ggb, in_=bvp_r), writes=["ggb"], dma=True)
            P.op("vector", lambda e, b=b: e.tensor_tensor(
                out=sv_sb[b], in0=pvT, in1=ggb, op=ALU.add),
                reads=pvTK + ["ggb"], writes=[f"sv{b}"])
        sbp.pop()
        P.barrier()
        if "C" not in PHASES:
            if DBG_DUMP is not None:
                c0, c1 = DBG_DUMP
                dbg_t = dt("dbg", [128, c1 - c0], F32, kind="ExternalOutput").ap()
                P.barrier()
                P.op("sync", lambda e: e.dma_start(out=dbg_t, in_=sbp.raw(0, 128, c0, c1)),
                     dma=True, qsem="dbgdump")
            P.emit(blk_ctx, mksem, final_eng="sync")
            return nc

        sbp.push()
        xC = sbp.alloc(2 * D)
        lncolC = sbp.alloc(8)
        xnTC = sbp.alloc(8 * TB)
        wing = [sbp.alloc(8 * 128) for _ in range(2)]
        u_raw = sbp.alloc(NFC * SEG)
        uconv = sbp.alloc(NFC * TB)
        go_sl = [sbp.alloc(NFC * TB) for _ in range(2)]
        delta = sbp.alloc(NFC * TB)
        du_sl = sbp.alloc(NFC * TB)
        y_sl = sbp.alloc(NFC * TB)
        g_sb2 = sbp.alloc(TB, parts=96)
        _sio = si_sb2[0].offset
        _sio1 = si_sb2[1].offset
        brep = [sbp.raw(0, 128, _sio + i * TB, _sio + (i + 1) * TB) for i in range(2)]
        crep = [sbp.raw(0, 128, _sio + (2 + i) * TB, _sio + (3 + i) * TB) for i in range(2)]
        fbuf = [sbp.raw(0, 128, _sio1 + i * TB, _sio1 + (i + 1) * TB) for i in range(4)]
        dtmp = fbuf[0]
        gslab = sbp.alloc(NFC * TB)
        hslab = sbp.alloc(8 * TB)
        mslab = sbp.alloc(8 * TB)
        fin_d = mslab
        woutb = None
        wob = None
        aC = sbp.alloc(2 * NSLOT)
        f0T_sb = delta[0:128, 0:2048]
        aT_sb = delta[0:64, 2048:2048 + 2 * 128]

        ps_uo = [R(0, 256), R(512, 768), R(256, 512), R(768, 1024)]
        ps_g = R(1024, 1280, p1=96)
        ps_d = [R(1280, 1536), R(1536, 1792)]
        ps_f0 = [R(1024, 1280), R(1536, 1792), R(1280, 1536), R(1792, 2048)]
        ps_tpx = [R(0, 128), R(512, 640)]
        ps_tpa = [R(1280, 1408, p1=64), R(1792, 1920, p1=64)]
        ps_fin = [[R(2048 + (sub * 2 + nh) * 512, 2048 + (sub * 2 + nh + 1) * 512)
                   for nh in range(2)] for sub in range(2)]

        def seg(slab, fc, w=TB):
            return slab[:, fc * w:(fc + 1) * w]

        u3d = u_raw.rearrange("p (c t) -> p c t", t=SEG)
        wslots = [(wing[0], "wing0"), (wing[1], "wing1"),
                  (xC[:, 0:D], "xCa"), (xC[:, D:2 * D], "xCb")]

        def stage_front(b, blk):
            t0 = blk * TB
            key = f"C{b}_{blk}"
            P.op("sync", lambda e: e.dma_start(
                out=xC.rearrange("p (s d) -> p s d", d=D),
                in_=x_in[b, t0:t0 + TB, :].rearrange("(s p) d -> p s d", p=128)),
                writes=["xCa", "xCb"], dma=True, tag=key + "xload")
            lnscr = u_raw[:, 2048:3072]
            for sub in range(2):
                xs = xC[:, sub * D:(sub + 1) * D]
                xn = u_raw[:, sub * D:(sub + 1) * D]
                layernorm(xs, xn, lncolC, lnscr, key + f"ln{sub}", "xCa" if sub == 0 else "xCb", "C", xtra=("u_raw",))
                for k in range(8):
                    tp, tpK = ps_tpx[k % 2]
                    P.op("tensor", lambda e, k=k, xn=xn, tp=tp: e.transpose(
                        tp, xn[:, k * 128:(k + 1) * 128], ident[:]),
                        reads=[key + f"ln{sub}xn", "u_raw", "ident"], writes=tpK)
                    P.op("vector", lambda e, k=k, sub=sub, tp=tp: e.tensor_copy(
                        xnTC[:, k * TB + sub * 128: k * TB + sub * 128 + 128], tp),
                        reads=tpK, writes=[f"CxnT_{k}_{sub}"])
            if blk == 0 and b == 0:
                P.op("vector", lambda e: e.memset(u3d[:, :, 0:3], 0.0),
                     reads=["u_raw"], writes=["u_raw"], tag=key + "hist0")
            elif blk == 0:
                P.op("vector", lambda e: e.memset(u3d[:, :, 0:3], 0.0),
                     reads=["u_raw"], writes=["u_raw"], tag=key + "hist0")
            else:
                P.op("vector", lambda e: e.tensor_copy(
                    u3d[:, :, 0:3], hist_t.rearrange("p (c t) -> p c t", t=3)),
                    reads=["u_raw", "hist_t"], writes=["u_raw"], tag=key + "hist")

        def stage_mid_grp(b, blk, par, grps):
            key = f"C{b}_{blk}"
            xnT_keys = [f"CxnT_{k}_{s}" for k in range(8) for s in range(2)]
            for grp in grps:
                wb = wing[grp % 2]
                P.op("sync", lambda e, grp=grp, wb=wb: e.dma_start(
                    out=wb, in_=w_inT[grp]),
                    writes=[f"wing{grp % 2}"], dma=True, tag=key + "windma")
                pm, pmK = ps_uo[grp % 4]
                for k in range(8):
                    P.op("tensor", lambda e, k=k, wb=wb, pm=pm: e.matmul(
                        pm, lhsT=wb[:, k * 128:(k + 1) * 128],
                        rhs=xnTC[:, k * TB:(k + 1) * TB],
                        start=(k == 0), stop=(k == 7), skip_group_check=True),
                        reads=[f"wing{grp % 2}"] + xnT_keys, writes=pmK,
                        tag=key + "uomm")
                fc = grp % 16
                if grp < 16:
                    P.op("vector", lambda e, fc=fc, pm=pm: e.tensor_copy(
                        u3d[:, fc, 3:3 + TB], pm),
                        reads=pmK, writes=["u_raw"], tag=key + "uevac")
                else:
                    P.op("scalar", lambda e, fc=fc, pm=pm, par=par: e.activation(
                        out=seg(go_sl[par], fc), in_=pm, func=AF.Copy),
                        reads=pmK, writes=[f"go_{par}_{fc}"], tag=key + "goevac")

        def stage_mid_post(b, blk):
            key = f"C{b}_{blk}"
            for q in range(NFC // 2):
                for j in range(4):
                    for fc in (2 * q, 2 * q + 1):
                        base = fc * SEG
                        srcv = u_raw[:, base + j:base + j + TB]
                        if j == 0:
                            P.op("vector", lambda e, srcv=srcv, fc=fc: e.tensor_scalar(
                                out=seg(uconv, fc), in0=srcv, scalar1=cwc[:, fc * 4:fc * 4 + 1],
                                scalar2=None, op0=ALU.mult),
                                reads=["u_raw", "cwc"], writes=[f"uc_{fc}"], tag=key + "conv")
                        else:
                            P.op("vector", lambda e, srcv=srcv, fc=fc, j=j: e.scalar_tensor_tensor(
                                out=seg(uconv, fc), in0=srcv, scalar=cwc[:, fc * 4 + j:fc * 4 + j + 1],
                                in1=seg(uconv, fc), op0=ALU.mult, op1=ALU.add),
                                reads=["u_raw", "cwc", f"uc_{fc}"], writes=[f"uc_{fc}"],
                                tag=key + "conv")
                for fc in (2 * q, 2 * q + 1):
                    P.op("scalar", lambda e, fc=fc: e.activation(
                        out=seg(uconv, fc), in_=seg(uconv, fc), func=AF.Silu,
                        bias=cbc[:, fc:fc + 1]),
                        reads=[f"uc_{fc}", "cbc"], writes=[f"uc_{fc}"], tag=key + "u2silu")
            P.op("vector", lambda e: e.tensor_copy(
                hist_t.rearrange("p (c t) -> p c t", t=3), u3d[:, :, TB:TB + 3]),
                reads=["u_raw"], writes=["hist_t"], tag=key + "tails")

        def stage_gate(b, blk):
            key = f"C{b}_{blk}"
            uc_keys = [f"uc_{fc}" for fc in range(NFC)]
            pg, pgK = ps_g
            for k in range(NFC):
                P.op("tensor", lambda e, k=k, pg=pg: e.matmul(
                    pg, lhsT=wgT_b[:, k * 96:(k + 1) * 96], rhs=seg(uconv, k),
                    start=(k == 0), stop=(k == NFC - 1), skip_group_check=True),
                    reads=["wgT", f"uc_{k}"], writes=pgK, tag=key + "gmm")
            P.op("vector", lambda e, pg=pg: e.tensor_copy(g_sb2, pg),
                 reads=pgK, writes=["gsb"], tag=key + "gevac")
            for m in range(NFC):
                pd, pdK = ps_d[m % 2]
                P.op("tensor", lambda e, m=m, pd=pd: e.matmul(
                    pd, lhsT=wdT_b[:, m * 128:(m + 1) * 128], rhs=g_sb2[0:64, :],
                    start=True, stop=True, skip_group_check=True),
                    reads=["wdT", "gsb"], writes=pdK, tag=key + "dmm")
                P.op("scalar", lambda e, m=m, pd=pd: e.activation(
                    out=dtmp, in_=pd, func=AF.Exp, bias=dbc[:, m:m + 1]),
                    reads=pdK + ["dbc"], writes=["dtmp", "fb0"], tag=key + "dexp")
                P.op("scalar", lambda e, m=m: e.activation(
                    out=seg(delta, m), in_=dtmp, func=AF.Ln, bias=1.0),
                    reads=["dtmp", "fb0", "dreg"], writes=[f"dl_{m}", "dreg"], tag=key + "dln")
            P.op("vector", lambda e: e.tensor_tensor(out=du_sl, in0=delta, in1=uconv,
                                                     op=ALU.mult),
                 reads=[f"dl_{m}" for m in range(NFC)] + uc_keys + ["dreg"],
                 writes=["du"], tag=key + "du")

        def stage_scan(b, blk, par, midnxt):
            t0s = blk * TB
            key = f"C{b}_{blk}"
            P.op("sync", lambda e, t0s=t0s, b=b: e.dma_start(
                out=aC.rearrange("p (s a) -> p s a", a=NSLOT),
                in_=ascr[b, t0s:t0s + TB, :].rearrange("(s p) a -> p s a", p=128)),
                writes=["aC"], dma=True, tag=key + "aload")

            def embrep(s):
                br, cr = brep[s % 2], crep[s % 2]
                P.op("sync", lambda e, s=s, br=br: e.dma_start(
                    out=br, in_=g_sb2[64 + s:65 + s, :].unsqueeze(1).broadcast_to((1, 128, TB))),
                    reads=["gsb"], writes=[f"brep{s % 2}"], dma=True, tag=key + "brep")
                P.op("sync", lambda e, s=s, cr=cr: e.dma_start(
                    out=cr, in_=g_sb2[80 + s:81 + s, :].unsqueeze(1).broadcast_to((1, 128, TB))),
                    reads=["gsb"], writes=[f"crep{s % 2}"], dma=True, tag=key + "crep")

            def emgbuild(s, gq, eng):
                br = brep[s % 2]
                gsl = gslab[:, gq * 4 * TB:(gq + 1) * 4 * TB]
                P.op(eng, lambda e, br=br, gsl=gsl: e.tensor_tensor(
                    out=gsl.rearrange("p (c t) -> p c t", t=TB),
                    in0=du_sl[:, gq * 4 * TB:(gq + 1) * 4 * TB].rearrange("p (c t) -> p c t", t=TB),
                    in1=br.unsqueeze(1).broadcast_to((128, 4, TB)), op=ALU.mult),
                    reads=["du", f"brep{s % 2}"], writes=[f"gsl{gq}"],
                    tag=key + "gbuild")

            embrep(0)
            for gq in range(4):
                emgbuild(0, gq, "vector")
            for s in range(NS):
                if s + 1 < NS:
                    embrep(s + 1)
                for half in range(2):
                    for fc8 in range(8):
                        fc = half * 8 + fc8
                        fb = fbuf[fc % 4]
                        P.op("scalar", lambda e, s=s, fc=fc, fb=fb: e.activation(
                            out=fb, in_=seg(delta, fc), func=AF.Exp,
                            scale=acol[:, fc * NS + s:fc * NS + s + 1]),
                            reads=[f"dl_{fc}", "acol", "dreg"], writes=[f"fb{fc % 4}"],
                            tag=key + "fexp")
                        P.op("vector", lambda e, s=s, fc=fc, fc8=fc8, fb=fb, b=b: e.tensor_tensor_scan(
                            out=seg(hslab, fc8), data0=fb, data1=seg(gslab, fc),
                            initial=carry[b][:, s * NFC + fc:s * NFC + fc + 1],
                            op0=ALU.mult, op1=ALU.add),
                            reads=[f"fb{fc % 4}", f"gsl{fc // 4}", f"ca_{b}_{s}_{fc}"],
                            writes=[f"h_{fc8}"], tag=key + "scan")
                        if s + 1 < NS and fc8 == 3:
                            emgbuild(s + 1, half * 2, "vector")
                        if s + 1 < NS and fc8 == 7:
                            emgbuild(s + 1, half * 2 + 1, "vector")
                    P.op("vector", lambda e, s=s, b=b, half=half: e.tensor_copy(
                        carry[b][:, s * NFC + half * 8:s * NFC + half * 8 + 8],
                        hslab.rearrange("p (c t) -> p c t", t=TB)[:, :, TB - 1]),
                        reads=[f"h_{j}" for j in range(8)],
                        writes=[f"ca_{b}_{s}_{half * 8 + j}" for j in range(8)],
                        tag=key + "carry")
                    if midnxt is not None:
                        stage_mid_grp(midnxt[0], midnxt[1], midnxt[2],
                                      [s * 2 + half])
                    for yq in range(2):
                        q0 = half * 8 * TB + yq * 4 * TB
                        ysl_q = y_sl[:, q0:q0 + 4 * TB]
                        h_q = hslab[:, yq * 4 * TB:(yq + 1) * 4 * TB]
                        m_q = mslab[:, yq * 4 * TB:(yq + 1) * 4 * TB]
                        hkeys = [f"h_{yq * 4 + j}" for j in range(4)]
                        if s == 0:
                            P.op("vector", lambda e, cr=crep[s % 2], ysl_q=ysl_q, h_q=h_q: e.tensor_tensor(
                                out=ysl_q.rearrange("p (c t) -> p c t", t=TB),
                                in0=h_q.rearrange("p (c t) -> p c t", t=TB),
                                in1=cr.unsqueeze(1).broadcast_to((128, 4, TB)), op=ALU.mult),
                                reads=hkeys + [f"crep{s % 2}"],
                                writes=["y_sl"], tag=key + "ymul")
                        else:
                            P.op("vector", lambda e, cr=crep[s % 2], h_q=h_q, m_q=m_q: e.tensor_tensor(
                                out=m_q.rearrange("p (c t) -> p c t", t=TB),
                                in0=h_q.rearrange("p (c t) -> p c t", t=TB),
                                in1=cr.unsqueeze(1).broadcast_to((128, 4, TB)), op=ALU.mult),
                                reads=hkeys + [f"crep{s % 2}"],
                                writes=["mslab"], tag=key + "ymul")
                            P.op("vector", lambda e, ysl_q=ysl_q, m_q=m_q: e.tensor_tensor(
                                out=ysl_q, in0=ysl_q, in1=m_q, op=ALU.add),
                                reads=["y_sl", "mslab"], writes=["y_sl"], tag=key + "yadd")
            for fc in range(NFC):
                P.op("scalar", lambda e, fc=fc, par=par: e.activation(
                    out=seg(go_sl[par], fc), in_=seg(go_sl[par], fc), func=AF.Silu),
                    reads=[f"go_{par}_{fc}"], writes=[f"go_{par}_{fc}"], tag=key + "gosilu")
            for fc in range(NFC):
                P.op("vector", lambda e, fc=fc: e.scalar_tensor_tensor(
                    out=seg(mslab, fc % 8), in0=seg(uconv, fc), scalar=dcolv[:, fc:fc + 1],
                    in1=seg(y_sl, fc), op0=ALU.mult, op1=ALU.add),
                    reads=[f"uc_{fc}", "y_sl", "dcolv"], writes=[f"t2_{fc}", "mslab"],
                    tag=key + "du2")
                P.op("vector", lambda e, fc=fc, par=par: e.tensor_tensor(
                    out=seg(y_sl, fc), in0=seg(mslab, fc % 8), in1=seg(go_sl[par], fc), op=ALU.mult),
                    reads=[f"t2_{fc}", f"go_{par}_{fc}"], writes=["y_sl", f"y2_{fc}"],
                    tag=key + "y2")

        def stage_tail_a(b, blk):
            t0 = blk * TB
            key = f"C{b}_{blk}"
            for sub in range(2):
                for nh in range(2):
                    pf, pfK = ps_fin[sub][nh]
                    P.op("tensor", lambda e, sub=sub, nh=nh, pf=pf: e.matmul(
                        pf, lhsT=ones[64:65, 0:128],
                        rhs=bo_row[:, nh * 512:(nh + 1) * 512],
                        start=True, stop=False, skip_group_check=True),
                        reads=["ones", "bo_row"], writes=pfK, tag=key + "bomm")
            for sub in range(2):
                pa, paK = ps_tpa[sub]
                P.op("tensor", lambda e, sub=sub, pa=pa: e.transpose(
                    pa, aC[:, sub * NSLOT:(sub + 1) * NSLOT], ident[:]),
                    reads=["aC", "ident"], writes=paK)
                P.op("vector", lambda e, sub=sub, pa=pa: e.tensor_copy(
                    aT_sb[:, sub * 128:(sub + 1) * 128], pa),
                    reads=paK + ["dreg"], writes=[f"aT{sub}", "dreg"])
            for c in range(NFC):
                wb, wbK = wslots[c % 4]
                P.op("sync", lambda e, c=c, wb=wb: e.dma_start(
                    out=wb, in_=w_outT[c * 128:(c + 1) * 128, :]),
                    writes=[wbK], dma=True, tag=key + "woutdma")
                for sub in range(2):
                    for nh in range(2):
                        pf, pfK = ps_fin[sub][nh]
                        P.op("tensor", lambda e, c=c, sub=sub, nh=nh, wb=wb, pf=pf: e.matmul(
                            pf, lhsT=seg(y_sl, c)[:, sub * 128:sub * 128 + 128],
                            rhs=wb[:, nh * 512:(nh + 1) * 512],
                            start=False, stop=False, skip_group_check=True),
                            reads=[wbK, f"y2_{c}", "y_sl"], writes=pfK,
                            tag=key + "woutmm")
        def stage_tail_b(b, blk):
            t0 = blk * TB
            key = f"C{b}_{blk}"
            for m in range(8):
                pf0, pf0K = ps_f0[m % 4]
                P.op("tensor", lambda e, m=m, b=b, pf0=pf0: e.matmul(
                    pf0, lhsT=sv_sb[b][:, m * 128:(m + 1) * 128], rhs=aT_sb,
                    start=True, stop=True, skip_group_check=True),
                    reads=[f"sv{b}", "aT0", "aT1", "dreg"], writes=pf0K, tag=key + "f0mm")
                P.op("vector", lambda e, m=m, pf0=pf0: e.tensor_copy(
                    f0T_sb[:, m * TB:(m + 1) * TB], pf0),
                    reads=pf0K + ["dreg"], writes=[f"f0T{m}", "dreg"])
            for k in range(8):
                wb, wbK = wslots[k % 4]
                P.op("sync", lambda e, k=k, wb=wb: e.dma_start(
                    out=wb, in_=wo_d[k * 128:(k + 1) * 128, :]),
                    writes=[wbK], dma=True, tag=key + "wodma")
                for sub in range(2):
                    for nh in range(2):
                        pf, pfK = ps_fin[sub][nh]
                        P.op("tensor", lambda e, k=k, sub=sub, nh=nh, wb=wb, pf=pf: e.matmul(
                            pf, lhsT=f0T_sb[:, k * TB + sub * 128: k * TB + sub * 128 + 128],
                            rhs=wb[:, nh * 512:(nh + 1) * 512],
                            start=False, stop=(k == 7), skip_group_check=True),
                            reads=[wbK, f"f0T{k}", "dreg"], writes=pfK,
                            tag=key + "womm")
            for sub in range(2):
                for nh in range(2):
                    pf, pfK = ps_fin[sub][nh]
                    idx = sub * 2 + nh
                    dst = fin_d[:, idx * 512:(idx + 1) * 512]
                    P.op("vector", lambda e, pf=pf, dst=dst: e.tensor_copy(dst, pf),
                         reads=pfK, writes=["finP", "mslab"], tag=key + "finevac")
            P.op("gpsimd", lambda e, b=b, t0=t0: e.dma_start(
                out=out_hbm[b, t0:t0 + 128, :],
                in_=fin_d[:, 0:1024], accum_op=ALU.add),
                reads=["finP", "mslab"], dma=True, tag=key + "acc0", qsem="acc0")
            P.op("gpsimd", lambda e, b=b, t0=t0: e.dma_start(
                out=out_hbm[b, t0 + 128:t0 + 256, :],
                in_=fin_d[:, 1024:2048], accum_op=ALU.add),
                reads=["finP", "mslab"], dma=True, tag=key + "acc1", qsem="acc1")

        nblk_c = LIMIT_BLOCKS or NBLK
        for b in range(BPC):
            P.op("vector", lambda e, b=b: e.memset(carry[b][:], 0.0),
                 writes=[f"ca_{b}_{s}_{fc}" for s in range(NS) for fc in range(NFC)])
        seqC = [(b, blk) for b in range(BPC) for blk in range(nblk_c)]
        stage_front(*seqC[0])
        stage_mid_grp(*seqC[0], 0, range(32))
        stage_mid_post(*seqC[0])
        stage_gate(*seqC[0])
        for i, cur in enumerate(seqC):
            nxt = seqC[i + 1] if i + 1 < len(seqC) else None
            mn = (nxt + ((i + 1) % 2,)) if nxt else None
            if nxt:
                stage_front(*nxt)
            stage_scan(*cur, i % 2, mn)
            stage_tail_a(*cur)
            if nxt:
                stage_mid_post(*nxt)
            stage_tail_b(*cur)
            if nxt:
                stage_gate(*nxt)
        sbp.pop()

        if DBG_DUMP is not None:
            c0, c1 = DBG_DUMP
            dbg_t = dt("dbg", [128, c1 - c0], F32, kind="ExternalOutput").ap()
            P.barrier()
            P.op("sync", lambda e: e.dma_start(out=dbg_t, in_=sbp.raw(0, 128, c0, c1)),
                 dma=True, qsem="dbgdump")
        P.emit(blk_ctx, mksem, final_eng="sync")
    return nc


_NC_CACHE = {}


def _get_nc():
    if "nc" not in _NC_CACHE:
        _NC_CACHE["nc"] = build_program()
    return _NC_CACHE["nc"]


def kernel(**inp):
    def c(a):
        return np.ascontiguousarray(np.asarray(a, dtype=np.float32))

    x = c(inp["L0_attn_x"])

    def cols(v, n):
        return np.ascontiguousarray(v.reshape(n, 128).T)

    W_in = c(inp["L0_attn_W_in"]); conv_w = c(inp["L0_attn_conv_w"])
    conv_b = c(inp["L0_attn_conv_b"]); W_gate = c(inp["L0_attn_W_gate"])
    W_delta = c(inp["L0_attn_W_delta"]); delta_bias = c(inp["L0_attn_delta_bias"])
    A_log = c(inp["L0_attn_A_log"]); Dv = c(inp["L0_attn_D"])
    W_out = c(inp["L0_attn_W_out"]); slot_init = c(inp["L0_ffn_slot_init"])
    w1 = c(inp["L0_ffn_w1"]); b1 = c(inp["L0_ffn_b1"])
    w2 = c(inp["L0_ffn_w2"]); b2 = c(inp["L0_ffn_b2"])
    tau = c(inp["L0_ffn_tau"]); Wih = c(inp["L0_ffn_Wih"]); Whh = c(inp["L0_ffn_Whh"])
    bih = c(inp["L0_ffn_bih"]); bhh = c(inp["L0_ffn_bhh"])
    wq = c(inp["L0_ffn_wq"]); bq = c(inp["L0_ffn_bq"])
    wk = c(inp["L0_ffn_wk"]); bk = c(inp["L0_ffn_bk"])
    wv = c(inp["L0_ffn_wv"]); bv = c(inp["L0_ffn_bv"])
    ln_g = c(inp["L0_ffn_ln_g"]); ln_b = c(inp["L0_ffn_ln_b"])
    wvp = c(inp["L0_ffn_wvp"]); bvp = c(inp["L0_ffn_bvp"])
    wo = c(inp["L0_ffn_wo"]); bo = c(inp["L0_ffn_bo"])

    common = {
        "w_inT": np.ascontiguousarray(W_in.T.reshape(8, 128, 32, 128).transpose(2, 1, 0, 3).reshape(32, 128, 1024)),
        "conv_w_c": np.ascontiguousarray(conv_w[:, 0, :].reshape(NFC, 128, 4)
                                         .transpose(1, 0, 2).reshape(128, NFC * 4)),
        "conv_b_c": cols(conv_b, NFC),
        "w_gateT": np.ascontiguousarray(W_gate.T),
        "w_deltaT": np.ascontiguousarray(W_delta.T),
        "dbias_c": cols(delta_bias, NFC),
        "alog_c": np.ascontiguousarray(A_log.reshape(NFC, 128, NS)
                                       .transpose(1, 0, 2).reshape(128, NFC * NS)),
        "dvec_c": cols(Dv, NFC),
        "w_outT": np.ascontiguousarray(W_out.T),
        "slot_init": slot_init,
        "slot_initT": np.ascontiguousarray(slot_init.T),
        "w1a": np.ascontiguousarray(w1[:D]),
        "w1b": np.ascontiguousarray(w1[D:]),
        "b1_c": cols(b1, 4),
        "w2": w2,
        "b2_c": np.ascontiguousarray(b2.reshape(NSLOT, 1)),
        "tau": np.ascontiguousarray(tau.reshape(1, 1)),
        "wihT": np.ascontiguousarray(Wih.T),
        "whhT": np.ascontiguousarray(Whh.T),
        "bih_c": cols(bih, 24),
        "bhh_c": cols(bhh, 24),
        "wq": wq, "wk": wk,
        "bqk_c": np.ascontiguousarray(np.concatenate([cols(bq, 2), cols(bk, 2)], axis=1)),
        "wv": wv,
        "bv_r": np.ascontiguousarray(np.broadcast_to(bv, (NSLOT, D))),
        "lng_r": np.ascontiguousarray(np.broadcast_to(ln_g, (NSLOT, D))),
        "lnb_r": np.ascontiguousarray(np.broadcast_to(ln_b, (NSLOT, D))),
        "wvp": wvp,
        "bvp_r": np.ascontiguousarray(np.broadcast_to(bvp, (NSLOT, D))),
        "wo": wo,
        "bo_r": np.ascontiguousarray(bo.reshape(1, D)),
    }

    nc = _get_nc()
    ncore = NCORE if N_CORES_OVERRIDE is None else N_CORES_OVERRIDE
    in_maps = []
    for core in range(ncore):
        m = dict(common)
        m["x"] = np.ascontiguousarray(x[core * BPC:(core + 1) * BPC])
        in_maps.append(m)
    res = run_bass_kernel_spmd(nc, in_maps, core_ids=list(range(ncore)))
    return np.concatenate([r["out"] for r in res.results], axis=0)


if __name__ == "__main__":
    build_program()
    print("built ok")


# version 14
# speedup vs baseline: 1.3309x; 1.0472x over previous
import numpy as np
import concourse.bass as bass
import concourse.mybir as mybir
from concourse.bass_utils import run_bass_kernel_spmd
from concourse.alu_op_type import AluOpType as ALU

F32 = mybir.dt.float32
AF = mybir.ActivationFunctionType
AX = mybir.AxisListType

B, T, D = 16, 4096, 1024
HID = 2048
NS = 16
NSLOT, TOPK = 64, 8
DQ, DH = 256, 512
NCORE = 8
BPC = B // NCORE
TB = 256
NBLK = T // TB
NFC = HID // 128
SEG = TB + 4


class Prog:
    ENGS = ("sync", "tensor", "vector", "scalar", "gpsimd")

    def __init__(self):
        self.ops = []
        self.deps = []
        self.writer = {}
        self.readers = {}
        self.fence = set()

    def op(self, eng, builder, reads=(), writes=(), dma=False, tag="", qsem=None):
        i = len(self.ops)
        if dma and qsem is None:
            qsem = writes[0] if writes else (tag or "dma")
        rb = [k for k in reads if k.startswith("PSB")]
        if rb:
            reads = [k for k in reads if not k.startswith("PSB")]
            writes = list(writes) + rb
        dep = set(self.fence)
        for k in reads:
            w = self.writer.get(k)
            if w is not None:
                dep.add(w)
        for k in writes:
            for r in self.readers.get(k, ()):
                dep.add(r)
            w = self.writer.get(k)
            if w is not None:
                dep.add(w)
        dep.discard(i)
        self.ops.append((eng, builder, dma, tag, qsem))
        self.deps.append(dep)
        for k in reads:
            self.readers.setdefault(k, []).append(i)
        for k in writes:
            self.writer[k] = i
            self.readers[k] = []
        return i

    def barrier(self):
        last = {}
        for i, (eng, _b, isdma, _t, qs) in enumerate(self.ops):
            last[(eng, isdma, qs)] = i
        self.fence = set(last.values())

    def emit(self, block, mksem, final_eng="sync"):
        if TRUNC_OPS is not None:
            self.ops = self.ops[:TRUNC_OPS]
            self.deps = [set(x for x in d if x < TRUNC_OPS)
                         for d in self.deps[:TRUNC_OPS]]
        n = len(self.ops)
        eng_of = [o[0] for o in self.ops]
        isdma = [o[2] for o in self.ops]
        qsem_of = [o[4] for o in self.ops]
        publish = [False] * n
        for j in range(n):
            for i in self.deps[j]:
                if (FULL_SYNC or eng_of[i] != eng_of[j] or isdma[i] or isdma[j]
                        or eng_of[i] in SYNC_ENGS):
                    publish[i] = True
        for i in range(n):
            if isdma[i]:
                publish[i] = True
        doms = []
        dom = [None] * n
        for i in range(n):
            d = ("D", qsem_of[i]) if isdma[i] else ("E", eng_of[i])
            dom[i] = d
            if d not in doms:
                doms.append(d)
        sems = {d: mksem(f"{d[0]}_{d[1]}"[:28]) for d in doms}
        pubcount = {}
        val_after = [0] * n
        for i in range(n):
            if publish[i]:
                pubcount[dom[i]] = pubcount.get(dom[i], 0) + (16 if isdma[i] else 1)
            val_after[i] = pubcount.get(dom[i], 0)
        finals = dict(pubcount)

        def section(engname):
            def go(eng):
                waited = {}
                for i in range(n):
                    if eng_of[i] != engname:
                        continue
                    need = {}
                    for j in self.deps[i]:
                        if not publish[j]:
                            continue
                        if (not FULL_SYNC and dom[j] == ("E", engname)
                                and engname not in SYNC_ENGS and not isdma[i]):
                            continue
                        v = val_after[j]
                        if need.get(dom[j], 0) < v:
                            need[dom[j]] = v
                    for d, v in sorted(need.items(), key=lambda kv: str(kv[0])):
                        if waited.get(d, 0) < v:
                            eng.wait_ge(sems[d], v)
                            waited[d] = v
                    if ((engname in NOP_ENGS or any(t in self.ops[i][3] for t in NOP_TAGS)
                         or (NOP_TAGS and "UNTAGGED" in NOP_TAGS and engname == "scalar"
                             and not self.ops[i][3])) and not isdma[i]):
                        inst = eng.nop()
                    else:
                        inst = self.ops[i][1](eng)
                    if publish[i]:
                        inst.then_inc(sems[dom[i]], 16 if isdma[i] else 1)
                if engname == final_eng:
                    for d, v in sorted(finals.items(), key=lambda kv: str(kv[0])):
                        if d[0] == "D":
                            eng.wait_ge(sems[d], v)
            return go

        block.sync(section("sync"))
        block.tensor(section("tensor"))
        block.vector(section("vector"))
        block.scalar(section("scalar"))
        block.gpsimd(section("gpsimd"))


class Pool:
    def __init__(self, ap, n):
        self.ap, self.n, self.off, self.marks = ap, n, 0, []

    def alloc(self, cols, parts=128):
        cols_al = (cols + 15) & ~15
        assert self.off + cols_al <= self.n, f"SBUF overflow {self.off}+{cols_al}>{self.n}"
        a = self.ap[0:parts, self.off:self.off + cols]
        self.off += cols_al
        return a

    def push(self):
        self.marks.append(self.off)

    def pop(self):
        self.off = self.marks.pop()

    def raw(self, p0, p1, c0, c1):
        return self.ap[p0:p1, c0:c1]


PHASES = "ABC"
LIMIT_BLOCKS = None
TRUNC_OPS = None
N_CORES_OVERRIDE = None
NOP_ENGS = ()
NOP_TAGS = ()
DBG = {}
DBG_DUMP = None
FULL_SYNC = False
RACE_DETECT = False
SYNC_ENGS = ("gpsimd", "vector", "scalar")


def build_program():
    nc = bass.Bass("TRN2", target_bir_lowering=False, detect_race_conditions=RACE_DETECT)
    dt = nc.dram_tensor

    x_in = dt("x", [BPC, T, D], F32, kind="ExternalInput").ap()
    out_hbm = dt("out", [BPC, T, D], F32, kind="ExternalOutput").ap()
    ascr = dt("ascr", [BPC, T, NSLOT], F32, kind="Internal").ap()

    w_inT = dt("w_inT", [32, 128, 8 * 128], F32, kind="ExternalInput").ap()
    conv_w_c = dt("conv_w_c", [128, NFC * 4], F32, kind="ExternalInput").ap()
    conv_b_c = dt("conv_b_c", [128, NFC], F32, kind="ExternalInput").ap()
    w_gateT = dt("w_gateT", [HID, 96], F32, kind="ExternalInput").ap()
    w_deltaT = dt("w_deltaT", [64, HID], F32, kind="ExternalInput").ap()
    dbias_c = dt("dbias_c", [128, NFC], F32, kind="ExternalInput").ap()
    alog_c = dt("alog_c", [128, NFC * NS], F32, kind="ExternalInput").ap()
    dvec_c = dt("dvec_c", [128, NFC], F32, kind="ExternalInput").ap()
    w_outT = dt("w_outT", [HID, D], F32, kind="ExternalInput").ap()

    slot_init = dt("slot_init", [NSLOT, D], F32, kind="ExternalInput").ap()
    slot_initT = dt("slot_initT", [D, NSLOT], F32, kind="ExternalInput").ap()
    w1a_d = dt("w1a", [D, DH], F32, kind="ExternalInput").ap()
    w1b_d = dt("w1b", [D, DH], F32, kind="ExternalInput").ap()
    b1_c = dt("b1_c", [128, 4], F32, kind="ExternalInput").ap()
    w2_d = dt("w2", [DH, NSLOT], F32, kind="ExternalInput").ap()
    b2_c = dt("b2_c", [NSLOT, 1], F32, kind="ExternalInput").ap()
    tau_d = dt("tau", [1, 1], F32, kind="ExternalInput").ap()
    wihT = dt("wihT", [D, 3 * D], F32, kind="ExternalInput").ap()
    whhT = dt("whhT", [D, 3 * D], F32, kind="ExternalInput").ap()
    bih_c = dt("bih_c", [128, 24], F32, kind="ExternalInput").ap()
    bhh_c = dt("bhh_c", [128, 24], F32, kind="ExternalInput").ap()
    wq_d = dt("wq", [D, DQ], F32, kind="ExternalInput").ap()
    wk_d = dt("wk", [D, DQ], F32, kind="ExternalInput").ap()
    bqk_c = dt("bqk_c", [128, 4], F32, kind="ExternalInput").ap()
    wv_d = dt("wv", [D, D], F32, kind="ExternalInput").ap()
    bv_r = dt("bv_r", [NSLOT, D], F32, kind="ExternalInput").ap()
    lng_r = dt("lng_r", [NSLOT, D], F32, kind="ExternalInput").ap()
    lnb_r = dt("lnb_r", [NSLOT, D], F32, kind="ExternalInput").ap()
    wvp_d = dt("wvp", [D, D], F32, kind="ExternalInput").ap()
    bvp_r = dt("bvp_r", [NSLOT, D], F32, kind="ExternalInput").ap()
    wo_d = dt("wo", [D, D], F32, kind="ExternalInput").ap()
    bo_r = dt("bo_r", [1, D], F32, kind="ExternalInput").ap()

    import contextlib
    st = contextlib.ExitStack()
    with st:
        POOLN = 53200
        pool_t = st.enter_context(nc.sbuf_tensor("pool", [128, POOLN], F32))
        sbp = Pool(pool_t.ap(), POOLN)
        psum_t = st.enter_context(nc.psum_tensor("ppool", [128, 4096], F32))
        ps = psum_t.ap()

        def R(c0, c1, p0=0, p1=128):
            return ps[p0:p1, c0:c1], ["PSB%d" % i for i in range(c0 // 512, (c1 + 511) // 512)]

        def mksem(name):
            return st.enter_context(nc.semaphore(name))

        blk_ctx = st.enter_context(nc.Block())
        P = Prog()

        ident = sbp.alloc(128)
        ones = sbp.alloc(128)
        konst = sbp.alloc(4)
        cwc = sbp.alloc(NFC * 4)
        cbc = sbp.alloc(NFC)
        dbc = sbp.alloc(NFC)
        acol = sbp.alloc(NFC * NS)
        dcolv = sbp.alloc(NFC)
        wgT_b = sbp.alloc(NFC * 96)
        wdT_b = sbp.alloc(HID, parts=64)
        si_sb2 = [sbp.alloc(D, parts=64) for _ in range(BPC)]
        dncol = sbp.alloc(2, parts=64)
        sv0_off = sbp.off
        sv_sb = [sbp.alloc(D, parts=64) for _ in range(BPC)]
        bo_row = sbp.raw(64, 65, sv0_off, sv0_off + D)
        DBG["sv0"] = sv0_off
        carry = [sbp.alloc(NS * NFC) for _ in range(BPC)]
        hist_t = sbp.alloc(NFC * 3)

        P.op("gpsimd", lambda e: e.memset(ident[:], 0.0), writes=["ident"])
        P.op("gpsimd", lambda e: e.affine_select(
            out=ident[:], in_=ident[:], compare_op=ALU.not_equal,
            fill=1.0, base=0, pattern=[[-1, 128]], channel_multiplier=1),
            reads=["ident"], writes=["ident"])
        P.op("gpsimd", lambda e: e.memset(ones[:], 1.0), writes=["ones"])
        P.op("gpsimd", lambda e: e.memset(konst[:, 0:1], 1e-5), writes=["konst"])
        P.op("gpsimd", lambda e: e.memset(konst[:, 1:2], -0.5), writes=["konst"])
        for nm, dst, src in [("cwc", cwc, conv_w_c), ("cbc", cbc, conv_b_c),
                             ("dbc", dbc, dbias_c), ("dcolv", dcolv, dvec_c),
                             ("bo_row", bo_row, bo_r)]:
            P.op("sync", lambda e, d=dst, s=src: e.dma_start(out=d, in_=s),
                 writes=[nm], dma=True)
        P.op("sync", lambda e: e.dma_start(out=acol, in_=alog_c), writes=["acol"], dma=True)
        P.op("scalar", lambda e: e.activation(out=acol, in_=acol, func=AF.Exp),
             reads=["acol"], writes=["acol"])
        P.op("vector", lambda e: e.tensor_scalar(out=acol, in0=acol, scalar1=-1.0,
                                                 scalar2=None, op0=ALU.mult),
             reads=["acol"], writes=["acol"])
        P.op("sync", lambda e: e.dma_start(
            out=wgT_b.rearrange("p (c n) -> p c n", n=96),
            in_=w_gateT.rearrange("(c p) n -> p c n", p=128)), writes=["wgT"], dma=True)
        P.op("sync", lambda e: e.dma_start(out=wdT_b, in_=w_deltaT), writes=["wdT"], dma=True)

        def layernorm(xt, xn_out, cols8, scratch, key, rk, scr_key, xtra=()):
            s_ = cols8
            P.op("vector", lambda e: e.reduce_sum(out=s_[:, 0:1], in_=xt, axis=AX.X),
                 reads=[rk], writes=[key + "s", "lncol" + scr_key])
            P.op("vector", lambda e: e.tensor_tensor(out=scratch, in0=xt, in1=xt,
                                                     op=ALU.mult),
                 reads=[rk], writes=["lnscr" + scr_key] + list(xtra))
            P.op("vector", lambda e: e.reduce_sum(out=s_[:, 1:2], in_=scratch, axis=AX.X),
                 reads=["lnscr" + scr_key], writes=[key + "q"])
            P.op("vector", lambda e: e.tensor_scalar(out=s_[:, 0:1], in0=s_[:, 0:1],
                                                     scalar1=1.0 / D, scalar2=None, op0=ALU.mult),
                 reads=[key + "s"], writes=[key + "m"])
            P.op("vector", lambda e: e.tensor_tensor(out=s_[:, 2:3], in0=s_[:, 0:1],
                                                     in1=s_[:, 0:1], op=ALU.mult),
                 reads=[key + "m"], writes=[key + "m2"])
            P.op("vector", lambda e: e.tensor_scalar(out=s_[:, 1:2], in0=s_[:, 1:2],
                                                     scalar1=1.0 / D, scalar2=None, op0=ALU.mult),
                 reads=[key + "q"], writes=[key + "q2"])
            P.op("vector", lambda e: e.tensor_tensor(out=s_[:, 1:2], in0=s_[:, 1:2],
                                                     in1=s_[:, 2:3], op=ALU.subtract),
                 reads=[key + "q2", key + "m2"], writes=[key + "v"])
            P.op("scalar", lambda e: e.activation(out=s_[:, 1:2], in_=s_[:, 1:2],
                                                  func=AF.Ln, bias=konst[:, 0:1]),
                 reads=[key + "v", "konst"], writes=[key + "l"])
            P.op("scalar", lambda e: e.activation(out=s_[:, 1:2], in_=s_[:, 1:2],
                                                  func=AF.Exp, scale=konst[:, 1:2]),
                 reads=[key + "l", "konst"], writes=[key + "i"])
            P.op("vector", lambda e: e.tensor_tensor(out=s_[:, 2:3], in0=s_[:, 0:1],
                                                     in1=s_[:, 1:2], op=ALU.mult),
                 reads=[key + "m", key + "i"], writes=[key + "mi"])
            P.op("vector", lambda e: e.tensor_scalar(out=xn_out, in0=xt,
                                                     scalar1=s_[:, 1:2], scalar2=s_[:, 2:3],
                                                     op0=ALU.mult, op1=ALU.subtract),
                 reads=[rk, key + "i", key + "mi"], writes=[key + "xn"] + list(xtra))

        sbp.push()
        b1tot = sbp.alloc(4)
        b2col = sbp.alloc(1, parts=64)
        invt64 = sbp.alloc(1, parts=64)
        tau_sb = sbp.alloc(4, parts=1)
        P.op("sync", lambda e: e.dma_start(out=b2col, in_=b2_c), writes=["b2col"], dma=True)
        P.op("sync", lambda e: e.dma_start(out=tau_sb[0:1, 0:1], in_=tau_d),
             writes=["tau"], dma=True)
        P.op("vector", lambda e: e.tensor_scalar(out=tau_sb[0:1, 1:2], in0=tau_sb[0:1, 0:1],
                                                 scalar1=-1.0, scalar2=None, op0=ALU.mult),
             reads=["tau"], writes=["tau"])
        P.op("vector", lambda e: e.tensor_tensor(out=tau_sb[0:1, 2:3], in0=tau_sb[0:1, 0:1],
                                                 in1=tau_sb[0:1, 1:2], op=ALU.max),
             reads=["tau"], writes=["tau"])
        P.op("vector", lambda e: e.tensor_scalar(out=tau_sb[0:1, 2:3], in0=tau_sb[0:1, 2:3],
                                                 scalar1=0.1, scalar2=None, op0=ALU.add),
             reads=["tau"], writes=["tau"])
        P.op("vector", lambda e: e.reciprocal(out=tau_sb[0:1, 3:4], in_=tau_sb[0:1, 2:3]),
             reads=["tau"], writes=["tau"])
        P.op("sync", lambda e: e.dma_start(
            out=invt64, in_=tau_sb[0:1, 3:4].unsqueeze(1).broadcast_to((1, 64, 1))),
            reads=["tau"], writes=["invt64"], dma=True)
        w1a_b = sbp.alloc(8 * DH)
        w1b_b = sbp.alloc(8 * DH)
        w2_b = sbp.alloc(4 * NSLOT)
        si_sp = sbp.alloc(D, parts=64)
        smT = sbp.alloc(8)
        bias2 = sbp.alloc(DH, parts=1)
        P.op("sync", lambda e: e.dma_start(
            out=w1a_b.rearrange("p (k f) -> p k f", f=DH),
            in_=w1a_d.rearrange("(k p) f -> p k f", p=128)), writes=["w1a"], dma=True)
        P.op("sync", lambda e: e.dma_start(
            out=w1b_b.rearrange("p (k f) -> p k f", f=DH),
            in_=w1b_d.rearrange("(k p) f -> p k f", p=128)), writes=["w1b"], dma=True)
        P.op("sync", lambda e: e.dma_start(
            out=w2_b.rearrange("p (k f) -> p k f", f=NSLOT),
            in_=w2_d.rearrange("(k p) f -> p k f", p=128)), writes=["w2"], dma=True)
        P.op("sync", lambda e: e.dma_start(out=si_sp, in_=slot_init), writes=["si_sp"], dma=True)
        P.op("sync", lambda e: e.dma_start(out=b1tot, in_=b1_c), writes=["b1tot"], dma=True)

        psm, psmK = R(3584, 3592)
        for k in range(8):
            P.op("tensor", lambda e, k=k: e.matmul(
                psm[:, k:k + 1], lhsT=si_sp[:, k * 128:(k + 1) * 128],
                rhs=ones[0:64, 0:1], start=True, stop=True, skip_group_check=True),
                reads=["si_sp", "ones"], writes=psmK)
        P.op("vector", lambda e: e.tensor_scalar(out=smT, in0=psm, scalar1=1.0 / 64,
                                                 scalar2=None, op0=ALU.mult),
             reads=psmK, writes=["smT"])
        pb2, pb2K = R(512, 1024, p1=1)
        for k in range(8):
            P.op("tensor", lambda e, k=k: e.matmul(
                pb2, lhsT=smT[:, k:k + 1], rhs=w1b_b[:, k * DH:(k + 1) * DH],
                start=(k == 0), stop=(k == 7), skip_group_check=True),
                reads=["smT", "w1b"], writes=pb2K)
        P.op("vector", lambda e: e.tensor_copy(bias2, pb2), reads=pb2K, writes=["bias2"])
        pbt, pbtK = R(3592, 3596)
        for m in range(4):
            P.op("tensor", lambda e, m=m: e.transpose(
                pbt[:, m:m + 1], bias2[0:1, m * 128:(m + 1) * 128], ident[0:1, 0:1]),
                reads=["bias2", "ident"], writes=pbtK)
        P.op("vector", lambda e: e.tensor_tensor(out=b1tot, in0=b1tot, in1=pbt, op=ALU.add),
             reads=pbtK + ["b1tot"], writes=["b1tot"])

        xA = sbp.alloc(2 * D)
        xnA = sbp.alloc(2 * D)
        lnscA = sbp.alloc(D)
        lncolA = sbp.alloc(8)
        xnTA = sbp.alloc(8 * TB)
        h_sb = sbp.alloc(4 * TB)
        lg_sb = sbp.alloc(TB, parts=64)
        lgT = sbp.alloc(2 * NSLOT)
        wkt = sbp.alloc(NSLOT)
        mcol = sbp.alloc(16)
        amt = sbp.alloc(NSLOT)
        alset = sbp.alloc(NSLOT)
        for _n, _a in [("xA", xA), ("xnA", xnA), ("lnscA", lnscA), ("lncolA", lncolA),
                       ("xnTA", xnTA), ("h_sb", h_sb), ("lg_sb", lg_sb), ("lgT", lgT),
                       ("wkt", wkt), ("mcol", mcol), ("amt", amt), ("alset", alset),
                       ("w1a", w1a_b), ("b1tot", b1tot), ("invt64", invt64), ("tau", tau_sb)]:
            DBG[_n] = _a.offset

        ps_hm = [R(0, 256), R(512, 768), R(256, 512), R(768, 1024)]
        ps_si = [R(1024, 2048, p1=64) for b in range(BPC)]
        ps_dn = [R(3072, 3080, p1=64) for b in range(BPC)]
        ps_lg = R(3584, 3840, p1=64)
        ps_tpA = [R(2048, 2176), R(2560, 2688)]

        def phaseA_block(b, blk):
            t0 = blk * TB
            key = f"A{b}_{blk}"
            P.op("sync", lambda e: e.dma_start(
                out=xA.rearrange("p (s d) -> p s d", d=D),
                in_=x_in[b, t0:t0 + TB, :].rearrange("(s p) d -> p s d", p=128)),
                writes=["xA"], dma=True, tag=key + "xload")
            P.op("scalar", lambda e: e.dma_start(
                out=out_hbm[b, t0:t0 + TB, :].rearrange("(s p) d -> p s d", p=128),
                in_=xA.rearrange("p (s d) -> p s d", d=D)),
                reads=["xA"], dma=True, tag=key + "xstore", qsem="store")
            for sub in range(2):
                xs = xA[:, sub * D:(sub + 1) * D]
                xn = xnA[:, sub * D:(sub + 1) * D]
                layernorm(xs, xn, lncolA, lnscA, key + f"ln{sub}", "xA", "A")
                for k in range(8):
                    tp, tpK = ps_tpA[k % 2]
                    P.op("tensor", lambda e, k=k, xn=xn, tp=tp: e.transpose(
                        tp[:, 0:128], xn[:, k * 128:(k + 1) * 128], ident[:]),
                        reads=[key + f"ln{sub}xn", "ident"], writes=tpK)
                    P.op("vector", lambda e, k=k, sub=sub, tp=tp: e.tensor_copy(
                        xnTA[:, k * TB + sub * 128: k * TB + sub * 128 + 128], tp[:, 0:128]),
                        reads=tpK, writes=[f"xnTA_{k}_{sub}"])
            xnT_keys = [f"xnTA_{k}_{s}" for k in range(8) for s in range(2)]
            for m in range(4):
                pm, pmK = ps_hm[m]
                for k in range(8):
                    P.op("tensor", lambda e, m=m, k=k, pm=pm: e.matmul(
                        pm, lhsT=w1a_b[:, k * DH + m * 128: k * DH + m * 128 + 128],
                        rhs=xnTA[:, k * TB:(k + 1) * TB],
                        start=(k == 0), stop=(k == 7), skip_group_check=True),
                        reads=["w1a"] + xnT_keys, writes=pmK, tag=key + "w1")
                P.op("scalar", lambda e, m=m, pm=pm: e.activation(
                    out=h_sb[:, m * TB:(m + 1) * TB], in_=pm, func=AF.Gelu,
                    bias=b1tot[:, m:m + 1]),
                    reads=pmK + ["b1tot"], writes=[f"h{m}"], tag=key + "gelu")
            plg, plgK = ps_lg
            for k in range(4):
                P.op("tensor", lambda e, k=k, plg=plg: e.matmul(
                    plg, lhsT=w2_b[:, k * NSLOT:(k + 1) * NSLOT],
                    rhs=h_sb[:, k * TB:(k + 1) * TB],
                    start=(k == 0), stop=(k == 3), skip_group_check=True),
                    reads=["w2", f"h{k}"], writes=plgK, tag=key + "w2")
            P.op("vector", lambda e, plg=plg: e.tensor_scalar(
                out=lg_sb, in0=plg, scalar1=b2col, scalar2=invt64,
                op0=ALU.add, op1=ALU.mult),
                reads=plgK + ["b2col", "invt64"], writes=["lg"], tag=key + "lgevac")
            for sub in range(2):
                tp, tpK = ps_tpA[sub]
                P.op("tensor", lambda e, sub=sub, tp=tp: e.transpose(
                    tp[:, 0:NSLOT], lg_sb[:, sub * 128:(sub + 1) * 128], ident[0:64, 0:64]),
                    reads=["lg", "ident"], writes=tpK)
                P.op("vector", lambda e, sub=sub, tp=tp: e.tensor_copy(
                    lgT[:, sub * NSLOT:(sub + 1) * NSLOT], tp[:, 0:NSLOT]),
                    reads=tpK, writes=[f"lgT{sub}"])
            for sub in range(2):
                lgs = lgT[:, sub * NSLOT:(sub + 1) * NSLOT]
                P.op("vector", lambda e, lgs=lgs: e.tensor_copy(wkt, lgs),
                     reads=[f"lgT{sub}"], writes=["wkt"])
                for it in range(TOPK):
                    P.op("vector", lambda e, it=it: e.reduce_max(
                        out=mcol[:, it:it + 1], in_=wkt, axis=AX.X, negate=True),
                        reads=["wkt"], writes=[f"mx{it}", "mcol"])
                    if it < TOPK - 1:
                        P.op("vector", lambda e, it=it: e.tensor_scalar(
                            out=amt, in0=wkt, scalar1=mcol[:, it:it + 1],
                            scalar2=0.0, op0=ALU.add, op1=ALU.is_ge),
                            reads=["wkt", f"mx{it}"], writes=["amt"])
                        P.op("vector", lambda e: e.scalar_tensor_tensor(
                            out=wkt, in0=amt, scalar=-1e38, in1=wkt,
                            op0=ALU.mult, op1=ALU.add),
                            reads=["amt", "wkt"], writes=["wkt"])
                P.op("scalar", lambda e, lgs=lgs: e.activation(
                    out=amt, in_=lgs, func=AF.Exp, bias=mcol[:, 0:1]),
                    reads=[f"lgT{sub}", "mx0"], writes=["amt"], tag=key + "aexp")
                P.op("vector", lambda e, lgs=lgs: e.tensor_scalar(
                    out=wkt, in0=lgs, scalar1=mcol[:, TOPK - 1:TOPK],
                    scalar2=0.0, op0=ALU.add, op1=ALU.is_ge),
                    reads=[f"lgT{sub}", f"mx{TOPK - 1}"], writes=["wkt"])
                P.op("vector", lambda e: e.tensor_tensor(
                    out=alset, in0=amt, in1=wkt, op=ALU.mult),
                    reads=["amt", "wkt"], writes=["alset"])
                P.op("vector", lambda e: e.reduce_sum(out=mcol[:, 8:9], in_=alset, axis=AX.X),
                     reads=["alset"], writes=["aZ", "mcol"])
                P.op("vector", lambda e: e.reciprocal(out=mcol[:, 9:10], in_=mcol[:, 8:9]),
                     reads=["aZ"], writes=["aZr", "mcol"])
                P.op("vector", lambda e: e.tensor_scalar(
                    out=alset, in0=alset, scalar1=mcol[:, 9:10], scalar2=None, op0=ALU.mult),
                    reads=["alset", "aZr"], writes=["alset"], tag=key + "anorm")
                P.op("scalar", lambda e, b=b, t0=t0, sub=sub: e.dma_start(
                    out=ascr[b, t0 + sub * 128: t0 + sub * 128 + 128, :], in_=alset),
                    reads=["alset"], dma=True, tag=key + "astore", qsem="astore")
                first = (blk == 0 and sub == 0)
                last = (blk == (LIMIT_BLOCKS or NBLK) - 1 and sub == 1)
                xn = xnA[:, sub * D:(sub + 1) * D]
                psi, psiK = ps_si[b]
                for half in range(2):
                    P.op("tensor", lambda e, xn=xn, half=half, first=first, last=last, psi=psi: e.matmul(
                        psi[:, half * 512:(half + 1) * 512], lhsT=alset,
                        rhs=xn[:, half * 512:(half + 1) * 512],
                        start=first, stop=last, skip_group_check=True),
                        reads=["alset", key + f"ln{sub}xn"], writes=psiK, tag=key + "simm")
                pdn, pdnK = ps_dn[b]
                P.op("tensor", lambda e, first=first, last=last, pdn=pdn: e.matmul(
                    pdn, lhsT=alset, rhs=ones[:, 0:8],
                    start=first, stop=last, skip_group_check=True),
                    reads=["alset", "ones"], writes=pdnK, tag=key + "dnmm")

        nblk_a = LIMIT_BLOCKS or NBLK
        for b in range(BPC):
            for blk in range(nblk_a):
                phaseA_block(b, blk)
            P.op("vector", lambda e, b=b: e.tensor_copy(dncol[:, b:b + 1], ps_dn[b][0][:, 0:1]),
                 reads=ps_dn[b][1], writes=[f"dncol{b}", "dncol"])
            P.op("vector", lambda e, b=b: e.tensor_copy(si_sb2[b], ps_si[b][0]),
                 reads=ps_si[b][1], writes=[f"siraw{b}"])
        sbp.pop()
        P.barrier()
        if "B" not in PHASES:
            if DBG_DUMP is not None:
                c0, c1 = DBG_DUMP
                dbg_t = dt("dbg", [128, c1 - c0], F32, kind="ExternalOutput").ap()
                P.barrier()
                P.op("sync", lambda e: e.dma_start(out=dbg_t, in_=sbp.raw(0, 128, c0, c1)),
                     dma=True, qsem="dbgdump")
            P.emit(blk_ctx, mksem, final_eng="sync")
            return nc

        sbp.push()
        s0T_b = sbp.alloc(8 * NSLOT)
        bihc = sbp.alloc(24)
        bhhc = sbp.alloc(24)
        bqkc = sbp.alloc(4)
        wm_b = [sbp.alloc(8 * 128) for _ in range(2)]
        wq_b = sbp.alloc(8 * DQ)
        wk_b = sbp.alloc(8 * DQ)
        wv_b = sbp.alloc(8 * D)
        wvp_b = sbp.alloc(8 * D)
        gh_sb = sbp.alloc(24 * NSLOT)
        gi_sb = sbp.alloc(24 * NSLOT)
        r_sb = sbp.alloc(8 * NSLOT)
        z_sb = sbp.alloc(8 * NSLOT)
        n_sb = sbp.alloc(8 * NSLOT)
        snew = sbp.alloc(8 * NSLOT)
        snewT = sbp.alloc(D, parts=64)
        q_sb = sbp.alloc(2 * NSLOT)
        k_sb = sbp.alloc(2 * NSLOT)
        vT_sb = sbp.alloc(D, parts=64)
        sc_sb = sbp.alloc(NSLOT, parts=64)
        at_sb = sbp.alloc(NSLOT, parts=64)
        y_sb = sbp.alloc(D, parts=64)
        yn_sb = sbp.alloc(D, parts=64)
        ynT_sb = sbp.alloc(NSLOT)
        siT = sbp.alloc(8 * NSLOT)
        colB = sbp.alloc(8, parts=64)
        scrB = sbp.alloc(D, parts=64)
        ggb = sbp.alloc(D, parts=64)
        for _n, _a in [("s0T", s0T_b), ("gh", gh_sb), ("gi", gi_sb), ("r", r_sb),
                       ("z", z_sb), ("n", n_sb), ("snew", snew), ("snewT", snewT),
                       ("q", q_sb), ("k", k_sb), ("vT", vT_sb), ("sc", sc_sb),
                       ("at", at_sb), ("y", y_sb), ("yn", yn_sb), ("siT", siT),
                       ("scrB", scrB), ("si0", si_sb2[0]), ("si1", si_sb2[1])]:
            DBG[_n] = _a.offset

        P.op("sync", lambda e: e.dma_start(
            out=s0T_b.rearrange("p (k f) -> p k f", f=NSLOT),
            in_=slot_initT.rearrange("(k p) f -> p k f", p=128)), writes=["s0T"], dma=True)
        P.op("sync", lambda e: e.dma_start(out=bihc, in_=bih_c), writes=["bihc"], dma=True)
        P.op("sync", lambda e: e.dma_start(out=bhhc, in_=bhh_c), writes=["bhhc"], dma=True)
        P.op("sync", lambda e: e.dma_start(out=bqkc, in_=bqk_c), writes=["bqkc"], dma=True)
        P.op("sync", lambda e: e.dma_start(
            out=wq_b.rearrange("p (k f) -> p k f", f=DQ),
            in_=wq_d.rearrange("(k p) f -> p k f", p=128)), writes=["wqb"], dma=True)
        P.op("sync", lambda e: e.dma_start(
            out=wk_b.rearrange("p (k f) -> p k f", f=DQ),
            in_=wk_d.rearrange("(k p) f -> p k f", p=128)), writes=["wkb"], dma=True)
        P.op("sync", lambda e: e.dma_start(
            out=wv_b.rearrange("p (k f) -> p k f", f=D),
            in_=wv_d.rearrange("(k p) f -> p k f", p=128)), writes=["wvb"], dma=True)
        P.op("sync", lambda e: e.dma_start(
            out=wvp_b.rearrange("p (k f) -> p k f", f=D),
            in_=wvp_d.rearrange("(k p) f -> p k f", p=128)), writes=["wvpb"], dma=True)

        for b in range(BPC):
            P.op("vector", lambda e, b=b: e.tensor_scalar(
                out=colB[:, 0:1], in0=dncol[:, b:b + 1], scalar1=1e-8, scalar2=None, op0=ALU.add),
                reads=[f"dncol{b}"], writes=[f"B{b}_dn1", "colB"])
            P.op("vector", lambda e, b=b: e.reciprocal(out=colB[:, 1:2], in_=colB[:, 0:1]),
                 reads=[f"B{b}_dn1"], writes=[f"B{b}_dnr", "colB"])
            P.op("vector", lambda e, b=b: e.tensor_scalar(
                out=si_sb2[b], in0=si_sb2[b], scalar1=colB[:, 1:2], scalar2=None, op0=ALU.mult),
                reads=[f"siraw{b}", f"B{b}_dnr"], writes=[f"B{b}_si"])

        pgh, pghK = R(0, 64)
        pgi, pgiK = R(64, 128)
        pq, pqK = R(128, 192)
        ptB, ptBK = R(256, 384)
        psc, pscK = R(384, 448, p1=64)
        pvT, pvTK = R(1024, 2048, p1=64)
        pcm, pcmK = R(2048, 3072, p1=64)

        for m in range(24):
            wb = wm_b[m % 2]
            P.op("sync", lambda e, m=m, wb=wb: e.dma_start(
                out=wb.rearrange("p (k f) -> p k f", f=128),
                in_=whhT[:, m * 128:(m + 1) * 128].rearrange("(k p) f -> p k f", p=128)),
                writes=[f"wm{m % 2}"], dma=True)
            for k in range(8):
                P.op("tensor", lambda e, k=k, wb=wb: e.matmul(
                    pgh, lhsT=wb[:, k * 128:(k + 1) * 128],
                    rhs=s0T_b[:, k * NSLOT:(k + 1) * NSLOT],
                    start=(k == 0), stop=(k == 7), skip_group_check=True),
                    reads=[f"wm{m % 2}", "s0T"], writes=pghK)
            P.op("vector", lambda e, m=m: e.tensor_copy(
                gh_sb[:, m * NSLOT:(m + 1) * NSLOT], pgh),
                reads=pghK, writes=[f"gh{m}"])

        for b in range(BPC):
            kb = f"B{b}_"
            for k in range(8):
                P.op("tensor", lambda e, k=k, b=b: e.transpose(
                    ptB[:, 0:64], si_sb2[b][:, k * 128:(k + 1) * 128], ident[0:64, 0:64]),
                    reads=[kb + "si", "ident"], writes=ptBK)
                P.op("vector", lambda e, k=k: e.tensor_copy(
                    siT[:, k * NSLOT:(k + 1) * NSLOT], ptB[:, 0:64]),
                    reads=ptBK, writes=[kb + f"siT{k}"])
            siT_keys = [kb + f"siT{k}" for k in range(8)]
            for m in range(24):
                wb = wm_b[m % 2]
                P.op("sync", lambda e, m=m, wb=wb: e.dma_start(
                    out=wb.rearrange("p (k f) -> p k f", f=128),
                    in_=wihT[:, m * 128:(m + 1) * 128].rearrange("(k p) f -> p k f", p=128)),
                    writes=[f"wm{m % 2}"], dma=True)
                for k in range(8):
                    P.op("tensor", lambda e, k=k, wb=wb: e.matmul(
                        pgi, lhsT=wb[:, k * 128:(k + 1) * 128],
                        rhs=siT[:, k * NSLOT:(k + 1) * NSLOT],
                        start=(k == 0), stop=(k == 7), skip_group_check=True),
                        reads=[f"wm{m % 2}"] + siT_keys, writes=pgiK)
                P.op("vector", lambda e, m=m: e.tensor_copy(
                    gi_sb[:, m * NSLOT:(m + 1) * NSLOT], pgi),
                    reads=pgiK, writes=[kb + f"gi{m}"])
            for out_sb, base in ((r_sb, 0), (z_sb, 8)):
                for m8 in range(8):
                    m = base + m8
                    sl = slice(m8 * NSLOT, (m8 + 1) * NSLOT)
                    P.op("vector", lambda e, m=m, sl=sl, o=out_sb: e.scalar_tensor_tensor(
                        out=o[:, sl], in0=gi_sb[:, m * NSLOT:(m + 1) * NSLOT],
                        scalar=bihc[:, m:m + 1], in1=gh_sb[:, m * NSLOT:(m + 1) * NSLOT],
                        op0=ALU.add, op1=ALU.add),
                        reads=[kb + f"gi{m}", f"gh{m}", "bihc"], writes=[kb + f"g1_{m}"])
                    P.op("vector", lambda e, m=m, sl=sl, o=out_sb: e.tensor_scalar(
                        out=o[:, sl], in0=o[:, sl], scalar1=bhhc[:, m:m + 1],
                        scalar2=None, op0=ALU.add),
                        reads=[kb + f"g1_{m}", "bhhc"], writes=[kb + f"g2_{m}"])
                    P.op("scalar", lambda e, sl=sl, o=out_sb: e.activation(
                        out=o[:, sl], in_=o[:, sl], func=AF.Sigmoid),
                        reads=[kb + f"g2_{m}"], writes=[kb + f"gact{m}"])
            for m8 in range(8):
                m = 16 + m8
                sl = slice(m8 * NSLOT, (m8 + 1) * NSLOT)
                P.op("vector", lambda e, m=m, sl=sl: e.tensor_scalar(
                    out=n_sb[:, sl], in0=gh_sb[:, m * NSLOT:(m + 1) * NSLOT],
                    scalar1=bhhc[:, m:m + 1], scalar2=None, op0=ALU.add),
                    reads=[f"gh{m}", "bhhc"], writes=[kb + f"nh{m8}"])
                P.op("vector", lambda e, m8=m8, sl=sl: e.tensor_tensor(
                    out=n_sb[:, sl], in0=r_sb[:, sl], in1=n_sb[:, sl], op=ALU.mult),
                    reads=[kb + f"gact{m8}", kb + f"nh{m8}"], writes=[kb + f"nr{m8}"])
                P.op("vector", lambda e, m=m, sl=sl: e.scalar_tensor_tensor(
                    out=n_sb[:, sl], in0=gi_sb[:, m * NSLOT:(m + 1) * NSLOT],
                    scalar=bihc[:, m:m + 1], in1=n_sb[:, sl], op0=ALU.add, op1=ALU.add),
                    reads=[kb + f"gi{m}", kb + f"nr{m8}", "bihc"], writes=[kb + f"np{m8}"])
                P.op("scalar", lambda e, sl=sl: e.activation(
                    out=n_sb[:, sl], in_=n_sb[:, sl], func=AF.Tanh),
                    reads=[kb + f"np{m8}"], writes=[kb + f"n{m8}"])
                P.op("vector", lambda e, m8=m8, sl=sl: e.tensor_tensor(
                    out=snew[:, sl], in0=s0T_b[:, sl], in1=n_sb[:, sl], op=ALU.subtract),
                    reads=["s0T", kb + f"n{m8}"], writes=[kb + f"sd{m8}"])
                P.op("vector", lambda e, m8=m8, sl=sl: e.tensor_tensor(
                    out=snew[:, sl], in0=z_sb[:, sl], in1=snew[:, sl], op=ALU.mult),
                    reads=[kb + f"gact{8 + m8}", kb + f"sd{m8}"], writes=[kb + f"zs{m8}"])
                P.op("vector", lambda e, m8=m8, sl=sl: e.tensor_tensor(
                    out=snew[:, sl], in0=n_sb[:, sl], in1=snew[:, sl], op=ALU.add),
                    reads=[kb + f"n{m8}", kb + f"zs{m8}"], writes=[kb + f"sn{m8}"])
            snew_keys = [kb + f"sn{m8}" for m8 in range(8)]
            for k in range(8):
                P.op("tensor", lambda e, k=k: e.transpose(
                    ptB[0:64, 0:128], snew[:, k * NSLOT:(k + 1) * NSLOT], ident[:]),
                    reads=snew_keys + ["ident"], writes=ptBK)
                P.op("vector", lambda e, k=k: e.tensor_copy(
                    snewT[:, k * 128:(k + 1) * 128], ptB[0:64, 0:128]),
                    reads=ptBK, writes=[kb + f"snT{k}"])
            snewT_keys = [kb + f"snT{k}" for k in range(8)]
            for wb_, bi, dst, wkey, rkey in ((wq_b, 0, q_sb, "q", "wqb"),
                                             (wk_b, 2, k_sb, "k", "wkb")):
                for c in range(2):
                    for k in range(8):
                        P.op("tensor", lambda e, c=c, k=k, wb_=wb_: e.matmul(
                            pq, lhsT=wb_[:, k * DQ + c * 128: k * DQ + c * 128 + 128],
                            rhs=snew[:, k * NSLOT:(k + 1) * NSLOT],
                            start=(k == 0), stop=(k == 7), skip_group_check=True),
                            reads=[rkey] + snew_keys, writes=pqK)
                    P.op("vector", lambda e, c=c, dst=dst, bi=bi: e.tensor_scalar(
                        out=dst[:, c * NSLOT:(c + 1) * NSLOT], in0=pq,
                        scalar1=bqkc[:, bi + c:bi + c + 1], scalar2=None, op0=ALU.add),
                        reads=pqK + ["bqkc"], writes=[kb + wkey + str(c)])
            for nh in range(2):
                for k in range(8):
                    P.op("tensor", lambda e, nh=nh, k=k: e.matmul(
                        pvT[:, nh * 512:(nh + 1) * 512],
                        lhsT=snew[:, k * NSLOT:(k + 1) * NSLOT],
                        rhs=wv_b[:, k * D + nh * 512: k * D + nh * 512 + 512],
                        start=(k == 0), stop=(k == 7), skip_group_check=True),
                        reads=["wvb"] + snew_keys, writes=pvTK)
            P.op("sync", lambda e: e.dma_start(out=ggb, in_=bv_r), writes=["ggb"], dma=True)
            P.op("vector", lambda e: e.tensor_tensor(out=vT_sb, in0=pvT, in1=ggb, op=ALU.add),
                 reads=pvTK + ["ggb"], writes=[kb + "vT"])
            for c in range(2):
                P.op("tensor", lambda e, c=c: e.matmul(
                    psc, lhsT=q_sb[:, c * NSLOT:(c + 1) * NSLOT],
                    rhs=k_sb[:, c * NSLOT:(c + 1) * NSLOT],
                    start=(c == 0), stop=(c == 1), skip_group_check=True),
                    reads=[kb + "q" + str(c), kb + "k" + str(c)], writes=pscK)
            P.op("vector", lambda e: e.tensor_scalar(
                out=sc_sb, in0=psc, scalar1=DQ ** -0.5, scalar2=None, op0=ALU.mult),
                reads=pscK, writes=[kb + "sc"])
            P.op("vector", lambda e: e.reduce_max(out=colB[:, 2:3], in_=sc_sb,
                                                  axis=AX.X, negate=True),
                 reads=[kb + "sc"], writes=[kb + "scm", "colB"])
            P.op("scalar", lambda e: e.activation(out=sc_sb, in_=sc_sb, func=AF.Exp,
                                                  bias=colB[:, 2:3]),
                 reads=[kb + "sc", kb + "scm"], writes=[kb + "sce"])
            P.op("vector", lambda e: e.reduce_sum(out=colB[:, 3:4], in_=sc_sb, axis=AX.X),
                 reads=[kb + "sce"], writes=[kb + "scs", "colB"])
            P.op("vector", lambda e: e.reciprocal(out=colB[:, 4:5], in_=colB[:, 3:4]),
                 reads=[kb + "scs"], writes=[kb + "scz", "colB"])
            P.op("vector", lambda e: e.tensor_scalar(
                out=sc_sb, in0=sc_sb, scalar1=colB[:, 4:5], scalar2=None, op0=ALU.mult),
                reads=[kb + "sce", kb + "scz"], writes=[kb + "attn"])
            P.op("tensor", lambda e: e.transpose(
                ptB[0:64, 0:64], sc_sb, ident[0:64, 0:64]),
                reads=[kb + "attn", "ident"], writes=ptBK)
            P.op("vector", lambda e: e.tensor_copy(at_sb, ptB[0:64, 0:64]),
                 reads=ptBK, writes=[kb + "attnT"])
            for nh in range(2):
                P.op("tensor", lambda e, nh=nh: e.matmul(
                    pcm[:, nh * 512:(nh + 1) * 512], lhsT=at_sb,
                    rhs=vT_sb[:, nh * 512:(nh + 1) * 512],
                    start=True, stop=True, skip_group_check=True),
                    reads=[kb + "attnT", kb + "vT"], writes=pcmK)
            P.op("vector", lambda e: e.tensor_tensor(out=y_sb, in0=pcm, in1=snewT, op=ALU.add),
                 reads=pcmK + snewT_keys, writes=[kb + "y"])
            P.op("vector", lambda e: e.reduce_sum(out=colB[:, 0:1], in_=y_sb, axis=AX.X),
                 reads=[kb + "y"], writes=[kb + "ys", "colB"])
            P.op("vector", lambda e: e.tensor_tensor(out=scrB, in0=y_sb, in1=y_sb,
                                                     op=ALU.mult),
                 reads=[kb + "y"], writes=["scrB"])
            P.op("vector", lambda e: e.reduce_sum(out=colB[:, 1:2], in_=scrB, axis=AX.X),
                 reads=["scrB"], writes=[kb + "yq", "colB"])
            P.op("vector", lambda e: e.tensor_scalar(out=colB[:, 0:1], in0=colB[:, 0:1],
                                                     scalar1=1.0 / D, scalar2=None, op0=ALU.mult),
                 reads=[kb + "ys"], writes=[kb + "ym", "colB"])
            P.op("vector", lambda e: e.tensor_tensor(out=colB[:, 2:3], in0=colB[:, 0:1],
                                                     in1=colB[:, 0:1], op=ALU.mult),
                 reads=[kb + "ym"], writes=[kb + "ym2", "colB"])
            P.op("vector", lambda e: e.tensor_scalar(out=colB[:, 1:2], in0=colB[:, 1:2],
                                                     scalar1=1.0 / D, scalar2=None, op0=ALU.mult),
                 reads=[kb + "yq"], writes=[kb + "yq2", "colB"])
            P.op("vector", lambda e: e.tensor_tensor(out=colB[:, 1:2], in0=colB[:, 1:2],
                                                     in1=colB[:, 2:3], op=ALU.subtract),
                 reads=[kb + "yq2", kb + "ym2"], writes=[kb + "yv", "colB"])
            P.op("scalar", lambda e: e.activation(out=colB[:, 1:2], in_=colB[:, 1:2],
                                                  func=AF.Ln, bias=konst[0:64, 0:1]),
                 reads=[kb + "yv", "konst"], writes=[kb + "yl", "colB"])
            P.op("scalar", lambda e: e.activation(out=colB[:, 1:2], in_=colB[:, 1:2],
                                                  func=AF.Exp, scale=konst[0:64, 1:2]),
                 reads=[kb + "yl", "konst"], writes=[kb + "yi", "colB"])
            P.op("vector", lambda e: e.tensor_tensor(out=colB[:, 2:3], in0=colB[:, 0:1],
                                                     in1=colB[:, 1:2], op=ALU.mult),
                 reads=[kb + "ym", kb + "yi"], writes=[kb + "ymi", "colB"])
            P.op("vector", lambda e: e.tensor_scalar(
                out=yn_sb, in0=y_sb, scalar1=colB[:, 1:2], scalar2=colB[:, 2:3],
                op0=ALU.mult, op1=ALU.subtract),
                reads=[kb + "y", kb + "yi", kb + "ymi"], writes=[kb + "yn0"])
            P.op("sync", lambda e: e.dma_start(out=ggb, in_=lng_r), writes=["ggb"], dma=True)
            P.op("vector", lambda e: e.tensor_tensor(out=yn_sb, in0=yn_sb, in1=ggb, op=ALU.mult),
                 reads=[kb + "yn0", "ggb"], writes=[kb + "yn1"])
            P.op("sync", lambda e: e.dma_start(out=ggb, in_=lnb_r), writes=["ggb"], dma=True)
            P.op("vector", lambda e: e.tensor_tensor(out=yn_sb, in0=yn_sb, in1=ggb, op=ALU.add),
                 reads=[kb + "yn1", "ggb"], writes=[kb + "yn"])
            for k in range(8):
                P.op("tensor", lambda e, k=k: e.transpose(
                    ptB[:, 0:64], yn_sb[:, k * 128:(k + 1) * 128], ident[0:64, 0:64]),
                    reads=[kb + "yn", "ident"], writes=ptBK)
                P.op("vector", lambda e: e.tensor_copy(ynT_sb, ptB[:, 0:64]),
                     reads=ptBK, writes=[kb + "ynT"])
                for nh in range(2):
                    P.op("tensor", lambda e, k=k, nh=nh: e.matmul(
                        pvT[:, nh * 512:(nh + 1) * 512], lhsT=ynT_sb,
                        rhs=wvp_b[:, k * D + nh * 512: k * D + nh * 512 + 512],
                        start=(k == 0), stop=(k == 7), skip_group_check=True),
                        reads=[kb + "ynT", "wvpb"], writes=pvTK)
            P.op("sync", lambda e: e.dma_start(out=ggb, in_=bvp_r), writes=["ggb"], dma=True)
            P.op("vector", lambda e, b=b: e.tensor_tensor(
                out=sv_sb[b], in0=pvT, in1=ggb, op=ALU.add),
                reads=pvTK + ["ggb"], writes=[f"sv{b}"])
        sbp.pop()
        P.barrier()
        if "C" not in PHASES:
            if DBG_DUMP is not None:
                c0, c1 = DBG_DUMP
                dbg_t = dt("dbg", [128, c1 - c0], F32, kind="ExternalOutput").ap()
                P.barrier()
                P.op("sync", lambda e: e.dma_start(out=dbg_t, in_=sbp.raw(0, 128, c0, c1)),
                     dma=True, qsem="dbgdump")
            P.emit(blk_ctx, mksem, final_eng="sync")
            return nc

        sbp.push()
        xC = sbp.alloc(2 * D)
        lncolC = sbp.alloc(8)
        xnTC = sbp.alloc(8 * TB)
        wing = [sbp.alloc(8 * 128) for _ in range(2)]
        u_raw = sbp.alloc(NFC * SEG)
        uconv = sbp.alloc(NFC * TB)
        go_sl = [sbp.alloc(NFC * TB) for _ in range(2)]
        delta = sbp.alloc(NFC * TB)
        du_sl = sbp.alloc(NFC * TB)
        y_sl = sbp.alloc(NFC * TB)
        g_sb2 = sbp.alloc(TB, parts=96)
        _sio = si_sb2[0].offset
        _sio1 = si_sb2[1].offset
        brep = [sbp.raw(0, 128, _sio + i * TB, _sio + (i + 1) * TB) for i in range(2)]
        crep = [sbp.raw(0, 128, _sio + (2 + i) * TB, _sio + (3 + i) * TB) for i in range(2)]
        fbuf = [sbp.raw(0, 128, _sio1 + i * TB, _sio1 + (i + 1) * TB) for i in range(4)]
        dtmp = fbuf[0]
        gslab = sbp.alloc(NFC * TB)
        hslab = sbp.alloc(8 * TB)
        mslab = sbp.alloc(8 * TB)
        fin_d = mslab
        woutb = None
        wob = None
        aC = sbp.alloc(2 * NSLOT)
        f0T_sb = delta[0:128, 0:2048]
        aT_sb = delta[0:64, 2048:2048 + 2 * 128]

        ps_uo = [R(0, 256), R(512, 768), R(256, 512), R(768, 1024)]
        ps_g = R(1024, 1280, p1=96)
        ps_d = [R(1280, 1536), R(1536, 1792), ps_uo[0], ps_uo[1]]
        ps_f0 = [R(1024, 1280), R(1536, 1792), R(1280, 1536), R(1792, 2048)]
        ps_tpx = [R(0, 128), R(512, 640)]
        ps_tpa = [R(1280, 1408, p1=64), R(1792, 1920, p1=64)]
        ps_fin = [[R(2048 + (sub * 2 + nh) * 512, 2048 + (sub * 2 + nh + 1) * 512)
                   for nh in range(2)] for sub in range(2)]

        def seg(slab, fc, w=TB):
            return slab[:, fc * w:(fc + 1) * w]

        u3d = u_raw.rearrange("p (c t) -> p c t", t=SEG)
        wslots = [(wing[0], "wing0"), (wing[1], "wing1"),
                  (xC[:, 0:D], "xCa"), (xC[:, D:2 * D], "xCb")]

        def stage_front(b, blk):
            t0 = blk * TB
            key = f"C{b}_{blk}"
            P.op("sync", lambda e: e.dma_start(
                out=xC.rearrange("p (s d) -> p s d", d=D),
                in_=x_in[b, t0:t0 + TB, :].rearrange("(s p) d -> p s d", p=128)),
                writes=["xCa", "xCb"], dma=True, tag=key + "xload")
            lnscr = u_raw[:, 2048:3072]
            for sub in range(2):
                xs = xC[:, sub * D:(sub + 1) * D]
                xn = u_raw[:, sub * D:(sub + 1) * D]
                layernorm(xs, xn, lncolC, lnscr, key + f"ln{sub}", "xCa" if sub == 0 else "xCb", "C", xtra=("u_raw",))
                for k in range(8):
                    tp, tpK = ps_tpx[k % 2]
                    P.op("tensor", lambda e, k=k, xn=xn, tp=tp: e.transpose(
                        tp, xn[:, k * 128:(k + 1) * 128], ident[:]),
                        reads=[key + f"ln{sub}xn", "u_raw", "ident"], writes=tpK)
                    P.op("vector", lambda e, k=k, sub=sub, tp=tp: e.tensor_copy(
                        xnTC[:, k * TB + sub * 128: k * TB + sub * 128 + 128], tp),
                        reads=tpK, writes=[f"CxnT_{k}_{sub}"])
            if blk == 0 and b == 0:
                P.op("vector", lambda e: e.memset(u3d[:, :, 0:3], 0.0),
                     reads=["u_raw"], writes=["u_raw"], tag=key + "hist0")
            elif blk == 0:
                P.op("vector", lambda e: e.memset(u3d[:, :, 0:3], 0.0),
                     reads=["u_raw"], writes=["u_raw"], tag=key + "hist0")
            else:
                P.op("vector", lambda e: e.tensor_copy(
                    u3d[:, :, 0:3], hist_t.rearrange("p (c t) -> p c t", t=3)),
                    reads=["u_raw", "hist_t"], writes=["u_raw"], tag=key + "hist")

        def stage_mid_dma(b, blk, grp):
            key = f"C{b}_{blk}"
            wb, wbK = wslots[grp % 4]
            P.op("sync", lambda e, grp=grp, wb=wb: e.dma_start(
                out=wb, in_=w_inT[grp]),
                writes=[wbK], dma=True, tag=key + "windma")

        def stage_mid_mm(b, blk, par, grp):
            key = f"C{b}_{blk}"
            xnT_keys = [f"CxnT_{k}_{s}" for k in range(8) for s in range(2)]
            wb, wbK = wslots[grp % 4]
            pm, pmK = ps_uo[grp % 4]
            for k in range(8):
                P.op("tensor", lambda e, k=k, wb=wb, pm=pm: e.matmul(
                    pm, lhsT=wb[:, k * 128:(k + 1) * 128],
                    rhs=xnTC[:, k * TB:(k + 1) * TB],
                    start=(k == 0), stop=(k == 7), skip_group_check=True),
                    reads=[wbK] + xnT_keys, writes=pmK,
                    tag=key + "uomm")
            fc = grp % 16
            if grp < 16:
                P.op("vector", lambda e, fc=fc, pm=pm: e.tensor_copy(
                    u3d[:, fc, 3:3 + TB], pm),
                    reads=pmK, writes=["u_raw"], tag=key + "uevac")
            else:
                P.op("scalar", lambda e, fc=fc, pm=pm, par=par: e.activation(
                    out=seg(go_sl[par], fc), in_=pm, func=AF.Copy),
                    reads=pmK, writes=[f"go_{par}_{fc}"], tag=key + "goevac")

        def stage_mid_grp(b, blk, par, grps):
            grps = list(grps)
            for j, grp in enumerate(grps):
                if j == 0:
                    stage_mid_dma(b, blk, grp)
                    if j + 1 < len(grps):
                        stage_mid_dma(b, blk, grps[j + 1])
                if j + 2 < len(grps):
                    stage_mid_dma(b, blk, grps[j + 2])
                stage_mid_mm(b, blk, par, grp)

        def stage_mid_post(b, blk):
            key = f"C{b}_{blk}"
            for q in range(NFC // 2):
                for j in range(4):
                    for fc in (2 * q, 2 * q + 1):
                        base = fc * SEG
                        srcv = u_raw[:, base + j:base + j + TB]
                        if j == 0:
                            P.op("vector", lambda e, srcv=srcv, fc=fc: e.tensor_scalar(
                                out=seg(uconv, fc), in0=srcv, scalar1=cwc[:, fc * 4:fc * 4 + 1],
                                scalar2=None, op0=ALU.mult),
                                reads=["u_raw", "cwc"], writes=[f"uc_{fc}"], tag=key + "conv")
                        else:
                            P.op("vector", lambda e, srcv=srcv, fc=fc, j=j: e.scalar_tensor_tensor(
                                out=seg(uconv, fc), in0=srcv, scalar=cwc[:, fc * 4 + j:fc * 4 + j + 1],
                                in1=seg(uconv, fc), op0=ALU.mult, op1=ALU.add),
                                reads=["u_raw", "cwc", f"uc_{fc}"], writes=[f"uc_{fc}"],
                                tag=key + "conv")
                for fc in (2 * q, 2 * q + 1):
                    P.op("scalar", lambda e, fc=fc: e.activation(
                        out=seg(uconv, fc), in_=seg(uconv, fc), func=AF.Silu,
                        bias=cbc[:, fc:fc + 1]),
                        reads=[f"uc_{fc}", "cbc"], writes=[f"uc_{fc}"], tag=key + "u2silu")
            P.op("vector", lambda e: e.tensor_copy(
                hist_t.rearrange("p (c t) -> p c t", t=3), u3d[:, :, TB:TB + 3]),
                reads=["u_raw"], writes=["hist_t"], tag=key + "tails")

        def stage_gate(b, blk):
            key = f"C{b}_{blk}"
            uc_keys = [f"uc_{fc}" for fc in range(NFC)]
            pg, pgK = ps_g
            for k in range(NFC):
                P.op("tensor", lambda e, k=k, pg=pg: e.matmul(
                    pg, lhsT=wgT_b[:, k * 96:(k + 1) * 96], rhs=seg(uconv, k),
                    start=(k == 0), stop=(k == NFC - 1), skip_group_check=True),
                    reads=["wgT", f"uc_{k}"], writes=pgK, tag=key + "gmm")
            P.op("vector", lambda e, pg=pg: e.tensor_copy(g_sb2, pg),
                 reads=pgK, writes=["gsb"], tag=key + "gevac")
            for m in range(NFC):
                pd, pdK = ps_d[m % 4]
                P.op("tensor", lambda e, m=m, pd=pd: e.matmul(
                    pd, lhsT=wdT_b[:, m * 128:(m + 1) * 128], rhs=g_sb2[0:64, :],
                    start=True, stop=True, skip_group_check=True),
                    reads=["wdT", "gsb"], writes=pdK, tag=key + "dmm")
                P.op("scalar", lambda e, m=m, pd=pd: e.activation(
                    out=dtmp, in_=pd, func=AF.Exp, bias=dbc[:, m:m + 1]),
                    reads=pdK + ["dbc"], writes=["dtmp", "fb0"], tag=key + "dexp")
                P.op("scalar", lambda e, m=m: e.activation(
                    out=seg(delta, m), in_=dtmp, func=AF.Ln, bias=1.0),
                    reads=["dtmp", "fb0", "dreg"], writes=[f"dl_{m}", "dreg"], tag=key + "dln")
            P.op("vector", lambda e: e.tensor_tensor(out=du_sl, in0=delta, in1=uconv,
                                                     op=ALU.mult),
                 reads=[f"dl_{m}" for m in range(NFC)] + uc_keys + ["dreg"],
                 writes=["du"], tag=key + "du")

        def stage_scan(b, blk, par, midnxt):
            t0s = blk * TB
            key = f"C{b}_{blk}"
            P.op("sync", lambda e, t0s=t0s, b=b: e.dma_start(
                out=aC.rearrange("p (s a) -> p s a", a=NSLOT),
                in_=ascr[b, t0s:t0s + TB, :].rearrange("(s p) a -> p s a", p=128)),
                writes=["aC"], dma=True, tag=key + "aload")

            def embrep(s):
                br, cr = brep[s % 2], crep[s % 2]
                P.op("sync", lambda e, s=s, br=br: e.dma_start(
                    out=br, in_=g_sb2[64 + s:65 + s, :].unsqueeze(1).broadcast_to((1, 128, TB))),
                    reads=["gsb"], writes=[f"brep{s % 2}"], dma=True, tag=key + "brep")
                P.op("sync", lambda e, s=s, cr=cr: e.dma_start(
                    out=cr, in_=g_sb2[80 + s:81 + s, :].unsqueeze(1).broadcast_to((1, 128, TB))),
                    reads=["gsb"], writes=[f"crep{s % 2}"], dma=True, tag=key + "crep")

            def emgbuild(s, gq, eng):
                br = brep[s % 2]
                gsl = gslab[:, gq * 4 * TB:(gq + 1) * 4 * TB]
                P.op(eng, lambda e, br=br, gsl=gsl: e.tensor_tensor(
                    out=gsl.rearrange("p (c t) -> p c t", t=TB),
                    in0=du_sl[:, gq * 4 * TB:(gq + 1) * 4 * TB].rearrange("p (c t) -> p c t", t=TB),
                    in1=br.unsqueeze(1).broadcast_to((128, 4, TB)), op=ALU.mult),
                    reads=["du", f"brep{s % 2}"], writes=[f"gsl{gq}"],
                    tag=key + "gbuild")

            embrep(0)
            for gq in range(4):
                emgbuild(0, gq, "vector")
            for s in range(NS):
                if s + 1 < NS:
                    embrep(s + 1)
                for half in range(2):
                    for fc8 in range(8):
                        fc = half * 8 + fc8
                        fb = fbuf[fc % 4]
                        P.op("scalar", lambda e, s=s, fc=fc, fb=fb: e.activation(
                            out=fb, in_=seg(delta, fc), func=AF.Exp,
                            scale=acol[:, fc * NS + s:fc * NS + s + 1]),
                            reads=[f"dl_{fc}", "acol", "dreg"], writes=[f"fb{fc % 4}"],
                            tag=key + "fexp")
                        P.op("vector", lambda e, s=s, fc=fc, fc8=fc8, fb=fb, b=b: e.tensor_tensor_scan(
                            out=seg(hslab, fc8), data0=fb, data1=seg(gslab, fc),
                            initial=carry[b][:, s * NFC + fc:s * NFC + fc + 1],
                            op0=ALU.mult, op1=ALU.add),
                            reads=[f"fb{fc % 4}", f"gsl{fc // 4}", f"ca_{b}_{s}_{fc}"],
                            writes=[f"h_{fc8}"], tag=key + "scan")
                        if s + 1 < NS and fc8 == 3:
                            emgbuild(s + 1, half * 2, "vector")
                        if s + 1 < NS and fc8 == 7:
                            emgbuild(s + 1, half * 2 + 1, "vector")
                    P.op("vector", lambda e, s=s, b=b, half=half: e.tensor_copy(
                        carry[b][:, s * NFC + half * 8:s * NFC + half * 8 + 8],
                        hslab.rearrange("p (c t) -> p c t", t=TB)[:, :, TB - 1]),
                        reads=[f"h_{j}" for j in range(8)],
                        writes=[f"ca_{b}_{s}_{half * 8 + j}" for j in range(8)],
                        tag=key + "carry")
                    if midnxt is not None:
                        g = s * 2 + half
                        if g == 0:
                            stage_mid_dma(midnxt[0], midnxt[1], 0)
                            stage_mid_dma(midnxt[0], midnxt[1], 1)
                        if g + 2 < 32:
                            stage_mid_dma(midnxt[0], midnxt[1], g + 2)
                        stage_mid_mm(midnxt[0], midnxt[1], midnxt[2], g)
                    for yq in range(2):
                        q0 = half * 8 * TB + yq * 4 * TB
                        ysl_q = y_sl[:, q0:q0 + 4 * TB]
                        h_q = hslab[:, yq * 4 * TB:(yq + 1) * 4 * TB]
                        m_q = mslab[:, yq * 4 * TB:(yq + 1) * 4 * TB]
                        hkeys = [f"h_{yq * 4 + j}" for j in range(4)]
                        if s == 0:
                            P.op("vector", lambda e, cr=crep[s % 2], ysl_q=ysl_q, h_q=h_q: e.tensor_tensor(
                                out=ysl_q.rearrange("p (c t) -> p c t", t=TB),
                                in0=h_q.rearrange("p (c t) -> p c t", t=TB),
                                in1=cr.unsqueeze(1).broadcast_to((128, 4, TB)), op=ALU.mult),
                                reads=hkeys + [f"crep{s % 2}"],
                                writes=["y_sl"], tag=key + "ymul")
                        else:
                            P.op("vector", lambda e, cr=crep[s % 2], h_q=h_q, m_q=m_q: e.tensor_tensor(
                                out=m_q.rearrange("p (c t) -> p c t", t=TB),
                                in0=h_q.rearrange("p (c t) -> p c t", t=TB),
                                in1=cr.unsqueeze(1).broadcast_to((128, 4, TB)), op=ALU.mult),
                                reads=hkeys + [f"crep{s % 2}"],
                                writes=["mslab"], tag=key + "ymul")
                            P.op("vector", lambda e, ysl_q=ysl_q, m_q=m_q: e.tensor_tensor(
                                out=ysl_q, in0=ysl_q, in1=m_q, op=ALU.add),
                                reads=["y_sl", "mslab"], writes=["y_sl"], tag=key + "yadd")
            for fc in range(NFC):
                P.op("scalar", lambda e, fc=fc, par=par: e.activation(
                    out=seg(go_sl[par], fc), in_=seg(go_sl[par], fc), func=AF.Silu),
                    reads=[f"go_{par}_{fc}"], writes=[f"go_{par}_{fc}"], tag=key + "gosilu")
            for fc in range(NFC):
                P.op("vector", lambda e, fc=fc: e.scalar_tensor_tensor(
                    out=seg(mslab, fc % 8), in0=seg(uconv, fc), scalar=dcolv[:, fc:fc + 1],
                    in1=seg(y_sl, fc), op0=ALU.mult, op1=ALU.add),
                    reads=[f"uc_{fc}", "y_sl", "dcolv"], writes=[f"t2_{fc}", "mslab"],
                    tag=key + "du2")
                P.op("vector", lambda e, fc=fc, par=par: e.tensor_tensor(
                    out=seg(y_sl, fc), in0=seg(mslab, fc % 8), in1=seg(go_sl[par], fc), op=ALU.mult),
                    reads=[f"t2_{fc}", f"go_{par}_{fc}"], writes=["y_sl", f"y2_{fc}"],
                    tag=key + "y2")

        def stage_tail_a(b, blk):
            t0 = blk * TB
            key = f"C{b}_{blk}"
            for sub in range(2):
                for nh in range(2):
                    pf, pfK = ps_fin[sub][nh]
                    P.op("tensor", lambda e, sub=sub, nh=nh, pf=pf: e.matmul(
                        pf, lhsT=ones[64:65, 0:128],
                        rhs=bo_row[:, nh * 512:(nh + 1) * 512],
                        start=True, stop=False, skip_group_check=True),
                        reads=["ones", "bo_row"], writes=pfK, tag=key + "bomm")
            for sub in range(2):
                pa, paK = ps_tpa[sub]
                P.op("tensor", lambda e, sub=sub, pa=pa: e.transpose(
                    pa, aC[:, sub * NSLOT:(sub + 1) * NSLOT], ident[:]),
                    reads=["aC", "ident"], writes=paK)
                P.op("vector", lambda e, sub=sub, pa=pa: e.tensor_copy(
                    aT_sb[:, sub * 128:(sub + 1) * 128], pa),
                    reads=paK + ["dreg"], writes=[f"aT{sub}", "dreg"])
            wslots6 = wslots + [(gslab[:, 0:1024], "gsl0"), (gslab[:, 1024:2048], "gsl1")]
            for c in range(NFC):
                wb, wbK2 = wslots6[c % 6]
                P.op("sync", lambda e, c=c, wb=wb: e.dma_start(
                    out=wb, in_=w_outT[c * 128:(c + 1) * 128, :]),
                    writes=[wbK2], dma=True, tag=key + "woutdma")
                for sub in range(2):
                    for nh in range(2):
                        pf, pfK = ps_fin[sub][nh]
                        P.op("tensor", lambda e, c=c, sub=sub, nh=nh, wb=wb, pf=pf: e.matmul(
                            pf, lhsT=seg(y_sl, c)[:, sub * 128:sub * 128 + 128],
                            rhs=wb[:, nh * 512:(nh + 1) * 512],
                            start=False, stop=False, skip_group_check=True),
                            reads=[wbK2, f"y2_{c}", "y_sl"], writes=pfK,
                            tag=key + "woutmm")
        def stage_tail_b(b, blk):
            t0 = blk * TB
            key = f"C{b}_{blk}"
            for m in range(8):
                pf0, pf0K = ps_f0[m % 4]
                P.op("tensor", lambda e, m=m, b=b, pf0=pf0: e.matmul(
                    pf0, lhsT=sv_sb[b][:, m * 128:(m + 1) * 128], rhs=aT_sb,
                    start=True, stop=True, skip_group_check=True),
                    reads=[f"sv{b}", "aT0", "aT1", "dreg"], writes=pf0K, tag=key + "f0mm")
                P.op("vector", lambda e, m=m, pf0=pf0: e.tensor_copy(
                    f0T_sb[:, m * TB:(m + 1) * TB], pf0),
                    reads=pf0K + ["dreg"], writes=[f"f0T{m}", "dreg"])
            for k in range(8):
                wb, wbK = wslots[k % 4]
                P.op("sync", lambda e, k=k, wb=wb: e.dma_start(
                    out=wb, in_=wo_d[k * 128:(k + 1) * 128, :]),
                    writes=[wbK], dma=True, tag=key + "wodma")
                for sub in range(2):
                    for nh in range(2):
                        pf, pfK = ps_fin[sub][nh]
                        P.op("tensor", lambda e, k=k, sub=sub, nh=nh, wb=wb, pf=pf: e.matmul(
                            pf, lhsT=f0T_sb[:, k * TB + sub * 128: k * TB + sub * 128 + 128],
                            rhs=wb[:, nh * 512:(nh + 1) * 512],
                            start=False, stop=(k == 7), skip_group_check=True),
                            reads=[wbK, f"f0T{k}", "dreg"], writes=pfK,
                            tag=key + "womm")
            for sub in range(2):
                for nh in range(2):
                    pf, pfK = ps_fin[sub][nh]
                    idx = sub * 2 + nh
                    dst = fin_d[:, idx * 512:(idx + 1) * 512]
                    P.op("vector", lambda e, pf=pf, dst=dst: e.tensor_copy(dst, pf),
                         reads=pfK, writes=["finP", "mslab"], tag=key + "finevac")
            P.op("gpsimd", lambda e, b=b, t0=t0: e.dma_start(
                out=out_hbm[b, t0:t0 + 128, :],
                in_=fin_d[:, 0:1024], accum_op=ALU.add),
                reads=["finP", "mslab"], dma=True, tag=key + "acc0", qsem="acc0")
            P.op("gpsimd", lambda e, b=b, t0=t0: e.dma_start(
                out=out_hbm[b, t0 + 128:t0 + 256, :],
                in_=fin_d[:, 1024:2048], accum_op=ALU.add),
                reads=["finP", "mslab"], dma=True, tag=key + "acc1", qsem="acc1")

        nblk_c = LIMIT_BLOCKS or NBLK
        for b in range(BPC):
            P.op("vector", lambda e, b=b: e.memset(carry[b][:], 0.0),
                 writes=[f"ca_{b}_{s}_{fc}" for s in range(NS) for fc in range(NFC)])
        seqC = [(b, blk) for b in range(BPC) for blk in range(nblk_c)]
        stage_front(*seqC[0])
        stage_mid_grp(*seqC[0], 0, range(32))
        stage_mid_post(*seqC[0])
        stage_gate(*seqC[0])
        for i, cur in enumerate(seqC):
            nxt = seqC[i + 1] if i + 1 < len(seqC) else None
            mn = (nxt + ((i + 1) % 2,)) if nxt else None
            if nxt:
                stage_front(*nxt)
            stage_scan(*cur, i % 2, mn)
            stage_tail_a(*cur)
            if nxt:
                stage_mid_post(*nxt)
            stage_tail_b(*cur)
            if nxt:
                stage_gate(*nxt)
        sbp.pop()

        if DBG_DUMP is not None:
            c0, c1 = DBG_DUMP
            dbg_t = dt("dbg", [128, c1 - c0], F32, kind="ExternalOutput").ap()
            P.barrier()
            P.op("sync", lambda e: e.dma_start(out=dbg_t, in_=sbp.raw(0, 128, c0, c1)),
                 dma=True, qsem="dbgdump")
        P.emit(blk_ctx, mksem, final_eng="sync")
    return nc


_NC_CACHE = {}


def _get_nc():
    if "nc" not in _NC_CACHE:
        _NC_CACHE["nc"] = build_program()
    return _NC_CACHE["nc"]


def kernel(**inp):
    def c(a):
        return np.ascontiguousarray(np.asarray(a, dtype=np.float32))

    x = c(inp["L0_attn_x"])

    def cols(v, n):
        return np.ascontiguousarray(v.reshape(n, 128).T)

    W_in = c(inp["L0_attn_W_in"]); conv_w = c(inp["L0_attn_conv_w"])
    conv_b = c(inp["L0_attn_conv_b"]); W_gate = c(inp["L0_attn_W_gate"])
    W_delta = c(inp["L0_attn_W_delta"]); delta_bias = c(inp["L0_attn_delta_bias"])
    A_log = c(inp["L0_attn_A_log"]); Dv = c(inp["L0_attn_D"])
    W_out = c(inp["L0_attn_W_out"]); slot_init = c(inp["L0_ffn_slot_init"])
    w1 = c(inp["L0_ffn_w1"]); b1 = c(inp["L0_ffn_b1"])
    w2 = c(inp["L0_ffn_w2"]); b2 = c(inp["L0_ffn_b2"])
    tau = c(inp["L0_ffn_tau"]); Wih = c(inp["L0_ffn_Wih"]); Whh = c(inp["L0_ffn_Whh"])
    bih = c(inp["L0_ffn_bih"]); bhh = c(inp["L0_ffn_bhh"])
    wq = c(inp["L0_ffn_wq"]); bq = c(inp["L0_ffn_bq"])
    wk = c(inp["L0_ffn_wk"]); bk = c(inp["L0_ffn_bk"])
    wv = c(inp["L0_ffn_wv"]); bv = c(inp["L0_ffn_bv"])
    ln_g = c(inp["L0_ffn_ln_g"]); ln_b = c(inp["L0_ffn_ln_b"])
    wvp = c(inp["L0_ffn_wvp"]); bvp = c(inp["L0_ffn_bvp"])
    wo = c(inp["L0_ffn_wo"]); bo = c(inp["L0_ffn_bo"])

    common = {
        "w_inT": np.ascontiguousarray(W_in.T.reshape(8, 128, 32, 128).transpose(2, 1, 0, 3).reshape(32, 128, 1024)),
        "conv_w_c": np.ascontiguousarray(conv_w[:, 0, :].reshape(NFC, 128, 4)
                                         .transpose(1, 0, 2).reshape(128, NFC * 4)),
        "conv_b_c": cols(conv_b, NFC),
        "w_gateT": np.ascontiguousarray(W_gate.T),
        "w_deltaT": np.ascontiguousarray(W_delta.T),
        "dbias_c": cols(delta_bias, NFC),
        "alog_c": np.ascontiguousarray(A_log.reshape(NFC, 128, NS)
                                       .transpose(1, 0, 2).reshape(128, NFC * NS)),
        "dvec_c": cols(Dv, NFC),
        "w_outT": np.ascontiguousarray(W_out.T),
        "slot_init": slot_init,
        "slot_initT": np.ascontiguousarray(slot_init.T),
        "w1a": np.ascontiguousarray(w1[:D]),
        "w1b": np.ascontiguousarray(w1[D:]),
        "b1_c": cols(b1, 4),
        "w2": w2,
        "b2_c": np.ascontiguousarray(b2.reshape(NSLOT, 1)),
        "tau": np.ascontiguousarray(tau.reshape(1, 1)),
        "wihT": np.ascontiguousarray(Wih.T),
        "whhT": np.ascontiguousarray(Whh.T),
        "bih_c": cols(bih, 24),
        "bhh_c": cols(bhh, 24),
        "wq": wq, "wk": wk,
        "bqk_c": np.ascontiguousarray(np.concatenate([cols(bq, 2), cols(bk, 2)], axis=1)),
        "wv": wv,
        "bv_r": np.ascontiguousarray(np.broadcast_to(bv, (NSLOT, D))),
        "lng_r": np.ascontiguousarray(np.broadcast_to(ln_g, (NSLOT, D))),
        "lnb_r": np.ascontiguousarray(np.broadcast_to(ln_b, (NSLOT, D))),
        "wvp": wvp,
        "bvp_r": np.ascontiguousarray(np.broadcast_to(bvp, (NSLOT, D))),
        "wo": wo,
        "bo_r": np.ascontiguousarray(bo.reshape(1, D)),
    }

    nc = _get_nc()
    ncore = NCORE if N_CORES_OVERRIDE is None else N_CORES_OVERRIDE
    in_maps = []
    for core in range(ncore):
        m = dict(common)
        m["x"] = np.ascontiguousarray(x[core * BPC:(core + 1) * BPC])
        in_maps.append(m)
    res = run_bass_kernel_spmd(nc, in_maps, core_ids=list(range(ncore)))
    return np.concatenate([r["out"] for r in res.results], axis=0)


if __name__ == "__main__":
    build_program()
    print("built ok")


# version 15
# speedup vs baseline: 1.3485x; 1.0132x over previous
import numpy as np
import concourse.bass as bass
import concourse.mybir as mybir
from concourse.bass_utils import run_bass_kernel_spmd
from concourse.alu_op_type import AluOpType as ALU

F32 = mybir.dt.float32
AF = mybir.ActivationFunctionType
AX = mybir.AxisListType

B, T, D = 16, 4096, 1024
HID = 2048
NS = 16
NSLOT, TOPK = 64, 8
DQ, DH = 256, 512
NCORE = 8
BPC = B // NCORE
TB = 256
NBLK = T // TB
NFC = HID // 128
SEG = TB + 4


class Prog:
    ENGS = ("sync", "tensor", "vector", "scalar", "gpsimd")

    def __init__(self):
        self.ops = []
        self.deps = []
        self.writer = {}
        self.readers = {}
        self.fence = set()

    def op(self, eng, builder, reads=(), writes=(), dma=False, tag="", qsem=None):
        i = len(self.ops)
        if dma and qsem is None:
            qsem = writes[0] if writes else (tag or "dma")
        rb = [k for k in reads if k.startswith("PSB")]
        if rb:
            reads = [k for k in reads if not k.startswith("PSB")]
            writes = list(writes) + rb
        dep = set(self.fence)
        for k in reads:
            w = self.writer.get(k)
            if w is not None:
                dep.add(w)
        for k in writes:
            for r in self.readers.get(k, ()):
                dep.add(r)
            w = self.writer.get(k)
            if w is not None:
                dep.add(w)
        dep.discard(i)
        self.ops.append((eng, builder, dma, tag, qsem))
        self.deps.append(dep)
        for k in reads:
            self.readers.setdefault(k, []).append(i)
        for k in writes:
            self.writer[k] = i
            self.readers[k] = []
        return i

    def barrier(self):
        last = {}
        for i, (eng, _b, isdma, _t, qs) in enumerate(self.ops):
            last[(eng, isdma, qs)] = i
        self.fence = set(last.values())

    def emit(self, block, mksem, final_eng="sync"):
        if TRUNC_OPS is not None:
            self.ops = self.ops[:TRUNC_OPS]
            self.deps = [set(x for x in d if x < TRUNC_OPS)
                         for d in self.deps[:TRUNC_OPS]]
        n = len(self.ops)
        eng_of = [o[0] for o in self.ops]
        isdma = [o[2] for o in self.ops]
        qsem_of = [o[4] for o in self.ops]
        publish = [False] * n
        for j in range(n):
            for i in self.deps[j]:
                if (FULL_SYNC or eng_of[i] != eng_of[j] or isdma[i] or isdma[j]
                        or eng_of[i] in SYNC_ENGS):
                    publish[i] = True
        for i in range(n):
            if isdma[i]:
                publish[i] = True
        doms = []
        dom = [None] * n
        for i in range(n):
            d = ("D", qsem_of[i]) if isdma[i] else ("E", eng_of[i])
            dom[i] = d
            if d not in doms:
                doms.append(d)
        sems = {d: mksem(f"{d[0]}_{d[1]}"[:28]) for d in doms}
        pubcount = {}
        val_after = [0] * n
        for i in range(n):
            if publish[i]:
                pubcount[dom[i]] = pubcount.get(dom[i], 0) + (16 if isdma[i] else 1)
            val_after[i] = pubcount.get(dom[i], 0)
        finals = dict(pubcount)

        def section(engname):
            def go(eng):
                waited = {}
                for i in range(n):
                    if eng_of[i] != engname:
                        continue
                    need = {}
                    for j in self.deps[i]:
                        if not publish[j]:
                            continue
                        if (not FULL_SYNC and dom[j] == ("E", engname)
                                and engname not in SYNC_ENGS and not isdma[i]):
                            continue
                        v = val_after[j]
                        if need.get(dom[j], 0) < v:
                            need[dom[j]] = v
                    for d, v in sorted(need.items(), key=lambda kv: str(kv[0])):
                        if waited.get(d, 0) < v:
                            eng.wait_ge(sems[d], v)
                            waited[d] = v
                    if ((engname in NOP_ENGS or any(t in self.ops[i][3] for t in NOP_TAGS)
                         or (NOP_TAGS and "UNTAGGED" in NOP_TAGS and engname == "scalar"
                             and not self.ops[i][3])) and not isdma[i]):
                        inst = eng.nop()
                    else:
                        inst = self.ops[i][1](eng)
                    if publish[i]:
                        inst.then_inc(sems[dom[i]], 16 if isdma[i] else 1)
                if engname == final_eng:
                    for d, v in sorted(finals.items(), key=lambda kv: str(kv[0])):
                        if d[0] == "D":
                            eng.wait_ge(sems[d], v)
            return go

        block.sync(section("sync"))
        block.tensor(section("tensor"))
        block.vector(section("vector"))
        block.scalar(section("scalar"))
        block.gpsimd(section("gpsimd"))


class Pool:
    def __init__(self, ap, n):
        self.ap, self.n, self.off, self.marks = ap, n, 0, []

    def alloc(self, cols, parts=128):
        cols_al = (cols + 15) & ~15
        assert self.off + cols_al <= self.n, f"SBUF overflow {self.off}+{cols_al}>{self.n}"
        a = self.ap[0:parts, self.off:self.off + cols]
        self.off += cols_al
        return a

    def push(self):
        self.marks.append(self.off)

    def pop(self):
        self.off = self.marks.pop()

    def raw(self, p0, p1, c0, c1):
        return self.ap[p0:p1, c0:c1]


PHASES = "ABC"
LIMIT_BLOCKS = None
TRUNC_OPS = None
N_CORES_OVERRIDE = None
NOP_ENGS = ()
NOP_TAGS = ()
DBG = {}
DBG_DUMP = None
FULL_SYNC = False
RACE_DETECT = False
SYNC_ENGS = ("gpsimd", "vector", "scalar")


def build_program():
    nc = bass.Bass("TRN2", target_bir_lowering=False, detect_race_conditions=RACE_DETECT)
    dt = nc.dram_tensor

    x_in = dt("x", [BPC, T, D], F32, kind="ExternalInput").ap()
    out_hbm = dt("out", [BPC, T, D], F32, kind="ExternalOutput").ap()
    ascr = dt("ascr", [BPC, T, NSLOT], F32, kind="Internal").ap()

    w_inT = dt("w_inT", [32, 128, 8 * 128], F32, kind="ExternalInput").ap()
    conv_w_c = dt("conv_w_c", [128, NFC * 4], F32, kind="ExternalInput").ap()
    conv_b_c = dt("conv_b_c", [128, NFC], F32, kind="ExternalInput").ap()
    w_gateT = dt("w_gateT", [HID, 96], F32, kind="ExternalInput").ap()
    w_deltaT = dt("w_deltaT", [64, HID], F32, kind="ExternalInput").ap()
    dbias_c = dt("dbias_c", [128, NFC], F32, kind="ExternalInput").ap()
    alog_c = dt("alog_c", [128, NFC * NS], F32, kind="ExternalInput").ap()
    dvec_c = dt("dvec_c", [128, NFC], F32, kind="ExternalInput").ap()
    w_outT = dt("w_outT", [HID, D], F32, kind="ExternalInput").ap()

    slot_init = dt("slot_init", [NSLOT, D], F32, kind="ExternalInput").ap()
    slot_initT = dt("slot_initT", [D, NSLOT], F32, kind="ExternalInput").ap()
    w1a_d = dt("w1a", [D, DH], F32, kind="ExternalInput").ap()
    w1b_d = dt("w1b", [D, DH], F32, kind="ExternalInput").ap()
    b1_c = dt("b1_c", [128, 4], F32, kind="ExternalInput").ap()
    w2_d = dt("w2", [DH, NSLOT], F32, kind="ExternalInput").ap()
    b2_c = dt("b2_c", [NSLOT, 1], F32, kind="ExternalInput").ap()
    tau_d = dt("tau", [1, 1], F32, kind="ExternalInput").ap()
    wihT = dt("wihT", [D, 3 * D], F32, kind="ExternalInput").ap()
    whhT = dt("whhT", [D, 3 * D], F32, kind="ExternalInput").ap()
    bih_c = dt("bih_c", [128, 24], F32, kind="ExternalInput").ap()
    bhh_c = dt("bhh_c", [128, 24], F32, kind="ExternalInput").ap()
    wq_d = dt("wq", [D, DQ], F32, kind="ExternalInput").ap()
    wk_d = dt("wk", [D, DQ], F32, kind="ExternalInput").ap()
    bqk_c = dt("bqk_c", [128, 4], F32, kind="ExternalInput").ap()
    wv_d = dt("wv", [D, D], F32, kind="ExternalInput").ap()
    bv_r = dt("bv_r", [NSLOT, D], F32, kind="ExternalInput").ap()
    lng_r = dt("lng_r", [NSLOT, D], F32, kind="ExternalInput").ap()
    lnb_r = dt("lnb_r", [NSLOT, D], F32, kind="ExternalInput").ap()
    wvp_d = dt("wvp", [D, D], F32, kind="ExternalInput").ap()
    bvp_r = dt("bvp_r", [NSLOT, D], F32, kind="ExternalInput").ap()
    wo_d = dt("wo", [D, D], F32, kind="ExternalInput").ap()
    bo_r = dt("bo_r", [1, D], F32, kind="ExternalInput").ap()

    import contextlib
    st = contextlib.ExitStack()
    with st:
        POOLN = 53200
        pool_t = st.enter_context(nc.sbuf_tensor("pool", [128, POOLN], F32))
        sbp = Pool(pool_t.ap(), POOLN)
        psum_t = st.enter_context(nc.psum_tensor("ppool", [128, 4096], F32))
        ps = psum_t.ap()

        def R(c0, c1, p0=0, p1=128):
            return ps[p0:p1, c0:c1], ["PSB%d" % i for i in range(c0 // 512, (c1 + 511) // 512)]

        def mksem(name):
            return st.enter_context(nc.semaphore(name))

        blk_ctx = st.enter_context(nc.Block())
        P = Prog()

        ident = sbp.alloc(128)
        ones = sbp.alloc(128)
        konst = sbp.alloc(4)
        cwc = sbp.alloc(NFC * 4)
        cbc = sbp.alloc(NFC)
        dbc = sbp.alloc(NFC)
        acol = sbp.alloc(NFC * NS)
        dcolv = sbp.alloc(NFC)
        wgT_b = sbp.alloc(NFC * 96)
        wdT_b = sbp.alloc(HID, parts=64)
        si_sb2 = [sbp.alloc(D, parts=64) for _ in range(BPC)]
        dncol = sbp.alloc(2, parts=64)
        sv0_off = sbp.off
        sv_sb = [sbp.alloc(D, parts=64) for _ in range(BPC)]
        bo_row = sbp.raw(64, 65, sv0_off, sv0_off + D)
        DBG["sv0"] = sv0_off
        carry = [sbp.alloc(NS * NFC) for _ in range(BPC)]
        hist_t = sbp.alloc(NFC * 3)

        P.op("gpsimd", lambda e: e.memset(ident[:], 0.0), writes=["ident"])
        P.op("gpsimd", lambda e: e.affine_select(
            out=ident[:], in_=ident[:], compare_op=ALU.not_equal,
            fill=1.0, base=0, pattern=[[-1, 128]], channel_multiplier=1),
            reads=["ident"], writes=["ident"])
        P.op("gpsimd", lambda e: e.memset(ones[:], 1.0), writes=["ones"])
        P.op("gpsimd", lambda e: e.memset(konst[:, 0:1], 1e-5), writes=["konst"])
        P.op("gpsimd", lambda e: e.memset(konst[:, 1:2], -0.5), writes=["konst"])
        for nm, dst, src in [("cwc", cwc, conv_w_c), ("cbc", cbc, conv_b_c),
                             ("dbc", dbc, dbias_c), ("dcolv", dcolv, dvec_c),
                             ("bo_row", bo_row, bo_r)]:
            P.op("sync", lambda e, d=dst, s=src: e.dma_start(out=d, in_=s),
                 writes=[nm], dma=True)
        P.op("sync", lambda e: e.dma_start(out=acol, in_=alog_c), writes=["acol"], dma=True)
        P.op("scalar", lambda e: e.activation(out=acol, in_=acol, func=AF.Exp),
             reads=["acol"], writes=["acol"])
        P.op("vector", lambda e: e.tensor_scalar(out=acol, in0=acol, scalar1=-1.0,
                                                 scalar2=None, op0=ALU.mult),
             reads=["acol"], writes=["acol"])
        P.op("sync", lambda e: e.dma_start(
            out=wgT_b.rearrange("p (c n) -> p c n", n=96),
            in_=w_gateT.rearrange("(c p) n -> p c n", p=128)), writes=["wgT"], dma=True)
        P.op("sync", lambda e: e.dma_start(out=wdT_b, in_=w_deltaT), writes=["wdT"], dma=True)

        def layernorm(xt, xn_out, cols8, scratch, key, rk, scr_key, xtra=()):
            s_ = cols8
            P.op("vector", lambda e: e.reduce_sum(out=s_[:, 0:1], in_=xt, axis=AX.X),
                 reads=[rk], writes=[key + "s", "lncol" + scr_key])
            P.op("vector", lambda e: e.tensor_tensor(out=scratch, in0=xt, in1=xt,
                                                     op=ALU.mult),
                 reads=[rk], writes=["lnscr" + scr_key] + list(xtra))
            P.op("vector", lambda e: e.reduce_sum(out=s_[:, 1:2], in_=scratch, axis=AX.X),
                 reads=["lnscr" + scr_key], writes=[key + "q"])
            P.op("vector", lambda e: e.tensor_scalar(out=s_[:, 0:1], in0=s_[:, 0:1],
                                                     scalar1=1.0 / D, scalar2=None, op0=ALU.mult),
                 reads=[key + "s"], writes=[key + "m"])
            P.op("vector", lambda e: e.tensor_tensor(out=s_[:, 2:3], in0=s_[:, 0:1],
                                                     in1=s_[:, 0:1], op=ALU.mult),
                 reads=[key + "m"], writes=[key + "m2"])
            P.op("vector", lambda e: e.tensor_scalar(out=s_[:, 1:2], in0=s_[:, 1:2],
                                                     scalar1=1.0 / D, scalar2=None, op0=ALU.mult),
                 reads=[key + "q"], writes=[key + "q2"])
            P.op("vector", lambda e: e.tensor_tensor(out=s_[:, 1:2], in0=s_[:, 1:2],
                                                     in1=s_[:, 2:3], op=ALU.subtract),
                 reads=[key + "q2", key + "m2"], writes=[key + "v"])
            P.op("scalar", lambda e: e.activation(out=s_[:, 1:2], in_=s_[:, 1:2],
                                                  func=AF.Ln, bias=konst[:, 0:1]),
                 reads=[key + "v", "konst"], writes=[key + "l"])
            P.op("scalar", lambda e: e.activation(out=s_[:, 1:2], in_=s_[:, 1:2],
                                                  func=AF.Exp, scale=konst[:, 1:2]),
                 reads=[key + "l", "konst"], writes=[key + "i"])
            P.op("vector", lambda e: e.tensor_tensor(out=s_[:, 2:3], in0=s_[:, 0:1],
                                                     in1=s_[:, 1:2], op=ALU.mult),
                 reads=[key + "m", key + "i"], writes=[key + "mi"])
            P.op("vector", lambda e: e.tensor_scalar(out=xn_out, in0=xt,
                                                     scalar1=s_[:, 1:2], scalar2=s_[:, 2:3],
                                                     op0=ALU.mult, op1=ALU.subtract),
                 reads=[rk, key + "i", key + "mi"], writes=[key + "xn"] + list(xtra))

        sbp.push()
        b1tot = sbp.alloc(4)
        b2col = sbp.alloc(1, parts=64)
        invt64 = sbp.alloc(1, parts=64)
        tau_sb = sbp.alloc(4, parts=1)
        P.op("sync", lambda e: e.dma_start(out=b2col, in_=b2_c), writes=["b2col"], dma=True)
        P.op("sync", lambda e: e.dma_start(out=tau_sb[0:1, 0:1], in_=tau_d),
             writes=["tau"], dma=True)
        P.op("vector", lambda e: e.tensor_scalar(out=tau_sb[0:1, 1:2], in0=tau_sb[0:1, 0:1],
                                                 scalar1=-1.0, scalar2=None, op0=ALU.mult),
             reads=["tau"], writes=["tau"])
        P.op("vector", lambda e: e.tensor_tensor(out=tau_sb[0:1, 2:3], in0=tau_sb[0:1, 0:1],
                                                 in1=tau_sb[0:1, 1:2], op=ALU.max),
             reads=["tau"], writes=["tau"])
        P.op("vector", lambda e: e.tensor_scalar(out=tau_sb[0:1, 2:3], in0=tau_sb[0:1, 2:3],
                                                 scalar1=0.1, scalar2=None, op0=ALU.add),
             reads=["tau"], writes=["tau"])
        P.op("vector", lambda e: e.reciprocal(out=tau_sb[0:1, 3:4], in_=tau_sb[0:1, 2:3]),
             reads=["tau"], writes=["tau"])
        P.op("sync", lambda e: e.dma_start(
            out=invt64, in_=tau_sb[0:1, 3:4].unsqueeze(1).broadcast_to((1, 64, 1))),
            reads=["tau"], writes=["invt64"], dma=True)
        w1a_b = sbp.alloc(8 * DH)
        w1b_b = sbp.alloc(8 * DH)
        w2_b = sbp.alloc(4 * NSLOT)
        si_sp = sbp.alloc(D, parts=64)
        smT = sbp.alloc(8)
        bias2 = sbp.alloc(DH, parts=1)
        P.op("sync", lambda e: e.dma_start(
            out=w1a_b.rearrange("p (k f) -> p k f", f=DH),
            in_=w1a_d.rearrange("(k p) f -> p k f", p=128)), writes=["w1a"], dma=True)
        P.op("sync", lambda e: e.dma_start(
            out=w1b_b.rearrange("p (k f) -> p k f", f=DH),
            in_=w1b_d.rearrange("(k p) f -> p k f", p=128)), writes=["w1b"], dma=True)
        P.op("sync", lambda e: e.dma_start(
            out=w2_b.rearrange("p (k f) -> p k f", f=NSLOT),
            in_=w2_d.rearrange("(k p) f -> p k f", p=128)), writes=["w2"], dma=True)
        P.op("sync", lambda e: e.dma_start(out=si_sp, in_=slot_init), writes=["si_sp"], dma=True)
        P.op("sync", lambda e: e.dma_start(out=b1tot, in_=b1_c), writes=["b1tot"], dma=True)

        psm, psmK = R(3584, 3592)
        for k in range(8):
            P.op("tensor", lambda e, k=k: e.matmul(
                psm[:, k:k + 1], lhsT=si_sp[:, k * 128:(k + 1) * 128],
                rhs=ones[0:64, 0:1], start=True, stop=True, skip_group_check=True),
                reads=["si_sp", "ones"], writes=psmK)
        P.op("vector", lambda e: e.tensor_scalar(out=smT, in0=psm, scalar1=1.0 / 64,
                                                 scalar2=None, op0=ALU.mult),
             reads=psmK, writes=["smT"])
        pb2, pb2K = R(512, 1024, p1=1)
        for k in range(8):
            P.op("tensor", lambda e, k=k: e.matmul(
                pb2, lhsT=smT[:, k:k + 1], rhs=w1b_b[:, k * DH:(k + 1) * DH],
                start=(k == 0), stop=(k == 7), skip_group_check=True),
                reads=["smT", "w1b"], writes=pb2K)
        P.op("vector", lambda e: e.tensor_copy(bias2, pb2), reads=pb2K, writes=["bias2"])
        pbt, pbtK = R(3592, 3596)
        for m in range(4):
            P.op("tensor", lambda e, m=m: e.transpose(
                pbt[:, m:m + 1], bias2[0:1, m * 128:(m + 1) * 128], ident[0:1, 0:1]),
                reads=["bias2", "ident"], writes=pbtK)
        P.op("vector", lambda e: e.tensor_tensor(out=b1tot, in0=b1tot, in1=pbt, op=ALU.add),
             reads=pbtK + ["b1tot"], writes=["b1tot"])

        xA = sbp.alloc(2 * D)
        xnA = sbp.alloc(2 * D)
        lnscA = sbp.alloc(D)
        lncolA = sbp.alloc(8)
        xnTA = sbp.alloc(8 * TB)
        h_sb = sbp.alloc(4 * TB)
        lg_sb = sbp.alloc(TB, parts=64)
        lgT = sbp.alloc(2 * NSLOT)
        wkt = sbp.alloc(NSLOT)
        mcol = sbp.alloc(16)
        amt = sbp.alloc(NSLOT)
        alset = sbp.alloc(NSLOT)
        for _n, _a in [("xA", xA), ("xnA", xnA), ("lnscA", lnscA), ("lncolA", lncolA),
                       ("xnTA", xnTA), ("h_sb", h_sb), ("lg_sb", lg_sb), ("lgT", lgT),
                       ("wkt", wkt), ("mcol", mcol), ("amt", amt), ("alset", alset),
                       ("w1a", w1a_b), ("b1tot", b1tot), ("invt64", invt64), ("tau", tau_sb)]:
            DBG[_n] = _a.offset

        ps_hm = [R(0, 256), R(512, 768), R(256, 512), R(768, 1024)]
        ps_si = [R(1024, 2048, p1=64) for b in range(BPC)]
        ps_dn = [R(3072, 3080, p1=64) for b in range(BPC)]
        ps_lg = R(3584, 3840, p1=64)
        ps_tpA = [R(2048, 2176), R(2560, 2688)]

        def phaseA_block(b, blk):
            t0 = blk * TB
            key = f"A{b}_{blk}"
            P.op("sync", lambda e: e.dma_start(
                out=xA.rearrange("p (s d) -> p s d", d=D),
                in_=x_in[b, t0:t0 + TB, :].rearrange("(s p) d -> p s d", p=128)),
                writes=["xA"], dma=True, tag=key + "xload")
            P.op("scalar", lambda e: e.dma_start(
                out=out_hbm[b, t0:t0 + TB, :].rearrange("(s p) d -> p s d", p=128),
                in_=xA.rearrange("p (s d) -> p s d", d=D)),
                reads=["xA"], dma=True, tag=key + "xstore", qsem="store")
            for sub in range(2):
                xs = xA[:, sub * D:(sub + 1) * D]
                xn = xnA[:, sub * D:(sub + 1) * D]
                layernorm(xs, xn, lncolA, lnscA, key + f"ln{sub}", "xA", "A")
                for k in range(8):
                    tp, tpK = ps_tpA[k % 2]
                    P.op("tensor", lambda e, k=k, xn=xn, tp=tp: e.transpose(
                        tp[:, 0:128], xn[:, k * 128:(k + 1) * 128], ident[:]),
                        reads=[key + f"ln{sub}xn", "ident"], writes=tpK)
                    P.op("vector", lambda e, k=k, sub=sub, tp=tp: e.tensor_copy(
                        xnTA[:, k * TB + sub * 128: k * TB + sub * 128 + 128], tp[:, 0:128]),
                        reads=tpK, writes=[f"xnTA_{k}_{sub}"])
            xnT_keys = [f"xnTA_{k}_{s}" for k in range(8) for s in range(2)]
            for m in range(4):
                pm, pmK = ps_hm[m]
                for k in range(8):
                    P.op("tensor", lambda e, m=m, k=k, pm=pm: e.matmul(
                        pm, lhsT=w1a_b[:, k * DH + m * 128: k * DH + m * 128 + 128],
                        rhs=xnTA[:, k * TB:(k + 1) * TB],
                        start=(k == 0), stop=(k == 7), skip_group_check=True),
                        reads=["w1a"] + xnT_keys, writes=pmK, tag=key + "w1")
                P.op("scalar", lambda e, m=m, pm=pm: e.activation(
                    out=h_sb[:, m * TB:(m + 1) * TB], in_=pm, func=AF.Gelu,
                    bias=b1tot[:, m:m + 1]),
                    reads=pmK + ["b1tot"], writes=[f"h{m}"], tag=key + "gelu")
            plg, plgK = ps_lg
            for k in range(4):
                P.op("tensor", lambda e, k=k, plg=plg: e.matmul(
                    plg, lhsT=w2_b[:, k * NSLOT:(k + 1) * NSLOT],
                    rhs=h_sb[:, k * TB:(k + 1) * TB],
                    start=(k == 0), stop=(k == 3), skip_group_check=True),
                    reads=["w2", f"h{k}"], writes=plgK, tag=key + "w2")
            P.op("vector", lambda e, plg=plg: e.tensor_scalar(
                out=lg_sb, in0=plg, scalar1=b2col, scalar2=invt64,
                op0=ALU.add, op1=ALU.mult),
                reads=plgK + ["b2col", "invt64"], writes=["lg"], tag=key + "lgevac")
            for sub in range(2):
                tp, tpK = ps_tpA[sub]
                P.op("tensor", lambda e, sub=sub, tp=tp: e.transpose(
                    tp[:, 0:NSLOT], lg_sb[:, sub * 128:(sub + 1) * 128], ident[0:64, 0:64]),
                    reads=["lg", "ident"], writes=tpK)
                P.op("vector", lambda e, sub=sub, tp=tp: e.tensor_copy(
                    lgT[:, sub * NSLOT:(sub + 1) * NSLOT], tp[:, 0:NSLOT]),
                    reads=tpK, writes=[f"lgT{sub}"])
            for sub in range(2):
                lgs = lgT[:, sub * NSLOT:(sub + 1) * NSLOT]
                P.op("vector", lambda e, lgs=lgs: e.tensor_copy(wkt, lgs),
                     reads=[f"lgT{sub}"], writes=["wkt"])
                for it in range(TOPK):
                    P.op("vector", lambda e, it=it: e.reduce_max(
                        out=mcol[:, it:it + 1], in_=wkt, axis=AX.X, negate=True),
                        reads=["wkt"], writes=[f"mx{it}", "mcol"])
                    if it < TOPK - 1:
                        P.op("vector", lambda e, it=it: e.tensor_scalar(
                            out=amt, in0=wkt, scalar1=mcol[:, it:it + 1],
                            scalar2=0.0, op0=ALU.add, op1=ALU.is_ge),
                            reads=["wkt", f"mx{it}"], writes=["amt"])
                        P.op("vector", lambda e: e.scalar_tensor_tensor(
                            out=wkt, in0=amt, scalar=-1e38, in1=wkt,
                            op0=ALU.mult, op1=ALU.add),
                            reads=["amt", "wkt"], writes=["wkt"])
                P.op("scalar", lambda e, lgs=lgs: e.activation(
                    out=amt, in_=lgs, func=AF.Exp, bias=mcol[:, 0:1]),
                    reads=[f"lgT{sub}", "mx0"], writes=["amt"], tag=key + "aexp")
                P.op("vector", lambda e, lgs=lgs: e.tensor_scalar(
                    out=wkt, in0=lgs, scalar1=mcol[:, TOPK - 1:TOPK],
                    scalar2=0.0, op0=ALU.add, op1=ALU.is_ge),
                    reads=[f"lgT{sub}", f"mx{TOPK - 1}"], writes=["wkt"])
                P.op("vector", lambda e: e.tensor_tensor(
                    out=alset, in0=amt, in1=wkt, op=ALU.mult),
                    reads=["amt", "wkt"], writes=["alset"])
                P.op("vector", lambda e: e.reduce_sum(out=mcol[:, 8:9], in_=alset, axis=AX.X),
                     reads=["alset"], writes=["aZ", "mcol"])
                P.op("vector", lambda e: e.reciprocal(out=mcol[:, 9:10], in_=mcol[:, 8:9]),
                     reads=["aZ"], writes=["aZr", "mcol"])
                P.op("vector", lambda e: e.tensor_scalar(
                    out=alset, in0=alset, scalar1=mcol[:, 9:10], scalar2=None, op0=ALU.mult),
                    reads=["alset", "aZr"], writes=["alset"], tag=key + "anorm")
                P.op("scalar", lambda e, b=b, t0=t0, sub=sub: e.dma_start(
                    out=ascr[b, t0 + sub * 128: t0 + sub * 128 + 128, :], in_=alset),
                    reads=["alset"], dma=True, tag=key + "astore", qsem="astore")
                first = (blk == 0 and sub == 0)
                last = (blk == (LIMIT_BLOCKS or NBLK) - 1 and sub == 1)
                xn = xnA[:, sub * D:(sub + 1) * D]
                psi, psiK = ps_si[b]
                for half in range(2):
                    P.op("tensor", lambda e, xn=xn, half=half, first=first, last=last, psi=psi: e.matmul(
                        psi[:, half * 512:(half + 1) * 512], lhsT=alset,
                        rhs=xn[:, half * 512:(half + 1) * 512],
                        start=first, stop=last, skip_group_check=True),
                        reads=["alset", key + f"ln{sub}xn"], writes=psiK, tag=key + "simm")
                pdn, pdnK = ps_dn[b]
                P.op("tensor", lambda e, first=first, last=last, pdn=pdn: e.matmul(
                    pdn, lhsT=alset, rhs=ones[:, 0:8],
                    start=first, stop=last, skip_group_check=True),
                    reads=["alset", "ones"], writes=pdnK, tag=key + "dnmm")

        nblk_a = LIMIT_BLOCKS or NBLK
        for b in range(BPC):
            for blk in range(nblk_a):
                phaseA_block(b, blk)
            P.op("vector", lambda e, b=b: e.tensor_copy(dncol[:, b:b + 1], ps_dn[b][0][:, 0:1]),
                 reads=ps_dn[b][1], writes=[f"dncol{b}", "dncol"])
            P.op("vector", lambda e, b=b: e.tensor_copy(si_sb2[b], ps_si[b][0]),
                 reads=ps_si[b][1], writes=[f"siraw{b}"])
        sbp.pop()
        P.barrier()
        if "B" not in PHASES:
            if DBG_DUMP is not None:
                c0, c1 = DBG_DUMP
                dbg_t = dt("dbg", [128, c1 - c0], F32, kind="ExternalOutput").ap()
                P.barrier()
                P.op("sync", lambda e: e.dma_start(out=dbg_t, in_=sbp.raw(0, 128, c0, c1)),
                     dma=True, qsem="dbgdump")
            P.emit(blk_ctx, mksem, final_eng="sync")
            return nc

        sbp.push()
        s0T_b = sbp.alloc(8 * NSLOT)
        bihc = sbp.alloc(24)
        bhhc = sbp.alloc(24)
        bqkc = sbp.alloc(4)
        wm_b = [sbp.alloc(8 * 128) for _ in range(2)]
        wq_b = sbp.alloc(8 * DQ)
        wk_b = sbp.alloc(8 * DQ)
        wv_b = sbp.alloc(8 * D)
        wvp_b = sbp.alloc(8 * D)
        gh_sb = sbp.alloc(24 * NSLOT)
        gi_sb = sbp.alloc(24 * NSLOT)
        r_sb = sbp.alloc(8 * NSLOT)
        z_sb = sbp.alloc(8 * NSLOT)
        n_sb = sbp.alloc(8 * NSLOT)
        snew = sbp.alloc(8 * NSLOT)
        snewT = sbp.alloc(D, parts=64)
        q_sb = sbp.alloc(2 * NSLOT)
        k_sb = sbp.alloc(2 * NSLOT)
        vT_sb = sbp.alloc(D, parts=64)
        sc_sb = sbp.alloc(NSLOT, parts=64)
        at_sb = sbp.alloc(NSLOT, parts=64)
        y_sb = sbp.alloc(D, parts=64)
        yn_sb = sbp.alloc(D, parts=64)
        ynT_sb = sbp.alloc(NSLOT)
        siT = sbp.alloc(8 * NSLOT)
        colB = sbp.alloc(8, parts=64)
        scrB = sbp.alloc(D, parts=64)
        ggb = sbp.alloc(D, parts=64)
        for _n, _a in [("s0T", s0T_b), ("gh", gh_sb), ("gi", gi_sb), ("r", r_sb),
                       ("z", z_sb), ("n", n_sb), ("snew", snew), ("snewT", snewT),
                       ("q", q_sb), ("k", k_sb), ("vT", vT_sb), ("sc", sc_sb),
                       ("at", at_sb), ("y", y_sb), ("yn", yn_sb), ("siT", siT),
                       ("scrB", scrB), ("si0", si_sb2[0]), ("si1", si_sb2[1])]:
            DBG[_n] = _a.offset

        P.op("sync", lambda e: e.dma_start(
            out=s0T_b.rearrange("p (k f) -> p k f", f=NSLOT),
            in_=slot_initT.rearrange("(k p) f -> p k f", p=128)), writes=["s0T"], dma=True)
        P.op("sync", lambda e: e.dma_start(out=bihc, in_=bih_c), writes=["bihc"], dma=True)
        P.op("sync", lambda e: e.dma_start(out=bhhc, in_=bhh_c), writes=["bhhc"], dma=True)
        P.op("sync", lambda e: e.dma_start(out=bqkc, in_=bqk_c), writes=["bqkc"], dma=True)
        P.op("sync", lambda e: e.dma_start(
            out=wq_b.rearrange("p (k f) -> p k f", f=DQ),
            in_=wq_d.rearrange("(k p) f -> p k f", p=128)), writes=["wqb"], dma=True)
        P.op("sync", lambda e: e.dma_start(
            out=wk_b.rearrange("p (k f) -> p k f", f=DQ),
            in_=wk_d.rearrange("(k p) f -> p k f", p=128)), writes=["wkb"], dma=True)
        P.op("sync", lambda e: e.dma_start(
            out=wv_b.rearrange("p (k f) -> p k f", f=D),
            in_=wv_d.rearrange("(k p) f -> p k f", p=128)), writes=["wvb"], dma=True)
        P.op("sync", lambda e: e.dma_start(
            out=wvp_b.rearrange("p (k f) -> p k f", f=D),
            in_=wvp_d.rearrange("(k p) f -> p k f", p=128)), writes=["wvpb"], dma=True)

        for b in range(BPC):
            P.op("vector", lambda e, b=b: e.tensor_scalar(
                out=colB[:, 0:1], in0=dncol[:, b:b + 1], scalar1=1e-8, scalar2=None, op0=ALU.add),
                reads=[f"dncol{b}"], writes=[f"B{b}_dn1", "colB"])
            P.op("vector", lambda e, b=b: e.reciprocal(out=colB[:, 1:2], in_=colB[:, 0:1]),
                 reads=[f"B{b}_dn1"], writes=[f"B{b}_dnr", "colB"])
            P.op("vector", lambda e, b=b: e.tensor_scalar(
                out=si_sb2[b], in0=si_sb2[b], scalar1=colB[:, 1:2], scalar2=None, op0=ALU.mult),
                reads=[f"siraw{b}", f"B{b}_dnr"], writes=[f"B{b}_si"])

        pgh, pghK = R(0, 64)
        pgi, pgiK = R(64, 128)
        pq, pqK = R(128, 192)
        ptB, ptBK = R(256, 384)
        psc, pscK = R(384, 448, p1=64)
        pvT, pvTK = R(1024, 2048, p1=64)
        pcm, pcmK = R(2048, 3072, p1=64)

        for m in range(24):
            wb = wm_b[m % 2]
            P.op("sync", lambda e, m=m, wb=wb: e.dma_start(
                out=wb.rearrange("p (k f) -> p k f", f=128),
                in_=whhT[:, m * 128:(m + 1) * 128].rearrange("(k p) f -> p k f", p=128)),
                writes=[f"wm{m % 2}"], dma=True)
            for k in range(8):
                P.op("tensor", lambda e, k=k, wb=wb: e.matmul(
                    pgh, lhsT=wb[:, k * 128:(k + 1) * 128],
                    rhs=s0T_b[:, k * NSLOT:(k + 1) * NSLOT],
                    start=(k == 0), stop=(k == 7), skip_group_check=True),
                    reads=[f"wm{m % 2}", "s0T"], writes=pghK)
            P.op("vector", lambda e, m=m: e.tensor_copy(
                gh_sb[:, m * NSLOT:(m + 1) * NSLOT], pgh),
                reads=pghK, writes=[f"gh{m}"])

        for b in range(BPC):
            kb = f"B{b}_"
            for k in range(8):
                P.op("tensor", lambda e, k=k, b=b: e.transpose(
                    ptB[:, 0:64], si_sb2[b][:, k * 128:(k + 1) * 128], ident[0:64, 0:64]),
                    reads=[kb + "si", "ident"], writes=ptBK)
                P.op("vector", lambda e, k=k: e.tensor_copy(
                    siT[:, k * NSLOT:(k + 1) * NSLOT], ptB[:, 0:64]),
                    reads=ptBK, writes=[kb + f"siT{k}"])
            siT_keys = [kb + f"siT{k}" for k in range(8)]
            for m in range(24):
                wb = wm_b[m % 2]
                P.op("sync", lambda e, m=m, wb=wb: e.dma_start(
                    out=wb.rearrange("p (k f) -> p k f", f=128),
                    in_=wihT[:, m * 128:(m + 1) * 128].rearrange("(k p) f -> p k f", p=128)),
                    writes=[f"wm{m % 2}"], dma=True)
                for k in range(8):
                    P.op("tensor", lambda e, k=k, wb=wb: e.matmul(
                        pgi, lhsT=wb[:, k * 128:(k + 1) * 128],
                        rhs=siT[:, k * NSLOT:(k + 1) * NSLOT],
                        start=(k == 0), stop=(k == 7), skip_group_check=True),
                        reads=[f"wm{m % 2}"] + siT_keys, writes=pgiK)
                P.op("vector", lambda e, m=m: e.tensor_copy(
                    gi_sb[:, m * NSLOT:(m + 1) * NSLOT], pgi),
                    reads=pgiK, writes=[kb + f"gi{m}"])
            for out_sb, base in ((r_sb, 0), (z_sb, 8)):
                for m8 in range(8):
                    m = base + m8
                    sl = slice(m8 * NSLOT, (m8 + 1) * NSLOT)
                    P.op("vector", lambda e, m=m, sl=sl, o=out_sb: e.scalar_tensor_tensor(
                        out=o[:, sl], in0=gi_sb[:, m * NSLOT:(m + 1) * NSLOT],
                        scalar=bihc[:, m:m + 1], in1=gh_sb[:, m * NSLOT:(m + 1) * NSLOT],
                        op0=ALU.add, op1=ALU.add),
                        reads=[kb + f"gi{m}", f"gh{m}", "bihc"], writes=[kb + f"g1_{m}"])
                    P.op("vector", lambda e, m=m, sl=sl, o=out_sb: e.tensor_scalar(
                        out=o[:, sl], in0=o[:, sl], scalar1=bhhc[:, m:m + 1],
                        scalar2=None, op0=ALU.add),
                        reads=[kb + f"g1_{m}", "bhhc"], writes=[kb + f"g2_{m}"])
                    P.op("scalar", lambda e, sl=sl, o=out_sb: e.activation(
                        out=o[:, sl], in_=o[:, sl], func=AF.Sigmoid),
                        reads=[kb + f"g2_{m}"], writes=[kb + f"gact{m}"])
            for m8 in range(8):
                m = 16 + m8
                sl = slice(m8 * NSLOT, (m8 + 1) * NSLOT)
                P.op("vector", lambda e, m=m, sl=sl: e.tensor_scalar(
                    out=n_sb[:, sl], in0=gh_sb[:, m * NSLOT:(m + 1) * NSLOT],
                    scalar1=bhhc[:, m:m + 1], scalar2=None, op0=ALU.add),
                    reads=[f"gh{m}", "bhhc"], writes=[kb + f"nh{m8}"])
                P.op("vector", lambda e, m8=m8, sl=sl: e.tensor_tensor(
                    out=n_sb[:, sl], in0=r_sb[:, sl], in1=n_sb[:, sl], op=ALU.mult),
                    reads=[kb + f"gact{m8}", kb + f"nh{m8}"], writes=[kb + f"nr{m8}"])
                P.op("vector", lambda e, m=m, sl=sl: e.scalar_tensor_tensor(
                    out=n_sb[:, sl], in0=gi_sb[:, m * NSLOT:(m + 1) * NSLOT],
                    scalar=bihc[:, m:m + 1], in1=n_sb[:, sl], op0=ALU.add, op1=ALU.add),
                    reads=[kb + f"gi{m}", kb + f"nr{m8}", "bihc"], writes=[kb + f"np{m8}"])
                P.op("scalar", lambda e, sl=sl: e.activation(
                    out=n_sb[:, sl], in_=n_sb[:, sl], func=AF.Tanh),
                    reads=[kb + f"np{m8}"], writes=[kb + f"n{m8}"])
                P.op("vector", lambda e, m8=m8, sl=sl: e.tensor_tensor(
                    out=snew[:, sl], in0=s0T_b[:, sl], in1=n_sb[:, sl], op=ALU.subtract),
                    reads=["s0T", kb + f"n{m8}"], writes=[kb + f"sd{m8}"])
                P.op("vector", lambda e, m8=m8, sl=sl: e.tensor_tensor(
                    out=snew[:, sl], in0=z_sb[:, sl], in1=snew[:, sl], op=ALU.mult),
                    reads=[kb + f"gact{8 + m8}", kb + f"sd{m8}"], writes=[kb + f"zs{m8}"])
                P.op("vector", lambda e, m8=m8, sl=sl: e.tensor_tensor(
                    out=snew[:, sl], in0=n_sb[:, sl], in1=snew[:, sl], op=ALU.add),
                    reads=[kb + f"n{m8}", kb + f"zs{m8}"], writes=[kb + f"sn{m8}"])
            snew_keys = [kb + f"sn{m8}" for m8 in range(8)]
            for k in range(8):
                P.op("tensor", lambda e, k=k: e.transpose(
                    ptB[0:64, 0:128], snew[:, k * NSLOT:(k + 1) * NSLOT], ident[:]),
                    reads=snew_keys + ["ident"], writes=ptBK)
                P.op("vector", lambda e, k=k: e.tensor_copy(
                    snewT[:, k * 128:(k + 1) * 128], ptB[0:64, 0:128]),
                    reads=ptBK, writes=[kb + f"snT{k}"])
            snewT_keys = [kb + f"snT{k}" for k in range(8)]
            for wb_, bi, dst, wkey, rkey in ((wq_b, 0, q_sb, "q", "wqb"),
                                             (wk_b, 2, k_sb, "k", "wkb")):
                for c in range(2):
                    for k in range(8):
                        P.op("tensor", lambda e, c=c, k=k, wb_=wb_: e.matmul(
                            pq, lhsT=wb_[:, k * DQ + c * 128: k * DQ + c * 128 + 128],
                            rhs=snew[:, k * NSLOT:(k + 1) * NSLOT],
                            start=(k == 0), stop=(k == 7), skip_group_check=True),
                            reads=[rkey] + snew_keys, writes=pqK)
                    P.op("vector", lambda e, c=c, dst=dst, bi=bi: e.tensor_scalar(
                        out=dst[:, c * NSLOT:(c + 1) * NSLOT], in0=pq,
                        scalar1=bqkc[:, bi + c:bi + c + 1], scalar2=None, op0=ALU.add),
                        reads=pqK + ["bqkc"], writes=[kb + wkey + str(c)])
            for nh in range(2):
                for k in range(8):
                    P.op("tensor", lambda e, nh=nh, k=k: e.matmul(
                        pvT[:, nh * 512:(nh + 1) * 512],
                        lhsT=snew[:, k * NSLOT:(k + 1) * NSLOT],
                        rhs=wv_b[:, k * D + nh * 512: k * D + nh * 512 + 512],
                        start=(k == 0), stop=(k == 7), skip_group_check=True),
                        reads=["wvb"] + snew_keys, writes=pvTK)
            P.op("sync", lambda e: e.dma_start(out=ggb, in_=bv_r), writes=["ggb"], dma=True)
            P.op("vector", lambda e: e.tensor_tensor(out=vT_sb, in0=pvT, in1=ggb, op=ALU.add),
                 reads=pvTK + ["ggb"], writes=[kb + "vT"])
            for c in range(2):
                P.op("tensor", lambda e, c=c: e.matmul(
                    psc, lhsT=q_sb[:, c * NSLOT:(c + 1) * NSLOT],
                    rhs=k_sb[:, c * NSLOT:(c + 1) * NSLOT],
                    start=(c == 0), stop=(c == 1), skip_group_check=True),
                    reads=[kb + "q" + str(c), kb + "k" + str(c)], writes=pscK)
            P.op("vector", lambda e: e.tensor_scalar(
                out=sc_sb, in0=psc, scalar1=DQ ** -0.5, scalar2=None, op0=ALU.mult),
                reads=pscK, writes=[kb + "sc"])
            P.op("vector", lambda e: e.reduce_max(out=colB[:, 2:3], in_=sc_sb,
                                                  axis=AX.X, negate=True),
                 reads=[kb + "sc"], writes=[kb + "scm", "colB"])
            P.op("scalar", lambda e: e.activation(out=sc_sb, in_=sc_sb, func=AF.Exp,
                                                  bias=colB[:, 2:3]),
                 reads=[kb + "sc", kb + "scm"], writes=[kb + "sce"])
            P.op("vector", lambda e: e.reduce_sum(out=colB[:, 3:4], in_=sc_sb, axis=AX.X),
                 reads=[kb + "sce"], writes=[kb + "scs", "colB"])
            P.op("vector", lambda e: e.reciprocal(out=colB[:, 4:5], in_=colB[:, 3:4]),
                 reads=[kb + "scs"], writes=[kb + "scz", "colB"])
            P.op("vector", lambda e: e.tensor_scalar(
                out=sc_sb, in0=sc_sb, scalar1=colB[:, 4:5], scalar2=None, op0=ALU.mult),
                reads=[kb + "sce", kb + "scz"], writes=[kb + "attn"])
            P.op("tensor", lambda e: e.transpose(
                ptB[0:64, 0:64], sc_sb, ident[0:64, 0:64]),
                reads=[kb + "attn", "ident"], writes=ptBK)
            P.op("vector", lambda e: e.tensor_copy(at_sb, ptB[0:64, 0:64]),
                 reads=ptBK, writes=[kb + "attnT"])
            for nh in range(2):
                P.op("tensor", lambda e, nh=nh: e.matmul(
                    pcm[:, nh * 512:(nh + 1) * 512], lhsT=at_sb,
                    rhs=vT_sb[:, nh * 512:(nh + 1) * 512],
                    start=True, stop=True, skip_group_check=True),
                    reads=[kb + "attnT", kb + "vT"], writes=pcmK)
            P.op("vector", lambda e: e.tensor_tensor(out=y_sb, in0=pcm, in1=snewT, op=ALU.add),
                 reads=pcmK + snewT_keys, writes=[kb + "y"])
            P.op("vector", lambda e: e.reduce_sum(out=colB[:, 0:1], in_=y_sb, axis=AX.X),
                 reads=[kb + "y"], writes=[kb + "ys", "colB"])
            P.op("vector", lambda e: e.tensor_tensor(out=scrB, in0=y_sb, in1=y_sb,
                                                     op=ALU.mult),
                 reads=[kb + "y"], writes=["scrB"])
            P.op("vector", lambda e: e.reduce_sum(out=colB[:, 1:2], in_=scrB, axis=AX.X),
                 reads=["scrB"], writes=[kb + "yq", "colB"])
            P.op("vector", lambda e: e.tensor_scalar(out=colB[:, 0:1], in0=colB[:, 0:1],
                                                     scalar1=1.0 / D, scalar2=None, op0=ALU.mult),
                 reads=[kb + "ys"], writes=[kb + "ym", "colB"])
            P.op("vector", lambda e: e.tensor_tensor(out=colB[:, 2:3], in0=colB[:, 0:1],
                                                     in1=colB[:, 0:1], op=ALU.mult),
                 reads=[kb + "ym"], writes=[kb + "ym2", "colB"])
            P.op("vector", lambda e: e.tensor_scalar(out=colB[:, 1:2], in0=colB[:, 1:2],
                                                     scalar1=1.0 / D, scalar2=None, op0=ALU.mult),
                 reads=[kb + "yq"], writes=[kb + "yq2", "colB"])
            P.op("vector", lambda e: e.tensor_tensor(out=colB[:, 1:2], in0=colB[:, 1:2],
                                                     in1=colB[:, 2:3], op=ALU.subtract),
                 reads=[kb + "yq2", kb + "ym2"], writes=[kb + "yv", "colB"])
            P.op("scalar", lambda e: e.activation(out=colB[:, 1:2], in_=colB[:, 1:2],
                                                  func=AF.Ln, bias=konst[0:64, 0:1]),
                 reads=[kb + "yv", "konst"], writes=[kb + "yl", "colB"])
            P.op("scalar", lambda e: e.activation(out=colB[:, 1:2], in_=colB[:, 1:2],
                                                  func=AF.Exp, scale=konst[0:64, 1:2]),
                 reads=[kb + "yl", "konst"], writes=[kb + "yi", "colB"])
            P.op("vector", lambda e: e.tensor_tensor(out=colB[:, 2:3], in0=colB[:, 0:1],
                                                     in1=colB[:, 1:2], op=ALU.mult),
                 reads=[kb + "ym", kb + "yi"], writes=[kb + "ymi", "colB"])
            P.op("vector", lambda e: e.tensor_scalar(
                out=yn_sb, in0=y_sb, scalar1=colB[:, 1:2], scalar2=colB[:, 2:3],
                op0=ALU.mult, op1=ALU.subtract),
                reads=[kb + "y", kb + "yi", kb + "ymi"], writes=[kb + "yn0"])
            P.op("sync", lambda e: e.dma_start(out=ggb, in_=lng_r), writes=["ggb"], dma=True)
            P.op("vector", lambda e: e.tensor_tensor(out=yn_sb, in0=yn_sb, in1=ggb, op=ALU.mult),
                 reads=[kb + "yn0", "ggb"], writes=[kb + "yn1"])
            P.op("sync", lambda e: e.dma_start(out=ggb, in_=lnb_r), writes=["ggb"], dma=True)
            P.op("vector", lambda e: e.tensor_tensor(out=yn_sb, in0=yn_sb, in1=ggb, op=ALU.add),
                 reads=[kb + "yn1", "ggb"], writes=[kb + "yn"])
            for k in range(8):
                P.op("tensor", lambda e, k=k: e.transpose(
                    ptB[:, 0:64], yn_sb[:, k * 128:(k + 1) * 128], ident[0:64, 0:64]),
                    reads=[kb + "yn", "ident"], writes=ptBK)
                P.op("vector", lambda e: e.tensor_copy(ynT_sb, ptB[:, 0:64]),
                     reads=ptBK, writes=[kb + "ynT"])
                for nh in range(2):
                    P.op("tensor", lambda e, k=k, nh=nh: e.matmul(
                        pvT[:, nh * 512:(nh + 1) * 512], lhsT=ynT_sb,
                        rhs=wvp_b[:, k * D + nh * 512: k * D + nh * 512 + 512],
                        start=(k == 0), stop=(k == 7), skip_group_check=True),
                        reads=[kb + "ynT", "wvpb"], writes=pvTK)
            P.op("sync", lambda e: e.dma_start(out=ggb, in_=bvp_r), writes=["ggb"], dma=True)
            P.op("vector", lambda e, b=b: e.tensor_tensor(
                out=sv_sb[b], in0=pvT, in1=ggb, op=ALU.add),
                reads=pvTK + ["ggb"], writes=[f"sv{b}"])
        sbp.pop()
        P.barrier()
        if "C" not in PHASES:
            if DBG_DUMP is not None:
                c0, c1 = DBG_DUMP
                dbg_t = dt("dbg", [128, c1 - c0], F32, kind="ExternalOutput").ap()
                P.barrier()
                P.op("sync", lambda e: e.dma_start(out=dbg_t, in_=sbp.raw(0, 128, c0, c1)),
                     dma=True, qsem="dbgdump")
            P.emit(blk_ctx, mksem, final_eng="sync")
            return nc

        sbp.push()
        xC = sbp.alloc(2 * D)
        lncolC = sbp.alloc(8)
        xnTC = sbp.alloc(8 * TB)
        wing = [sbp.alloc(8 * 128) for _ in range(2)]
        u_raw = sbp.alloc(NFC * SEG)
        uconv = sbp.alloc(NFC * TB)
        go_sl = [sbp.alloc(NFC * TB) for _ in range(2)]
        delta = sbp.alloc(NFC * TB)
        du_sl = sbp.alloc(NFC * TB)
        y_sl = sbp.alloc(NFC * TB)
        g_sb2 = sbp.alloc(TB, parts=96)
        _sio = si_sb2[0].offset
        _sio1 = si_sb2[1].offset
        brep = [sbp.raw(0, 128, _sio + i * TB, _sio + (i + 1) * TB) for i in range(2)]
        crep = [sbp.raw(0, 128, _sio + (2 + i) * TB, _sio + (3 + i) * TB) for i in range(2)]
        fbuf = [sbp.raw(0, 128, _sio1 + i * TB, _sio1 + (i + 1) * TB) for i in range(4)]
        dtmp = fbuf[0]
        gslab = sbp.alloc(NFC * TB)
        hslab = sbp.alloc(8 * TB)
        mslab = sbp.alloc(8 * TB)
        fin_d = mslab
        woutb = None
        wob = None
        aC = sbp.alloc(2 * NSLOT)
        f0T_sb = delta[0:128, 0:2048]
        aT_sb = sbp.alloc(2 * 128, parts=64)

        ps_uo = [R(0, 256), R(512, 768), R(256, 512), R(768, 1024)]
        ps_g = R(1024, 1280, p1=96)
        ps_d = [R(1280, 1536), R(1536, 1792), ps_uo[0], ps_uo[1]]
        ps_f0 = [R(1024, 1280), R(1536, 1792), R(1280, 1536), R(1792, 2048)]
        ps_tpx = [R(0, 128), R(512, 640)]
        ps_tpa = [R(1280, 1408, p1=64), R(1792, 1920, p1=64)]
        ps_fin = [[R(2048 + (sub * 2 + nh) * 512, 2048 + (sub * 2 + nh + 1) * 512)
                   for nh in range(2)] for sub in range(2)]

        def seg(slab, fc, w=TB):
            return slab[:, fc * w:(fc + 1) * w]

        u3d = u_raw.rearrange("p (c t) -> p c t", t=SEG)
        wslots = [(wing[0], "wing0"), (wing[1], "wing1"),
                  (xC[:, 0:D], "xCa"), (xC[:, D:2 * D], "xCb")]

        def stage_front(b, blk):
            t0 = blk * TB
            key = f"C{b}_{blk}"
            P.op("sync", lambda e: e.dma_start(
                out=xC.rearrange("p (s d) -> p s d", d=D),
                in_=x_in[b, t0:t0 + TB, :].rearrange("(s p) d -> p s d", p=128)),
                writes=["xCa", "xCb"], dma=True, tag=key + "xload")
            lnscr = u_raw[:, 2048:3072]
            for sub in range(2):
                xs = xC[:, sub * D:(sub + 1) * D]
                xn = u_raw[:, sub * D:(sub + 1) * D]
                layernorm(xs, xn, lncolC, lnscr, key + f"ln{sub}", "xCa" if sub == 0 else "xCb", "C", xtra=("u_raw",))
                for k in range(8):
                    tp, tpK = ps_tpx[k % 2]
                    P.op("tensor", lambda e, k=k, xn=xn, tp=tp: e.transpose(
                        tp, xn[:, k * 128:(k + 1) * 128], ident[:]),
                        reads=[key + f"ln{sub}xn", "u_raw", "ident"], writes=tpK)
                    P.op("vector", lambda e, k=k, sub=sub, tp=tp: e.tensor_copy(
                        xnTC[:, k * TB + sub * 128: k * TB + sub * 128 + 128], tp),
                        reads=tpK, writes=[f"CxnT_{k}_{sub}"])
            if blk == 0 and b == 0:
                P.op("vector", lambda e: e.memset(u3d[:, :, 0:3], 0.0),
                     reads=["u_raw"], writes=["u_raw"], tag=key + "hist0")
            elif blk == 0:
                P.op("vector", lambda e: e.memset(u3d[:, :, 0:3], 0.0),
                     reads=["u_raw"], writes=["u_raw"], tag=key + "hist0")
            else:
                P.op("vector", lambda e: e.tensor_copy(
                    u3d[:, :, 0:3], hist_t.rearrange("p (c t) -> p c t", t=3)),
                    reads=["u_raw", "hist_t"], writes=["u_raw"], tag=key + "hist")

        def stage_mid_dma(b, blk, grp):
            key = f"C{b}_{blk}"
            wb, wbK = wslots[grp % 4]
            P.op("sync", lambda e, grp=grp, wb=wb: e.dma_start(
                out=wb, in_=w_inT[grp]),
                writes=[wbK], dma=True, tag=key + "windma")

        def stage_mid_mm(b, blk, par, grp):
            key = f"C{b}_{blk}"
            xnT_keys = [f"CxnT_{k}_{s}" for k in range(8) for s in range(2)]
            wb, wbK = wslots[grp % 4]
            pm, pmK = ps_uo[grp % 4]
            for k in range(8):
                P.op("tensor", lambda e, k=k, wb=wb, pm=pm: e.matmul(
                    pm, lhsT=wb[:, k * 128:(k + 1) * 128],
                    rhs=xnTC[:, k * TB:(k + 1) * TB],
                    start=(k == 0), stop=(k == 7), skip_group_check=True),
                    reads=[wbK] + xnT_keys, writes=pmK,
                    tag=key + "uomm")
            fc = grp % 16
            if grp < 16:
                P.op("vector", lambda e, fc=fc, pm=pm: e.tensor_copy(
                    u3d[:, fc, 3:3 + TB], pm),
                    reads=pmK, writes=["u_raw"], tag=key + "uevac")
            else:
                P.op("scalar", lambda e, fc=fc, pm=pm, par=par: e.activation(
                    out=seg(go_sl[par], fc), in_=pm, func=AF.Copy),
                    reads=pmK, writes=[f"go_{par}_{fc}"], tag=key + "goevac")

        def stage_mid_grp(b, blk, par, grps):
            grps = list(grps)
            for j, grp in enumerate(grps):
                if j == 0:
                    stage_mid_dma(b, blk, grp)
                    if j + 1 < len(grps):
                        stage_mid_dma(b, blk, grps[j + 1])
                if j + 2 < len(grps):
                    stage_mid_dma(b, blk, grps[j + 2])
                stage_mid_mm(b, blk, par, grp)

        def stage_mid_post(b, blk):
            key = f"C{b}_{blk}"
            for q in range(NFC // 2):
                for j in range(4):
                    for fc in (2 * q, 2 * q + 1):
                        base = fc * SEG
                        srcv = u_raw[:, base + j:base + j + TB]
                        if j == 0:
                            P.op("vector", lambda e, srcv=srcv, fc=fc: e.tensor_scalar(
                                out=seg(uconv, fc), in0=srcv, scalar1=cwc[:, fc * 4:fc * 4 + 1],
                                scalar2=None, op0=ALU.mult),
                                reads=["u_raw", "cwc"], writes=[f"uc_{fc}"], tag=key + "conv")
                        else:
                            P.op("vector", lambda e, srcv=srcv, fc=fc, j=j: e.scalar_tensor_tensor(
                                out=seg(uconv, fc), in0=srcv, scalar=cwc[:, fc * 4 + j:fc * 4 + j + 1],
                                in1=seg(uconv, fc), op0=ALU.mult, op1=ALU.add),
                                reads=["u_raw", "cwc", f"uc_{fc}"], writes=[f"uc_{fc}"],
                                tag=key + "conv")
                for fc in (2 * q, 2 * q + 1):
                    P.op("scalar", lambda e, fc=fc: e.activation(
                        out=seg(uconv, fc), in_=seg(uconv, fc), func=AF.Silu,
                        bias=cbc[:, fc:fc + 1]),
                        reads=[f"uc_{fc}", "cbc"], writes=[f"uc_{fc}"], tag=key + "u2silu")
            P.op("vector", lambda e: e.tensor_copy(
                hist_t.rearrange("p (c t) -> p c t", t=3), u3d[:, :, TB:TB + 3]),
                reads=["u_raw"], writes=["hist_t"], tag=key + "tails")

        def gate_head(b, blk):
            key = f"C{b}_{blk}"
            pg, pgK = ps_g
            for k in range(NFC):
                P.op("tensor", lambda e, k=k, pg=pg: e.matmul(
                    pg, lhsT=wgT_b[:, k * 96:(k + 1) * 96], rhs=seg(uconv, k),
                    start=(k == 0), stop=(k == NFC - 1), skip_group_check=True),
                    reads=["wgT", f"uc_{k}"], writes=pgK, tag=key + "gmm")
            P.op("vector", lambda e, pg=pg: e.tensor_copy(g_sb2, pg),
                 reads=pgK, writes=["gsb"], tag=key + "gevac")

        def gate_dm(b, blk, m):
            key = f"C{b}_{blk}"
            pd, pdK = ps_d[m % 4]
            P.op("tensor", lambda e, m=m, pd=pd: e.matmul(
                pd, lhsT=wdT_b[:, m * 128:(m + 1) * 128], rhs=g_sb2[0:64, :],
                start=True, stop=True, skip_group_check=True),
                reads=["wdT", "gsb"], writes=pdK, tag=key + "dmm")
            P.op("scalar", lambda e, m=m, pd=pd: e.activation(
                out=dtmp, in_=pd, func=AF.Exp, bias=dbc[:, m:m + 1]),
                reads=pdK + ["dbc"], writes=["dtmp", "fb0"], tag=key + "dexp")
            P.op("scalar", lambda e, m=m: e.activation(
                out=seg(delta, m), in_=dtmp, func=AF.Ln, bias=1.0),
                reads=["dtmp", "fb0"], writes=[f"dl_{m}"], tag=key + "dln")

        def gate_du(b, blk):
            key = f"C{b}_{blk}"
            uc_keys = [f"uc_{fc}" for fc in range(NFC)]
            P.op("vector", lambda e: e.tensor_tensor(out=du_sl, in0=delta, in1=uconv,
                                                     op=ALU.mult),
                 reads=[f"dl_{m}" for m in range(NFC)] + uc_keys,
                 writes=["du"], tag=key + "du")

        def stage_gate(b, blk):
            gate_head(b, blk)
            for m in range(NFC):
                gate_dm(b, blk, m)
            gate_du(b, blk)

        def stage_scan(b, blk, par, midnxt):
            t0s = blk * TB
            key = f"C{b}_{blk}"
            P.op("sync", lambda e, t0s=t0s, b=b: e.dma_start(
                out=aC.rearrange("p (s a) -> p s a", a=NSLOT),
                in_=ascr[b, t0s:t0s + TB, :].rearrange("(s p) a -> p s a", p=128)),
                writes=["aC"], dma=True, tag=key + "aload")

            def embrep(s):
                br, cr = brep[s % 2], crep[s % 2]
                P.op("sync", lambda e, s=s, br=br: e.dma_start(
                    out=br, in_=g_sb2[64 + s:65 + s, :].unsqueeze(1).broadcast_to((1, 128, TB))),
                    reads=["gsb"], writes=[f"brep{s % 2}"], dma=True, tag=key + "brep")
                P.op("sync", lambda e, s=s, cr=cr: e.dma_start(
                    out=cr, in_=g_sb2[80 + s:81 + s, :].unsqueeze(1).broadcast_to((1, 128, TB))),
                    reads=["gsb"], writes=[f"crep{s % 2}"], dma=True, tag=key + "crep")

            def emgbuild(s, gq, eng):
                br = brep[s % 2]
                gsl = gslab[:, gq * 4 * TB:(gq + 1) * 4 * TB]
                P.op(eng, lambda e, br=br, gsl=gsl: e.tensor_tensor(
                    out=gsl.rearrange("p (c t) -> p c t", t=TB),
                    in0=du_sl[:, gq * 4 * TB:(gq + 1) * 4 * TB].rearrange("p (c t) -> p c t", t=TB),
                    in1=br.unsqueeze(1).broadcast_to((128, 4, TB)), op=ALU.mult),
                    reads=["du", f"brep{s % 2}"], writes=[f"gsl{gq}"],
                    tag=key + "gbuild")

            embrep(0)
            for gq in range(4):
                emgbuild(0, gq, "vector")
            for s in range(NS):
                if s + 1 < NS:
                    embrep(s + 1)
                for half in range(2):
                    for fc8 in range(8):
                        fc = half * 8 + fc8
                        fb = fbuf[fc % 4]
                        P.op("scalar", lambda e, s=s, fc=fc, fb=fb: e.activation(
                            out=fb, in_=seg(delta, fc), func=AF.Exp,
                            scale=acol[:, fc * NS + s:fc * NS + s + 1]),
                            reads=[f"dl_{fc}", "acol"], writes=[f"fb{fc % 4}"],
                            tag=key + "fexp")
                        P.op("vector", lambda e, s=s, fc=fc, fc8=fc8, fb=fb, b=b: e.tensor_tensor_scan(
                            out=seg(hslab, fc8), data0=fb, data1=seg(gslab, fc),
                            initial=carry[b][:, s * NFC + fc:s * NFC + fc + 1],
                            op0=ALU.mult, op1=ALU.add),
                            reads=[f"fb{fc % 4}", f"gsl{fc // 4}", f"ca_{b}_{s}_{fc}"],
                            writes=[f"h_{fc8}"], tag=key + "scan")
                        if s + 1 < NS and fc8 == 3:
                            emgbuild(s + 1, half * 2, "vector")
                        if s + 1 < NS and fc8 == 7:
                            emgbuild(s + 1, half * 2 + 1, "vector")
                    P.op("vector", lambda e, s=s, b=b, half=half: e.tensor_copy(
                        carry[b][:, s * NFC + half * 8:s * NFC + half * 8 + 8],
                        hslab.rearrange("p (c t) -> p c t", t=TB)[:, :, TB - 1]),
                        reads=[f"h_{j}" for j in range(8)],
                        writes=[f"ca_{b}_{s}_{half * 8 + j}" for j in range(8)],
                        tag=key + "carry")
                    if midnxt is not None:
                        g = s * 2 + half
                        if g == 0:
                            stage_mid_dma(midnxt[0], midnxt[1], 0)
                            stage_mid_dma(midnxt[0], midnxt[1], 1)
                        if g + 2 < 32:
                            stage_mid_dma(midnxt[0], midnxt[1], g + 2)
                        stage_mid_mm(midnxt[0], midnxt[1], midnxt[2], g)
                    for yq in range(2):
                        q0 = half * 8 * TB + yq * 4 * TB
                        ysl_q = y_sl[:, q0:q0 + 4 * TB]
                        h_q = hslab[:, yq * 4 * TB:(yq + 1) * 4 * TB]
                        m_q = mslab[:, yq * 4 * TB:(yq + 1) * 4 * TB]
                        hkeys = [f"h_{yq * 4 + j}" for j in range(4)]
                        if s == 0:
                            P.op("vector", lambda e, cr=crep[s % 2], ysl_q=ysl_q, h_q=h_q: e.tensor_tensor(
                                out=ysl_q.rearrange("p (c t) -> p c t", t=TB),
                                in0=h_q.rearrange("p (c t) -> p c t", t=TB),
                                in1=cr.unsqueeze(1).broadcast_to((128, 4, TB)), op=ALU.mult),
                                reads=hkeys + [f"crep{s % 2}"],
                                writes=["y_sl"], tag=key + "ymul")
                        else:
                            P.op("vector", lambda e, cr=crep[s % 2], h_q=h_q, m_q=m_q: e.tensor_tensor(
                                out=m_q.rearrange("p (c t) -> p c t", t=TB),
                                in0=h_q.rearrange("p (c t) -> p c t", t=TB),
                                in1=cr.unsqueeze(1).broadcast_to((128, 4, TB)), op=ALU.mult),
                                reads=hkeys + [f"crep{s % 2}"],
                                writes=["mslab"], tag=key + "ymul")
                            P.op("vector", lambda e, ysl_q=ysl_q, m_q=m_q: e.tensor_tensor(
                                out=ysl_q, in0=ysl_q, in1=m_q, op=ALU.add),
                                reads=["y_sl", "mslab"], writes=["y_sl"], tag=key + "yadd")
            for fc in range(NFC):
                P.op("scalar", lambda e, fc=fc, par=par: e.activation(
                    out=seg(go_sl[par], fc), in_=seg(go_sl[par], fc), func=AF.Silu),
                    reads=[f"go_{par}_{fc}"], writes=[f"go_{par}_{fc}"], tag=key + "gosilu")
            for fc in range(NFC):
                P.op("vector", lambda e, fc=fc: e.scalar_tensor_tensor(
                    out=seg(mslab, fc % 8), in0=seg(uconv, fc), scalar=dcolv[:, fc:fc + 1],
                    in1=seg(y_sl, fc), op0=ALU.mult, op1=ALU.add),
                    reads=[f"uc_{fc}", "y_sl", "dcolv"], writes=[f"t2_{fc}", "mslab"],
                    tag=key + "du2")
                P.op("vector", lambda e, fc=fc, par=par: e.tensor_tensor(
                    out=seg(y_sl, fc), in0=seg(mslab, fc % 8), in1=seg(go_sl[par], fc), op=ALU.mult),
                    reads=[f"t2_{fc}", f"go_{par}_{fc}"], writes=["y_sl", f"y2_{fc}"],
                    tag=key + "y2")

        def stage_tail_a(b, blk):
            t0 = blk * TB
            key = f"C{b}_{blk}"
            for sub in range(2):
                for nh in range(2):
                    pf, pfK = ps_fin[sub][nh]
                    P.op("tensor", lambda e, sub=sub, nh=nh, pf=pf: e.matmul(
                        pf, lhsT=ones[64:65, 0:128],
                        rhs=bo_row[:, nh * 512:(nh + 1) * 512],
                        start=True, stop=False, skip_group_check=True),
                        reads=["ones", "bo_row"], writes=pfK, tag=key + "bomm")
            for sub in range(2):
                pa, paK = ps_tpa[sub]
                P.op("tensor", lambda e, sub=sub, pa=pa: e.transpose(
                    pa, aC[:, sub * NSLOT:(sub + 1) * NSLOT], ident[:]),
                    reads=["aC", "ident"], writes=paK)
                P.op("vector", lambda e, sub=sub, pa=pa: e.tensor_copy(
                    aT_sb[:, sub * 128:(sub + 1) * 128], pa),
                    reads=paK, writes=[f"aT{sub}"])
            wslots6 = wslots + [(gslab[:, 0:1024], "gsl0"), (gslab[:, 1024:2048], "gsl1")]
            for c in range(NFC):
                wb, wbK2 = wslots6[c % 6]
                P.op("sync", lambda e, c=c, wb=wb: e.dma_start(
                    out=wb, in_=w_outT[c * 128:(c + 1) * 128, :]),
                    writes=[wbK2], dma=True, tag=key + "woutdma")
                for sub in range(2):
                    for nh in range(2):
                        pf, pfK = ps_fin[sub][nh]
                        P.op("tensor", lambda e, c=c, sub=sub, nh=nh, wb=wb, pf=pf: e.matmul(
                            pf, lhsT=seg(y_sl, c)[:, sub * 128:sub * 128 + 128],
                            rhs=wb[:, nh * 512:(nh + 1) * 512],
                            start=False, stop=False, skip_group_check=True),
                            reads=[wbK2, f"y2_{c}", "y_sl"], writes=pfK,
                            tag=key + "woutmm")
        def stage_tail_b(b, blk, gn):
            t0 = blk * TB
            key = f"C{b}_{blk}"
            for m in range(8):
                pf0, pf0K = ps_f0[m % 4]
                P.op("tensor", lambda e, m=m, b=b, pf0=pf0: e.matmul(
                    pf0, lhsT=sv_sb[b][:, m * 128:(m + 1) * 128], rhs=aT_sb,
                    start=True, stop=True, skip_group_check=True),
                    reads=[f"sv{b}", "aT0", "aT1"], writes=pf0K, tag=key + "f0mm")
                P.op("vector", lambda e, m=m, pf0=pf0: e.tensor_copy(
                    f0T_sb[:, m * TB:(m + 1) * TB], pf0),
                    reads=pf0K, writes=[f"dl_{m}"])
            if gn is not None:
                gate_head(*gn)
            for k in range(8):
                wb, wbK = wslots[k % 4]
                P.op("sync", lambda e, k=k, wb=wb: e.dma_start(
                    out=wb, in_=wo_d[k * 128:(k + 1) * 128, :]),
                    writes=[wbK], dma=True, tag=key + "wodma")
                for sub in range(2):
                    for nh in range(2):
                        pf, pfK = ps_fin[sub][nh]
                        P.op("tensor", lambda e, k=k, sub=sub, nh=nh, wb=wb, pf=pf: e.matmul(
                            pf, lhsT=f0T_sb[:, k * TB + sub * 128: k * TB + sub * 128 + 128],
                            rhs=wb[:, nh * 512:(nh + 1) * 512],
                            start=False, stop=(k == 7), skip_group_check=True),
                            reads=[wbK, f"dl_{k}"], writes=pfK,
                            tag=key + "womm")
                if gn is not None:
                    gate_dm(*gn, k)
                    gate_dm(*gn, 8 + k)
            for sub in range(2):
                for nh in range(2):
                    pf, pfK = ps_fin[sub][nh]
                    idx = sub * 2 + nh
                    dst = fin_d[:, idx * 512:(idx + 1) * 512]
                    P.op("vector", lambda e, pf=pf, dst=dst: e.tensor_copy(dst, pf),
                         reads=pfK, writes=["finP", "mslab"], tag=key + "finevac")
            P.op("gpsimd", lambda e, b=b, t0=t0: e.dma_start(
                out=out_hbm[b, t0:t0 + 128, :],
                in_=fin_d[:, 0:1024], accum_op=ALU.add),
                reads=["finP", "mslab"], dma=True, tag=key + "acc0", qsem="acc0")
            P.op("gpsimd", lambda e, b=b, t0=t0: e.dma_start(
                out=out_hbm[b, t0 + 128:t0 + 256, :],
                in_=fin_d[:, 1024:2048], accum_op=ALU.add),
                reads=["finP", "mslab"], dma=True, tag=key + "acc1", qsem="acc1")
            if gn is not None:
                gate_du(*gn)

        nblk_c = LIMIT_BLOCKS or NBLK
        for b in range(BPC):
            P.op("vector", lambda e, b=b: e.memset(carry[b][:], 0.0),
                 writes=[f"ca_{b}_{s}_{fc}" for s in range(NS) for fc in range(NFC)])
        seqC = [(b, blk) for b in range(BPC) for blk in range(nblk_c)]
        stage_front(*seqC[0])
        stage_mid_grp(*seqC[0], 0, range(32))
        stage_mid_post(*seqC[0])
        stage_gate(*seqC[0])
        for i, cur in enumerate(seqC):
            nxt = seqC[i + 1] if i + 1 < len(seqC) else None
            mn = (nxt + ((i + 1) % 2,)) if nxt else None
            if nxt:
                stage_front(*nxt)
            stage_scan(*cur, i % 2, mn)
            stage_tail_a(*cur)
            if nxt:
                stage_mid_post(*nxt)
            stage_tail_b(*cur, nxt)
        sbp.pop()

        if DBG_DUMP is not None:
            c0, c1 = DBG_DUMP
            dbg_t = dt("dbg", [128, c1 - c0], F32, kind="ExternalOutput").ap()
            P.barrier()
            P.op("sync", lambda e: e.dma_start(out=dbg_t, in_=sbp.raw(0, 128, c0, c1)),
                 dma=True, qsem="dbgdump")
        P.emit(blk_ctx, mksem, final_eng="sync")
    return nc


_NC_CACHE = {}


def _get_nc():
    if "nc" not in _NC_CACHE:
        _NC_CACHE["nc"] = build_program()
    return _NC_CACHE["nc"]


def kernel(**inp):
    def c(a):
        return np.ascontiguousarray(np.asarray(a, dtype=np.float32))

    x = c(inp["L0_attn_x"])

    def cols(v, n):
        return np.ascontiguousarray(v.reshape(n, 128).T)

    W_in = c(inp["L0_attn_W_in"]); conv_w = c(inp["L0_attn_conv_w"])
    conv_b = c(inp["L0_attn_conv_b"]); W_gate = c(inp["L0_attn_W_gate"])
    W_delta = c(inp["L0_attn_W_delta"]); delta_bias = c(inp["L0_attn_delta_bias"])
    A_log = c(inp["L0_attn_A_log"]); Dv = c(inp["L0_attn_D"])
    W_out = c(inp["L0_attn_W_out"]); slot_init = c(inp["L0_ffn_slot_init"])
    w1 = c(inp["L0_ffn_w1"]); b1 = c(inp["L0_ffn_b1"])
    w2 = c(inp["L0_ffn_w2"]); b2 = c(inp["L0_ffn_b2"])
    tau = c(inp["L0_ffn_tau"]); Wih = c(inp["L0_ffn_Wih"]); Whh = c(inp["L0_ffn_Whh"])
    bih = c(inp["L0_ffn_bih"]); bhh = c(inp["L0_ffn_bhh"])
    wq = c(inp["L0_ffn_wq"]); bq = c(inp["L0_ffn_bq"])
    wk = c(inp["L0_ffn_wk"]); bk = c(inp["L0_ffn_bk"])
    wv = c(inp["L0_ffn_wv"]); bv = c(inp["L0_ffn_bv"])
    ln_g = c(inp["L0_ffn_ln_g"]); ln_b = c(inp["L0_ffn_ln_b"])
    wvp = c(inp["L0_ffn_wvp"]); bvp = c(inp["L0_ffn_bvp"])
    wo = c(inp["L0_ffn_wo"]); bo = c(inp["L0_ffn_bo"])

    common = {
        "w_inT": np.ascontiguousarray(W_in.T.reshape(8, 128, 32, 128).transpose(2, 1, 0, 3).reshape(32, 128, 1024)),
        "conv_w_c": np.ascontiguousarray(conv_w[:, 0, :].reshape(NFC, 128, 4)
                                         .transpose(1, 0, 2).reshape(128, NFC * 4)),
        "conv_b_c": cols(conv_b, NFC),
        "w_gateT": np.ascontiguousarray(W_gate.T),
        "w_deltaT": np.ascontiguousarray(W_delta.T),
        "dbias_c": cols(delta_bias, NFC),
        "alog_c": np.ascontiguousarray(A_log.reshape(NFC, 128, NS)
                                       .transpose(1, 0, 2).reshape(128, NFC * NS)),
        "dvec_c": cols(Dv, NFC),
        "w_outT": np.ascontiguousarray(W_out.T),
        "slot_init": slot_init,
        "slot_initT": np.ascontiguousarray(slot_init.T),
        "w1a": np.ascontiguousarray(w1[:D]),
        "w1b": np.ascontiguousarray(w1[D:]),
        "b1_c": cols(b1, 4),
        "w2": w2,
        "b2_c": np.ascontiguousarray(b2.reshape(NSLOT, 1)),
        "tau": np.ascontiguousarray(tau.reshape(1, 1)),
        "wihT": np.ascontiguousarray(Wih.T),
        "whhT": np.ascontiguousarray(Whh.T),
        "bih_c": cols(bih, 24),
        "bhh_c": cols(bhh, 24),
        "wq": wq, "wk": wk,
        "bqk_c": np.ascontiguousarray(np.concatenate([cols(bq, 2), cols(bk, 2)], axis=1)),
        "wv": wv,
        "bv_r": np.ascontiguousarray(np.broadcast_to(bv, (NSLOT, D))),
        "lng_r": np.ascontiguousarray(np.broadcast_to(ln_g, (NSLOT, D))),
        "lnb_r": np.ascontiguousarray(np.broadcast_to(ln_b, (NSLOT, D))),
        "wvp": wvp,
        "bvp_r": np.ascontiguousarray(np.broadcast_to(bvp, (NSLOT, D))),
        "wo": wo,
        "bo_r": np.ascontiguousarray(bo.reshape(1, D)),
    }

    nc = _get_nc()
    ncore = NCORE if N_CORES_OVERRIDE is None else N_CORES_OVERRIDE
    in_maps = []
    for core in range(ncore):
        m = dict(common)
        m["x"] = np.ascontiguousarray(x[core * BPC:(core + 1) * BPC])
        in_maps.append(m)
    res = run_bass_kernel_spmd(nc, in_maps, core_ids=list(range(ncore)))
    return np.concatenate([r["out"] for r in res.results], axis=0)


if __name__ == "__main__":
    build_program()
    print("built ok")


# version 16 (final)
# speedup vs baseline: 1.3578x; 1.0069x over previous
import numpy as np
import concourse.bass as bass
import concourse.mybir as mybir
from concourse.bass_utils import run_bass_kernel_spmd
from concourse.alu_op_type import AluOpType as ALU

F32 = mybir.dt.float32
AF = mybir.ActivationFunctionType
AX = mybir.AxisListType

B, T, D = 16, 4096, 1024
HID = 2048
NS = 16
NSLOT, TOPK = 64, 8
DQ, DH = 256, 512
NCORE = 8
BPC = B // NCORE
TB = 256
NBLK = T // TB
NFC = HID // 128
SEG = TB + 4


class Prog:
    ENGS = ("sync", "tensor", "vector", "scalar", "gpsimd")

    def __init__(self):
        self.ops = []
        self.deps = []
        self.writer = {}
        self.readers = {}
        self.fence = set()

    def op(self, eng, builder, reads=(), writes=(), dma=False, tag="", qsem=None):
        i = len(self.ops)
        if dma and qsem is None:
            qsem = writes[0] if writes else (tag or "dma")
        rb = [k for k in reads if k.startswith("PSB")]
        if rb:
            reads = [k for k in reads if not k.startswith("PSB")]
            writes = list(writes) + rb
        dep = set(self.fence)
        for k in reads:
            w = self.writer.get(k)
            if w is not None:
                dep.add(w)
        for k in writes:
            for r in self.readers.get(k, ()):
                dep.add(r)
            w = self.writer.get(k)
            if w is not None:
                dep.add(w)
        dep.discard(i)
        self.ops.append((eng, builder, dma, tag, qsem))
        self.deps.append(dep)
        for k in reads:
            self.readers.setdefault(k, []).append(i)
        for k in writes:
            self.writer[k] = i
            self.readers[k] = []
        return i

    def barrier(self):
        last = {}
        for i, (eng, _b, isdma, _t, qs) in enumerate(self.ops):
            last[(eng, isdma, qs)] = i
        self.fence = set(last.values())

    def emit(self, block, mksem, final_eng="sync"):
        if TRUNC_OPS is not None:
            self.ops = self.ops[:TRUNC_OPS]
            self.deps = [set(x for x in d if x < TRUNC_OPS)
                         for d in self.deps[:TRUNC_OPS]]
        n = len(self.ops)
        eng_of = [o[0] for o in self.ops]
        isdma = [o[2] for o in self.ops]
        qsem_of = [o[4] for o in self.ops]
        publish = [False] * n
        for j in range(n):
            for i in self.deps[j]:
                if (FULL_SYNC or eng_of[i] != eng_of[j] or isdma[i] or isdma[j]
                        or eng_of[i] in SYNC_ENGS):
                    publish[i] = True
        for i in range(n):
            if isdma[i]:
                publish[i] = True
        doms = []
        dom = [None] * n
        for i in range(n):
            d = ("D", qsem_of[i]) if isdma[i] else ("E", eng_of[i])
            dom[i] = d
            if d not in doms:
                doms.append(d)
        sems = {d: mksem(f"{d[0]}_{d[1]}"[:28]) for d in doms}
        pubcount = {}
        val_after = [0] * n
        for i in range(n):
            if publish[i]:
                pubcount[dom[i]] = pubcount.get(dom[i], 0) + (16 if isdma[i] else 1)
            val_after[i] = pubcount.get(dom[i], 0)
        finals = dict(pubcount)

        def section(engname):
            def go(eng):
                waited = {}
                for i in range(n):
                    if eng_of[i] != engname:
                        continue
                    need = {}
                    for j in self.deps[i]:
                        if not publish[j]:
                            continue
                        if (not FULL_SYNC and dom[j] == ("E", engname)
                                and engname not in SYNC_ENGS and not isdma[i]):
                            continue
                        v = val_after[j]
                        if need.get(dom[j], 0) < v:
                            need[dom[j]] = v
                    for d, v in sorted(need.items(), key=lambda kv: str(kv[0])):
                        if waited.get(d, 0) < v:
                            eng.wait_ge(sems[d], v)
                            waited[d] = v
                    if ((engname in NOP_ENGS or any(t in self.ops[i][3] for t in NOP_TAGS)
                         or (NOP_TAGS and "UNTAGGED" in NOP_TAGS and engname == "scalar"
                             and not self.ops[i][3])) and not isdma[i]):
                        inst = eng.nop()
                    else:
                        inst = self.ops[i][1](eng)
                    if publish[i]:
                        inst.then_inc(sems[dom[i]], 16 if isdma[i] else 1)
                if engname == final_eng:
                    for d, v in sorted(finals.items(), key=lambda kv: str(kv[0])):
                        if d[0] == "D":
                            eng.wait_ge(sems[d], v)
            return go

        block.sync(section("sync"))
        block.tensor(section("tensor"))
        block.vector(section("vector"))
        block.scalar(section("scalar"))
        block.gpsimd(section("gpsimd"))


class Pool:
    def __init__(self, ap, n):
        self.ap, self.n, self.off, self.marks = ap, n, 0, []

    def alloc(self, cols, parts=128):
        cols_al = (cols + 15) & ~15
        assert self.off + cols_al <= self.n, f"SBUF overflow {self.off}+{cols_al}>{self.n}"
        a = self.ap[0:parts, self.off:self.off + cols]
        self.off += cols_al
        return a

    def push(self):
        self.marks.append(self.off)

    def pop(self):
        self.off = self.marks.pop()

    def raw(self, p0, p1, c0, c1):
        return self.ap[p0:p1, c0:c1]


PHASES = "ABC"
LIMIT_BLOCKS = None
TRUNC_OPS = None
N_CORES_OVERRIDE = None
NOP_ENGS = ()
NOP_TAGS = ()
DBG = {}
DBG_DUMP = None
FULL_SYNC = False
RACE_DETECT = False
SYNC_ENGS = ("gpsimd", "vector", "scalar")


def build_program():
    nc = bass.Bass("TRN2", target_bir_lowering=False, detect_race_conditions=RACE_DETECT)
    dt = nc.dram_tensor

    x_in = dt("x", [BPC, T, D], F32, kind="ExternalInput").ap()
    out_hbm = dt("out", [BPC, T, D], F32, kind="ExternalOutput").ap()
    ascr = dt("ascr", [BPC, T, NSLOT], F32, kind="Internal").ap()

    w_inT = dt("w_inT", [32, 128, 8 * 128], F32, kind="ExternalInput").ap()
    conv_w_c = dt("conv_w_c", [128, NFC * 4], F32, kind="ExternalInput").ap()
    conv_b_c = dt("conv_b_c", [128, NFC], F32, kind="ExternalInput").ap()
    w_gateT = dt("w_gateT", [HID, 96], F32, kind="ExternalInput").ap()
    w_deltaT = dt("w_deltaT", [64, HID], F32, kind="ExternalInput").ap()
    dbias_c = dt("dbias_c", [128, NFC], F32, kind="ExternalInput").ap()
    alog_c = dt("alog_c", [128, NFC * NS], F32, kind="ExternalInput").ap()
    dvec_c = dt("dvec_c", [128, NFC], F32, kind="ExternalInput").ap()
    w_outT = dt("w_outT", [HID, D], F32, kind="ExternalInput").ap()

    slot_init = dt("slot_init", [NSLOT, D], F32, kind="ExternalInput").ap()
    slot_initT = dt("slot_initT", [D, NSLOT], F32, kind="ExternalInput").ap()
    w1a_d = dt("w1a", [D, DH], F32, kind="ExternalInput").ap()
    w1b_d = dt("w1b", [D, DH], F32, kind="ExternalInput").ap()
    b1_c = dt("b1_c", [128, 4], F32, kind="ExternalInput").ap()
    w2_d = dt("w2", [DH, NSLOT], F32, kind="ExternalInput").ap()
    b2_c = dt("b2_c", [NSLOT, 1], F32, kind="ExternalInput").ap()
    tau_d = dt("tau", [1, 1], F32, kind="ExternalInput").ap()
    wihT = dt("wihT", [D, 3 * D], F32, kind="ExternalInput").ap()
    whhT = dt("whhT", [D, 3 * D], F32, kind="ExternalInput").ap()
    bih_c = dt("bih_c", [128, 24], F32, kind="ExternalInput").ap()
    bhh_c = dt("bhh_c", [128, 24], F32, kind="ExternalInput").ap()
    wq_d = dt("wq", [D, DQ], F32, kind="ExternalInput").ap()
    wk_d = dt("wk", [D, DQ], F32, kind="ExternalInput").ap()
    bqk_c = dt("bqk_c", [128, 4], F32, kind="ExternalInput").ap()
    wv_d = dt("wv", [D, D], F32, kind="ExternalInput").ap()
    bv_r = dt("bv_r", [NSLOT, D], F32, kind="ExternalInput").ap()
    lng_r = dt("lng_r", [NSLOT, D], F32, kind="ExternalInput").ap()
    lnb_r = dt("lnb_r", [NSLOT, D], F32, kind="ExternalInput").ap()
    wvp_d = dt("wvp", [D, D], F32, kind="ExternalInput").ap()
    bvp_r = dt("bvp_r", [NSLOT, D], F32, kind="ExternalInput").ap()
    wo_d = dt("wo", [D, D], F32, kind="ExternalInput").ap()
    bo_r = dt("bo_r", [1, D], F32, kind="ExternalInput").ap()

    import contextlib
    st = contextlib.ExitStack()
    with st:
        POOLN = 53200
        pool_t = st.enter_context(nc.sbuf_tensor("pool", [128, POOLN], F32))
        sbp = Pool(pool_t.ap(), POOLN)
        psum_t = st.enter_context(nc.psum_tensor("ppool", [128, 4096], F32))
        ps = psum_t.ap()

        def R(c0, c1, p0=0, p1=128):
            return ps[p0:p1, c0:c1], ["PSB%d" % i for i in range(c0 // 512, (c1 + 511) // 512)]

        def mksem(name):
            return st.enter_context(nc.semaphore(name))

        blk_ctx = st.enter_context(nc.Block())
        P = Prog()

        ident = sbp.alloc(128)
        ones = sbp.alloc(128)
        konst = sbp.alloc(4)
        cwc = sbp.alloc(NFC * 4)
        cbc = sbp.alloc(NFC)
        dbc = sbp.alloc(NFC)
        acol = sbp.alloc(NFC * NS)
        dcolv = sbp.alloc(NFC)
        wgT_b = sbp.alloc(NFC * 96)
        wdT_b = sbp.alloc(HID, parts=64)
        si_sb2 = [sbp.alloc(D, parts=64) for _ in range(BPC)]
        dncol = sbp.alloc(2, parts=64)
        sv0_off = sbp.off
        sv_sb = [sbp.alloc(D, parts=64) for _ in range(BPC)]
        bo_row = sbp.raw(64, 65, sv0_off, sv0_off + D)
        DBG["sv0"] = sv0_off
        carry = [sbp.alloc(NS * NFC) for _ in range(BPC)]
        hist_t = sbp.alloc(NFC * 3)

        P.op("gpsimd", lambda e: e.memset(ident[:], 0.0), writes=["ident"])
        P.op("gpsimd", lambda e: e.affine_select(
            out=ident[:], in_=ident[:], compare_op=ALU.not_equal,
            fill=1.0, base=0, pattern=[[-1, 128]], channel_multiplier=1),
            reads=["ident"], writes=["ident"])
        P.op("gpsimd", lambda e: e.memset(ones[:], 1.0), writes=["ones"])
        P.op("gpsimd", lambda e: e.memset(konst[:, 0:1], 1e-5), writes=["konst"])
        P.op("gpsimd", lambda e: e.memset(konst[:, 1:2], -0.5), writes=["konst"])
        for nm, dst, src in [("cwc", cwc, conv_w_c), ("cbc", cbc, conv_b_c),
                             ("dbc", dbc, dbias_c), ("dcolv", dcolv, dvec_c),
                             ("bo_row", bo_row, bo_r)]:
            P.op("sync", lambda e, d=dst, s=src: e.dma_start(out=d, in_=s),
                 writes=[nm], dma=True)
        P.op("sync", lambda e: e.dma_start(out=acol, in_=alog_c), writes=["acol"], dma=True)
        P.op("scalar", lambda e: e.activation(out=acol, in_=acol, func=AF.Exp),
             reads=["acol"], writes=["acol"])
        P.op("vector", lambda e: e.tensor_scalar(out=acol, in0=acol, scalar1=-1.0,
                                                 scalar2=None, op0=ALU.mult),
             reads=["acol"], writes=["acol"])
        P.op("sync", lambda e: e.dma_start(
            out=wgT_b.rearrange("p (c n) -> p c n", n=96),
            in_=w_gateT.rearrange("(c p) n -> p c n", p=128)), writes=["wgT"], dma=True)
        P.op("sync", lambda e: e.dma_start(out=wdT_b, in_=w_deltaT), writes=["wdT"], dma=True)

        def layernorm(xt, xn_out, cols8, scratch, key, rk, scr_key, xtra=()):
            s_ = cols8
            P.op("vector", lambda e: e.reduce_sum(out=s_[:, 0:1], in_=xt, axis=AX.X),
                 reads=[rk], writes=[key + "s", "lncol" + scr_key])
            P.op("vector", lambda e: e.tensor_tensor(out=scratch, in0=xt, in1=xt,
                                                     op=ALU.mult),
                 reads=[rk], writes=["lnscr" + scr_key] + list(xtra))
            P.op("vector", lambda e: e.reduce_sum(out=s_[:, 1:2], in_=scratch, axis=AX.X),
                 reads=["lnscr" + scr_key], writes=[key + "q"])
            P.op("vector", lambda e: e.tensor_scalar(out=s_[:, 0:1], in0=s_[:, 0:1],
                                                     scalar1=1.0 / D, scalar2=None, op0=ALU.mult),
                 reads=[key + "s"], writes=[key + "m"])
            P.op("vector", lambda e: e.tensor_tensor(out=s_[:, 2:3], in0=s_[:, 0:1],
                                                     in1=s_[:, 0:1], op=ALU.mult),
                 reads=[key + "m"], writes=[key + "m2"])
            P.op("vector", lambda e: e.tensor_scalar(out=s_[:, 1:2], in0=s_[:, 1:2],
                                                     scalar1=1.0 / D, scalar2=None, op0=ALU.mult),
                 reads=[key + "q"], writes=[key + "q2"])
            P.op("vector", lambda e: e.tensor_tensor(out=s_[:, 1:2], in0=s_[:, 1:2],
                                                     in1=s_[:, 2:3], op=ALU.subtract),
                 reads=[key + "q2", key + "m2"], writes=[key + "v"])
            P.op("scalar", lambda e: e.activation(out=s_[:, 1:2], in_=s_[:, 1:2],
                                                  func=AF.Ln, bias=konst[:, 0:1]),
                 reads=[key + "v", "konst"], writes=[key + "l"])
            P.op("scalar", lambda e: e.activation(out=s_[:, 1:2], in_=s_[:, 1:2],
                                                  func=AF.Exp, scale=konst[:, 1:2]),
                 reads=[key + "l", "konst"], writes=[key + "i"])
            P.op("vector", lambda e: e.tensor_tensor(out=s_[:, 2:3], in0=s_[:, 0:1],
                                                     in1=s_[:, 1:2], op=ALU.mult),
                 reads=[key + "m", key + "i"], writes=[key + "mi"])
            P.op("vector", lambda e: e.tensor_scalar(out=xn_out, in0=xt,
                                                     scalar1=s_[:, 1:2], scalar2=s_[:, 2:3],
                                                     op0=ALU.mult, op1=ALU.subtract),
                 reads=[rk, key + "i", key + "mi"], writes=[key + "xn"] + list(xtra))

        sbp.push()
        b1tot = sbp.alloc(4)
        b2col = sbp.alloc(1, parts=64)
        invt64 = sbp.alloc(1, parts=64)
        tau_sb = sbp.alloc(4, parts=1)
        P.op("sync", lambda e: e.dma_start(out=b2col, in_=b2_c), writes=["b2col"], dma=True)
        P.op("sync", lambda e: e.dma_start(out=tau_sb[0:1, 0:1], in_=tau_d),
             writes=["tau"], dma=True)
        P.op("vector", lambda e: e.tensor_scalar(out=tau_sb[0:1, 1:2], in0=tau_sb[0:1, 0:1],
                                                 scalar1=-1.0, scalar2=None, op0=ALU.mult),
             reads=["tau"], writes=["tau"])
        P.op("vector", lambda e: e.tensor_tensor(out=tau_sb[0:1, 2:3], in0=tau_sb[0:1, 0:1],
                                                 in1=tau_sb[0:1, 1:2], op=ALU.max),
             reads=["tau"], writes=["tau"])
        P.op("vector", lambda e: e.tensor_scalar(out=tau_sb[0:1, 2:3], in0=tau_sb[0:1, 2:3],
                                                 scalar1=0.1, scalar2=None, op0=ALU.add),
             reads=["tau"], writes=["tau"])
        P.op("vector", lambda e: e.reciprocal(out=tau_sb[0:1, 3:4], in_=tau_sb[0:1, 2:3]),
             reads=["tau"], writes=["tau"])
        P.op("sync", lambda e: e.dma_start(
            out=invt64, in_=tau_sb[0:1, 3:4].unsqueeze(1).broadcast_to((1, 64, 1))),
            reads=["tau"], writes=["invt64"], dma=True)
        w1a_b = sbp.alloc(8 * DH)
        w1b_b = sbp.alloc(8 * DH)
        w2_b = sbp.alloc(4 * NSLOT)
        si_sp = sbp.alloc(D, parts=64)
        smT = sbp.alloc(8)
        bias2 = sbp.alloc(DH, parts=1)
        P.op("sync", lambda e: e.dma_start(
            out=w1a_b.rearrange("p (k f) -> p k f", f=DH),
            in_=w1a_d.rearrange("(k p) f -> p k f", p=128)), writes=["w1a"], dma=True)
        P.op("sync", lambda e: e.dma_start(
            out=w1b_b.rearrange("p (k f) -> p k f", f=DH),
            in_=w1b_d.rearrange("(k p) f -> p k f", p=128)), writes=["w1b"], dma=True)
        P.op("sync", lambda e: e.dma_start(
            out=w2_b.rearrange("p (k f) -> p k f", f=NSLOT),
            in_=w2_d.rearrange("(k p) f -> p k f", p=128)), writes=["w2"], dma=True)
        P.op("sync", lambda e: e.dma_start(out=si_sp, in_=slot_init), writes=["si_sp"], dma=True)
        P.op("sync", lambda e: e.dma_start(out=b1tot, in_=b1_c), writes=["b1tot"], dma=True)

        psm, psmK = R(3584, 3592)
        for k in range(8):
            P.op("tensor", lambda e, k=k: e.matmul(
                psm[:, k:k + 1], lhsT=si_sp[:, k * 128:(k + 1) * 128],
                rhs=ones[0:64, 0:1], start=True, stop=True, skip_group_check=True),
                reads=["si_sp", "ones"], writes=psmK)
        P.op("vector", lambda e: e.tensor_scalar(out=smT, in0=psm, scalar1=1.0 / 64,
                                                 scalar2=None, op0=ALU.mult),
             reads=psmK, writes=["smT"])
        pb2, pb2K = R(512, 1024, p1=1)
        for k in range(8):
            P.op("tensor", lambda e, k=k: e.matmul(
                pb2, lhsT=smT[:, k:k + 1], rhs=w1b_b[:, k * DH:(k + 1) * DH],
                start=(k == 0), stop=(k == 7), skip_group_check=True),
                reads=["smT", "w1b"], writes=pb2K)
        P.op("vector", lambda e: e.tensor_copy(bias2, pb2), reads=pb2K, writes=["bias2"])
        pbt, pbtK = R(3592, 3596)
        for m in range(4):
            P.op("tensor", lambda e, m=m: e.transpose(
                pbt[:, m:m + 1], bias2[0:1, m * 128:(m + 1) * 128], ident[0:1, 0:1]),
                reads=["bias2", "ident"], writes=pbtK)
        P.op("vector", lambda e: e.tensor_tensor(out=b1tot, in0=b1tot, in1=pbt, op=ALU.add),
             reads=pbtK + ["b1tot"], writes=["b1tot"])

        xA = sbp.alloc(2 * D)
        xnA = sbp.alloc(2 * D)
        lnscA = sbp.alloc(D)
        lncolA = sbp.alloc(8)
        xnTA = sbp.alloc(8 * TB)
        h_sb = sbp.alloc(4 * TB)
        lg_sb = sbp.alloc(TB, parts=64)
        lgT = sbp.alloc(2 * NSLOT)
        wkt = sbp.alloc(NSLOT)
        mcol = sbp.alloc(16)
        amt = sbp.alloc(NSLOT)
        alset = sbp.alloc(NSLOT)
        for _n, _a in [("xA", xA), ("xnA", xnA), ("lnscA", lnscA), ("lncolA", lncolA),
                       ("xnTA", xnTA), ("h_sb", h_sb), ("lg_sb", lg_sb), ("lgT", lgT),
                       ("wkt", wkt), ("mcol", mcol), ("amt", amt), ("alset", alset),
                       ("w1a", w1a_b), ("b1tot", b1tot), ("invt64", invt64), ("tau", tau_sb)]:
            DBG[_n] = _a.offset

        ps_hm = [R(0, 256), R(512, 768), R(256, 512), R(768, 1024)]
        ps_si = [R(1024, 2048, p1=64) for b in range(BPC)]
        ps_dn = [R(3072, 3080, p1=64) for b in range(BPC)]
        ps_lg = R(3584, 3840, p1=64)
        ps_tpA = [R(2048, 2176), R(2560, 2688)]

        def phaseA_block(b, blk):
            t0 = blk * TB
            key = f"A{b}_{blk}"
            P.op("sync", lambda e: e.dma_start(
                out=xA.rearrange("p (s d) -> p s d", d=D),
                in_=x_in[b, t0:t0 + TB, :].rearrange("(s p) d -> p s d", p=128)),
                writes=["xA"], dma=True, tag=key + "xload")
            P.op("scalar", lambda e: e.dma_start(
                out=out_hbm[b, t0:t0 + TB, :].rearrange("(s p) d -> p s d", p=128),
                in_=xA.rearrange("p (s d) -> p s d", d=D)),
                reads=["xA"], dma=True, tag=key + "xstore", qsem="store")
            for sub in range(2):
                xs = xA[:, sub * D:(sub + 1) * D]
                xn = xnA[:, sub * D:(sub + 1) * D]
                layernorm(xs, xn, lncolA, lnscA, key + f"ln{sub}", "xA", "A")
                for k in range(8):
                    tp, tpK = ps_tpA[k % 2]
                    P.op("tensor", lambda e, k=k, xn=xn, tp=tp: e.transpose(
                        tp[:, 0:128], xn[:, k * 128:(k + 1) * 128], ident[:]),
                        reads=[key + f"ln{sub}xn", "ident"], writes=tpK)
                    P.op("vector", lambda e, k=k, sub=sub, tp=tp: e.tensor_copy(
                        xnTA[:, k * TB + sub * 128: k * TB + sub * 128 + 128], tp[:, 0:128]),
                        reads=tpK, writes=[f"xnTA_{k}_{sub}"])
            xnT_keys = [f"xnTA_{k}_{s}" for k in range(8) for s in range(2)]
            for m in range(4):
                pm, pmK = ps_hm[m]
                for k in range(8):
                    P.op("tensor", lambda e, m=m, k=k, pm=pm: e.matmul(
                        pm, lhsT=w1a_b[:, k * DH + m * 128: k * DH + m * 128 + 128],
                        rhs=xnTA[:, k * TB:(k + 1) * TB],
                        start=(k == 0), stop=(k == 7), skip_group_check=True),
                        reads=["w1a"] + xnT_keys, writes=pmK, tag=key + "w1")
                P.op("scalar", lambda e, m=m, pm=pm: e.activation(
                    out=h_sb[:, m * TB:(m + 1) * TB], in_=pm, func=AF.Gelu,
                    bias=b1tot[:, m:m + 1]),
                    reads=pmK + ["b1tot"], writes=[f"h{m}"], tag=key + "gelu")
            plg, plgK = ps_lg
            for k in range(4):
                P.op("tensor", lambda e, k=k, plg=plg: e.matmul(
                    plg, lhsT=w2_b[:, k * NSLOT:(k + 1) * NSLOT],
                    rhs=h_sb[:, k * TB:(k + 1) * TB],
                    start=(k == 0), stop=(k == 3), skip_group_check=True),
                    reads=["w2", f"h{k}"], writes=plgK, tag=key + "w2")
            P.op("vector", lambda e, plg=plg: e.tensor_scalar(
                out=lg_sb, in0=plg, scalar1=b2col, scalar2=invt64,
                op0=ALU.add, op1=ALU.mult),
                reads=plgK + ["b2col", "invt64"], writes=["lg"], tag=key + "lgevac")
            for sub in range(2):
                tp, tpK = ps_tpA[sub]
                P.op("tensor", lambda e, sub=sub, tp=tp: e.transpose(
                    tp[:, 0:NSLOT], lg_sb[:, sub * 128:(sub + 1) * 128], ident[0:64, 0:64]),
                    reads=["lg", "ident"], writes=tpK)
                P.op("vector", lambda e, sub=sub, tp=tp: e.tensor_copy(
                    lgT[:, sub * NSLOT:(sub + 1) * NSLOT], tp[:, 0:NSLOT]),
                    reads=tpK, writes=[f"lgT{sub}"])
            for sub in range(2):
                lgs = lgT[:, sub * NSLOT:(sub + 1) * NSLOT]
                P.op("vector", lambda e, lgs=lgs: e.tensor_copy(wkt, lgs),
                     reads=[f"lgT{sub}"], writes=["wkt"])
                for it in range(TOPK):
                    P.op("vector", lambda e, it=it: e.reduce_max(
                        out=mcol[:, it:it + 1], in_=wkt, axis=AX.X, negate=True),
                        reads=["wkt"], writes=[f"mx{it}", "mcol"])
                    if it < TOPK - 1:
                        P.op("vector", lambda e, it=it: e.tensor_scalar(
                            out=amt, in0=wkt, scalar1=mcol[:, it:it + 1],
                            scalar2=0.0, op0=ALU.add, op1=ALU.is_ge),
                            reads=["wkt", f"mx{it}"], writes=["amt"])
                        P.op("vector", lambda e: e.scalar_tensor_tensor(
                            out=wkt, in0=amt, scalar=-1e38, in1=wkt,
                            op0=ALU.mult, op1=ALU.add),
                            reads=["amt", "wkt"], writes=["wkt"])
                P.op("scalar", lambda e, lgs=lgs: e.activation(
                    out=amt, in_=lgs, func=AF.Exp, bias=mcol[:, 0:1]),
                    reads=[f"lgT{sub}", "mx0"], writes=["amt"], tag=key + "aexp")
                P.op("vector", lambda e, lgs=lgs: e.tensor_scalar(
                    out=wkt, in0=lgs, scalar1=mcol[:, TOPK - 1:TOPK],
                    scalar2=0.0, op0=ALU.add, op1=ALU.is_ge),
                    reads=[f"lgT{sub}", f"mx{TOPK - 1}"], writes=["wkt"])
                P.op("vector", lambda e: e.tensor_tensor(
                    out=alset, in0=amt, in1=wkt, op=ALU.mult),
                    reads=["amt", "wkt"], writes=["alset"])
                P.op("vector", lambda e: e.reduce_sum(out=mcol[:, 8:9], in_=alset, axis=AX.X),
                     reads=["alset"], writes=["aZ", "mcol"])
                P.op("vector", lambda e: e.reciprocal(out=mcol[:, 9:10], in_=mcol[:, 8:9]),
                     reads=["aZ"], writes=["aZr", "mcol"])
                P.op("vector", lambda e: e.tensor_scalar(
                    out=alset, in0=alset, scalar1=mcol[:, 9:10], scalar2=None, op0=ALU.mult),
                    reads=["alset", "aZr"], writes=["alset"], tag=key + "anorm")
                P.op("scalar", lambda e, b=b, t0=t0, sub=sub: e.dma_start(
                    out=ascr[b, t0 + sub * 128: t0 + sub * 128 + 128, :], in_=alset),
                    reads=["alset"], dma=True, tag=key + "astore", qsem="astore")
                first = (blk == 0 and sub == 0)
                last = (blk == (LIMIT_BLOCKS or NBLK) - 1 and sub == 1)
                xn = xnA[:, sub * D:(sub + 1) * D]
                psi, psiK = ps_si[b]
                for half in range(2):
                    P.op("tensor", lambda e, xn=xn, half=half, first=first, last=last, psi=psi: e.matmul(
                        psi[:, half * 512:(half + 1) * 512], lhsT=alset,
                        rhs=xn[:, half * 512:(half + 1) * 512],
                        start=first, stop=last, skip_group_check=True),
                        reads=["alset", key + f"ln{sub}xn"], writes=psiK, tag=key + "simm")
                pdn, pdnK = ps_dn[b]
                P.op("tensor", lambda e, first=first, last=last, pdn=pdn: e.matmul(
                    pdn, lhsT=alset, rhs=ones[:, 0:8],
                    start=first, stop=last, skip_group_check=True),
                    reads=["alset", "ones"], writes=pdnK, tag=key + "dnmm")

        nblk_a = LIMIT_BLOCKS or NBLK
        for b in range(BPC):
            for blk in range(nblk_a):
                phaseA_block(b, blk)
            P.op("vector", lambda e, b=b: e.tensor_copy(dncol[:, b:b + 1], ps_dn[b][0][:, 0:1]),
                 reads=ps_dn[b][1], writes=[f"dncol{b}", "dncol"])
            P.op("vector", lambda e, b=b: e.tensor_copy(si_sb2[b], ps_si[b][0]),
                 reads=ps_si[b][1], writes=[f"siraw{b}"])
        sbp.pop()
        P.barrier()
        if "B" not in PHASES:
            if DBG_DUMP is not None:
                c0, c1 = DBG_DUMP
                dbg_t = dt("dbg", [128, c1 - c0], F32, kind="ExternalOutput").ap()
                P.barrier()
                P.op("sync", lambda e: e.dma_start(out=dbg_t, in_=sbp.raw(0, 128, c0, c1)),
                     dma=True, qsem="dbgdump")
            P.emit(blk_ctx, mksem, final_eng="sync")
            return nc

        sbp.push()
        s0T_b = sbp.alloc(8 * NSLOT)
        bihc = sbp.alloc(24)
        bhhc = sbp.alloc(24)
        bqkc = sbp.alloc(4)
        wm_b = [sbp.alloc(8 * 128) for _ in range(2)]
        wq_b = sbp.alloc(8 * DQ)
        wk_b = sbp.alloc(8 * DQ)
        wv_b = sbp.alloc(8 * D)
        wvp_b = sbp.alloc(8 * D)
        gh_sb = sbp.alloc(24 * NSLOT)
        gi_sb = sbp.alloc(24 * NSLOT)
        r_sb = sbp.alloc(8 * NSLOT)
        z_sb = sbp.alloc(8 * NSLOT)
        n_sb = sbp.alloc(8 * NSLOT)
        snew = sbp.alloc(8 * NSLOT)
        snewT = sbp.alloc(D, parts=64)
        q_sb = sbp.alloc(2 * NSLOT)
        k_sb = sbp.alloc(2 * NSLOT)
        vT_sb = sbp.alloc(D, parts=64)
        sc_sb = sbp.alloc(NSLOT, parts=64)
        at_sb = sbp.alloc(NSLOT, parts=64)
        y_sb = sbp.alloc(D, parts=64)
        yn_sb = sbp.alloc(D, parts=64)
        ynT_sb = sbp.alloc(NSLOT)
        siT = sbp.alloc(8 * NSLOT)
        colB = sbp.alloc(8, parts=64)
        scrB = sbp.alloc(D, parts=64)
        ggb = sbp.alloc(D, parts=64)
        for _n, _a in [("s0T", s0T_b), ("gh", gh_sb), ("gi", gi_sb), ("r", r_sb),
                       ("z", z_sb), ("n", n_sb), ("snew", snew), ("snewT", snewT),
                       ("q", q_sb), ("k", k_sb), ("vT", vT_sb), ("sc", sc_sb),
                       ("at", at_sb), ("y", y_sb), ("yn", yn_sb), ("siT", siT),
                       ("scrB", scrB), ("si0", si_sb2[0]), ("si1", si_sb2[1])]:
            DBG[_n] = _a.offset

        P.op("sync", lambda e: e.dma_start(
            out=s0T_b.rearrange("p (k f) -> p k f", f=NSLOT),
            in_=slot_initT.rearrange("(k p) f -> p k f", p=128)), writes=["s0T"], dma=True)
        P.op("sync", lambda e: e.dma_start(out=bihc, in_=bih_c), writes=["bihc"], dma=True)
        P.op("sync", lambda e: e.dma_start(out=bhhc, in_=bhh_c), writes=["bhhc"], dma=True)
        P.op("sync", lambda e: e.dma_start(out=bqkc, in_=bqk_c), writes=["bqkc"], dma=True)
        P.op("sync", lambda e: e.dma_start(
            out=wq_b.rearrange("p (k f) -> p k f", f=DQ),
            in_=wq_d.rearrange("(k p) f -> p k f", p=128)), writes=["wqb"], dma=True)
        P.op("sync", lambda e: e.dma_start(
            out=wk_b.rearrange("p (k f) -> p k f", f=DQ),
            in_=wk_d.rearrange("(k p) f -> p k f", p=128)), writes=["wkb"], dma=True)
        P.op("sync", lambda e: e.dma_start(
            out=wv_b.rearrange("p (k f) -> p k f", f=D),
            in_=wv_d.rearrange("(k p) f -> p k f", p=128)), writes=["wvb"], dma=True)
        P.op("sync", lambda e: e.dma_start(
            out=wvp_b.rearrange("p (k f) -> p k f", f=D),
            in_=wvp_d.rearrange("(k p) f -> p k f", p=128)), writes=["wvpb"], dma=True)

        for b in range(BPC):
            P.op("vector", lambda e, b=b: e.tensor_scalar(
                out=colB[:, 0:1], in0=dncol[:, b:b + 1], scalar1=1e-8, scalar2=None, op0=ALU.add),
                reads=[f"dncol{b}"], writes=[f"B{b}_dn1", "colB"])
            P.op("vector", lambda e, b=b: e.reciprocal(out=colB[:, 1:2], in_=colB[:, 0:1]),
                 reads=[f"B{b}_dn1"], writes=[f"B{b}_dnr", "colB"])
            P.op("vector", lambda e, b=b: e.tensor_scalar(
                out=si_sb2[b], in0=si_sb2[b], scalar1=colB[:, 1:2], scalar2=None, op0=ALU.mult),
                reads=[f"siraw{b}", f"B{b}_dnr"], writes=[f"B{b}_si"])

        pgh, pghK = R(0, 64)
        pgi, pgiK = R(64, 128)
        pq, pqK = R(128, 192)
        ptB, ptBK = R(256, 384)
        psc, pscK = R(384, 448, p1=64)
        pvT, pvTK = R(1024, 2048, p1=64)
        pcm, pcmK = R(2048, 3072, p1=64)

        for m in range(24):
            wb = wm_b[m % 2]
            P.op("sync", lambda e, m=m, wb=wb: e.dma_start(
                out=wb.rearrange("p (k f) -> p k f", f=128),
                in_=whhT[:, m * 128:(m + 1) * 128].rearrange("(k p) f -> p k f", p=128)),
                writes=[f"wm{m % 2}"], dma=True)
            for k in range(8):
                P.op("tensor", lambda e, k=k, wb=wb: e.matmul(
                    pgh, lhsT=wb[:, k * 128:(k + 1) * 128],
                    rhs=s0T_b[:, k * NSLOT:(k + 1) * NSLOT],
                    start=(k == 0), stop=(k == 7), skip_group_check=True),
                    reads=[f"wm{m % 2}", "s0T"], writes=pghK)
            P.op("vector", lambda e, m=m: e.tensor_copy(
                gh_sb[:, m * NSLOT:(m + 1) * NSLOT], pgh),
                reads=pghK, writes=[f"gh{m}"])

        for b in range(BPC):
            kb = f"B{b}_"
            for k in range(8):
                P.op("tensor", lambda e, k=k, b=b: e.transpose(
                    ptB[:, 0:64], si_sb2[b][:, k * 128:(k + 1) * 128], ident[0:64, 0:64]),
                    reads=[kb + "si", "ident"], writes=ptBK)
                P.op("vector", lambda e, k=k: e.tensor_copy(
                    siT[:, k * NSLOT:(k + 1) * NSLOT], ptB[:, 0:64]),
                    reads=ptBK, writes=[kb + f"siT{k}"])
            siT_keys = [kb + f"siT{k}" for k in range(8)]
            for m in range(24):
                wb = wm_b[m % 2]
                P.op("sync", lambda e, m=m, wb=wb: e.dma_start(
                    out=wb.rearrange("p (k f) -> p k f", f=128),
                    in_=wihT[:, m * 128:(m + 1) * 128].rearrange("(k p) f -> p k f", p=128)),
                    writes=[f"wm{m % 2}"], dma=True)
                for k in range(8):
                    P.op("tensor", lambda e, k=k, wb=wb: e.matmul(
                        pgi, lhsT=wb[:, k * 128:(k + 1) * 128],
                        rhs=siT[:, k * NSLOT:(k + 1) * NSLOT],
                        start=(k == 0), stop=(k == 7), skip_group_check=True),
                        reads=[f"wm{m % 2}"] + siT_keys, writes=pgiK)
                P.op("vector", lambda e, m=m: e.tensor_copy(
                    gi_sb[:, m * NSLOT:(m + 1) * NSLOT], pgi),
                    reads=pgiK, writes=[kb + f"gi{m}"])
            for out_sb, base in ((r_sb, 0), (z_sb, 8)):
                for m8 in range(8):
                    m = base + m8
                    sl = slice(m8 * NSLOT, (m8 + 1) * NSLOT)
                    P.op("vector", lambda e, m=m, sl=sl, o=out_sb: e.scalar_tensor_tensor(
                        out=o[:, sl], in0=gi_sb[:, m * NSLOT:(m + 1) * NSLOT],
                        scalar=bihc[:, m:m + 1], in1=gh_sb[:, m * NSLOT:(m + 1) * NSLOT],
                        op0=ALU.add, op1=ALU.add),
                        reads=[kb + f"gi{m}", f"gh{m}", "bihc"], writes=[kb + f"g1_{m}"])
                    P.op("vector", lambda e, m=m, sl=sl, o=out_sb: e.tensor_scalar(
                        out=o[:, sl], in0=o[:, sl], scalar1=bhhc[:, m:m + 1],
                        scalar2=None, op0=ALU.add),
                        reads=[kb + f"g1_{m}", "bhhc"], writes=[kb + f"g2_{m}"])
                    P.op("scalar", lambda e, sl=sl, o=out_sb: e.activation(
                        out=o[:, sl], in_=o[:, sl], func=AF.Sigmoid),
                        reads=[kb + f"g2_{m}"], writes=[kb + f"gact{m}"])
            for m8 in range(8):
                m = 16 + m8
                sl = slice(m8 * NSLOT, (m8 + 1) * NSLOT)
                P.op("vector", lambda e, m=m, sl=sl: e.tensor_scalar(
                    out=n_sb[:, sl], in0=gh_sb[:, m * NSLOT:(m + 1) * NSLOT],
                    scalar1=bhhc[:, m:m + 1], scalar2=None, op0=ALU.add),
                    reads=[f"gh{m}", "bhhc"], writes=[kb + f"nh{m8}"])
                P.op("vector", lambda e, m8=m8, sl=sl: e.tensor_tensor(
                    out=n_sb[:, sl], in0=r_sb[:, sl], in1=n_sb[:, sl], op=ALU.mult),
                    reads=[kb + f"gact{m8}", kb + f"nh{m8}"], writes=[kb + f"nr{m8}"])
                P.op("vector", lambda e, m=m, sl=sl: e.scalar_tensor_tensor(
                    out=n_sb[:, sl], in0=gi_sb[:, m * NSLOT:(m + 1) * NSLOT],
                    scalar=bihc[:, m:m + 1], in1=n_sb[:, sl], op0=ALU.add, op1=ALU.add),
                    reads=[kb + f"gi{m}", kb + f"nr{m8}", "bihc"], writes=[kb + f"np{m8}"])
                P.op("scalar", lambda e, sl=sl: e.activation(
                    out=n_sb[:, sl], in_=n_sb[:, sl], func=AF.Tanh),
                    reads=[kb + f"np{m8}"], writes=[kb + f"n{m8}"])
                P.op("vector", lambda e, m8=m8, sl=sl: e.tensor_tensor(
                    out=snew[:, sl], in0=s0T_b[:, sl], in1=n_sb[:, sl], op=ALU.subtract),
                    reads=["s0T", kb + f"n{m8}"], writes=[kb + f"sd{m8}"])
                P.op("vector", lambda e, m8=m8, sl=sl: e.tensor_tensor(
                    out=snew[:, sl], in0=z_sb[:, sl], in1=snew[:, sl], op=ALU.mult),
                    reads=[kb + f"gact{8 + m8}", kb + f"sd{m8}"], writes=[kb + f"zs{m8}"])
                P.op("vector", lambda e, m8=m8, sl=sl: e.tensor_tensor(
                    out=snew[:, sl], in0=n_sb[:, sl], in1=snew[:, sl], op=ALU.add),
                    reads=[kb + f"n{m8}", kb + f"zs{m8}"], writes=[kb + f"sn{m8}"])
            snew_keys = [kb + f"sn{m8}" for m8 in range(8)]
            for k in range(8):
                P.op("tensor", lambda e, k=k: e.transpose(
                    ptB[0:64, 0:128], snew[:, k * NSLOT:(k + 1) * NSLOT], ident[:]),
                    reads=snew_keys + ["ident"], writes=ptBK)
                P.op("vector", lambda e, k=k: e.tensor_copy(
                    snewT[:, k * 128:(k + 1) * 128], ptB[0:64, 0:128]),
                    reads=ptBK, writes=[kb + f"snT{k}"])
            snewT_keys = [kb + f"snT{k}" for k in range(8)]
            for wb_, bi, dst, wkey, rkey in ((wq_b, 0, q_sb, "q", "wqb"),
                                             (wk_b, 2, k_sb, "k", "wkb")):
                for c in range(2):
                    for k in range(8):
                        P.op("tensor", lambda e, c=c, k=k, wb_=wb_: e.matmul(
                            pq, lhsT=wb_[:, k * DQ + c * 128: k * DQ + c * 128 + 128],
                            rhs=snew[:, k * NSLOT:(k + 1) * NSLOT],
                            start=(k == 0), stop=(k == 7), skip_group_check=True),
                            reads=[rkey] + snew_keys, writes=pqK)
                    P.op("vector", lambda e, c=c, dst=dst, bi=bi: e.tensor_scalar(
                        out=dst[:, c * NSLOT:(c + 1) * NSLOT], in0=pq,
                        scalar1=bqkc[:, bi + c:bi + c + 1], scalar2=None, op0=ALU.add),
                        reads=pqK + ["bqkc"], writes=[kb + wkey + str(c)])
            for nh in range(2):
                for k in range(8):
                    P.op("tensor", lambda e, nh=nh, k=k: e.matmul(
                        pvT[:, nh * 512:(nh + 1) * 512],
                        lhsT=snew[:, k * NSLOT:(k + 1) * NSLOT],
                        rhs=wv_b[:, k * D + nh * 512: k * D + nh * 512 + 512],
                        start=(k == 0), stop=(k == 7), skip_group_check=True),
                        reads=["wvb"] + snew_keys, writes=pvTK)
            P.op("sync", lambda e: e.dma_start(out=ggb, in_=bv_r), writes=["ggb"], dma=True)
            P.op("vector", lambda e: e.tensor_tensor(out=vT_sb, in0=pvT, in1=ggb, op=ALU.add),
                 reads=pvTK + ["ggb"], writes=[kb + "vT"])
            for c in range(2):
                P.op("tensor", lambda e, c=c: e.matmul(
                    psc, lhsT=q_sb[:, c * NSLOT:(c + 1) * NSLOT],
                    rhs=k_sb[:, c * NSLOT:(c + 1) * NSLOT],
                    start=(c == 0), stop=(c == 1), skip_group_check=True),
                    reads=[kb + "q" + str(c), kb + "k" + str(c)], writes=pscK)
            P.op("vector", lambda e: e.tensor_scalar(
                out=sc_sb, in0=psc, scalar1=DQ ** -0.5, scalar2=None, op0=ALU.mult),
                reads=pscK, writes=[kb + "sc"])
            P.op("vector", lambda e: e.reduce_max(out=colB[:, 2:3], in_=sc_sb,
                                                  axis=AX.X, negate=True),
                 reads=[kb + "sc"], writes=[kb + "scm", "colB"])
            P.op("scalar", lambda e: e.activation(out=sc_sb, in_=sc_sb, func=AF.Exp,
                                                  bias=colB[:, 2:3]),
                 reads=[kb + "sc", kb + "scm"], writes=[kb + "sce"])
            P.op("vector", lambda e: e.reduce_sum(out=colB[:, 3:4], in_=sc_sb, axis=AX.X),
                 reads=[kb + "sce"], writes=[kb + "scs", "colB"])
            P.op("vector", lambda e: e.reciprocal(out=colB[:, 4:5], in_=colB[:, 3:4]),
                 reads=[kb + "scs"], writes=[kb + "scz", "colB"])
            P.op("vector", lambda e: e.tensor_scalar(
                out=sc_sb, in0=sc_sb, scalar1=colB[:, 4:5], scalar2=None, op0=ALU.mult),
                reads=[kb + "sce", kb + "scz"], writes=[kb + "attn"])
            P.op("tensor", lambda e: e.transpose(
                ptB[0:64, 0:64], sc_sb, ident[0:64, 0:64]),
                reads=[kb + "attn", "ident"], writes=ptBK)
            P.op("vector", lambda e: e.tensor_copy(at_sb, ptB[0:64, 0:64]),
                 reads=ptBK, writes=[kb + "attnT"])
            for nh in range(2):
                P.op("tensor", lambda e, nh=nh: e.matmul(
                    pcm[:, nh * 512:(nh + 1) * 512], lhsT=at_sb,
                    rhs=vT_sb[:, nh * 512:(nh + 1) * 512],
                    start=True, stop=True, skip_group_check=True),
                    reads=[kb + "attnT", kb + "vT"], writes=pcmK)
            P.op("vector", lambda e: e.tensor_tensor(out=y_sb, in0=pcm, in1=snewT, op=ALU.add),
                 reads=pcmK + snewT_keys, writes=[kb + "y"])
            P.op("vector", lambda e: e.reduce_sum(out=colB[:, 0:1], in_=y_sb, axis=AX.X),
                 reads=[kb + "y"], writes=[kb + "ys", "colB"])
            P.op("vector", lambda e: e.tensor_tensor(out=scrB, in0=y_sb, in1=y_sb,
                                                     op=ALU.mult),
                 reads=[kb + "y"], writes=["scrB"])
            P.op("vector", lambda e: e.reduce_sum(out=colB[:, 1:2], in_=scrB, axis=AX.X),
                 reads=["scrB"], writes=[kb + "yq", "colB"])
            P.op("vector", lambda e: e.tensor_scalar(out=colB[:, 0:1], in0=colB[:, 0:1],
                                                     scalar1=1.0 / D, scalar2=None, op0=ALU.mult),
                 reads=[kb + "ys"], writes=[kb + "ym", "colB"])
            P.op("vector", lambda e: e.tensor_tensor(out=colB[:, 2:3], in0=colB[:, 0:1],
                                                     in1=colB[:, 0:1], op=ALU.mult),
                 reads=[kb + "ym"], writes=[kb + "ym2", "colB"])
            P.op("vector", lambda e: e.tensor_scalar(out=colB[:, 1:2], in0=colB[:, 1:2],
                                                     scalar1=1.0 / D, scalar2=None, op0=ALU.mult),
                 reads=[kb + "yq"], writes=[kb + "yq2", "colB"])
            P.op("vector", lambda e: e.tensor_tensor(out=colB[:, 1:2], in0=colB[:, 1:2],
                                                     in1=colB[:, 2:3], op=ALU.subtract),
                 reads=[kb + "yq2", kb + "ym2"], writes=[kb + "yv", "colB"])
            P.op("scalar", lambda e: e.activation(out=colB[:, 1:2], in_=colB[:, 1:2],
                                                  func=AF.Ln, bias=konst[0:64, 0:1]),
                 reads=[kb + "yv", "konst"], writes=[kb + "yl", "colB"])
            P.op("scalar", lambda e: e.activation(out=colB[:, 1:2], in_=colB[:, 1:2],
                                                  func=AF.Exp, scale=konst[0:64, 1:2]),
                 reads=[kb + "yl", "konst"], writes=[kb + "yi", "colB"])
            P.op("vector", lambda e: e.tensor_tensor(out=colB[:, 2:3], in0=colB[:, 0:1],
                                                     in1=colB[:, 1:2], op=ALU.mult),
                 reads=[kb + "ym", kb + "yi"], writes=[kb + "ymi", "colB"])
            P.op("vector", lambda e: e.tensor_scalar(
                out=yn_sb, in0=y_sb, scalar1=colB[:, 1:2], scalar2=colB[:, 2:3],
                op0=ALU.mult, op1=ALU.subtract),
                reads=[kb + "y", kb + "yi", kb + "ymi"], writes=[kb + "yn0"])
            P.op("sync", lambda e: e.dma_start(out=ggb, in_=lng_r), writes=["ggb"], dma=True)
            P.op("vector", lambda e: e.tensor_tensor(out=yn_sb, in0=yn_sb, in1=ggb, op=ALU.mult),
                 reads=[kb + "yn0", "ggb"], writes=[kb + "yn1"])
            P.op("sync", lambda e: e.dma_start(out=ggb, in_=lnb_r), writes=["ggb"], dma=True)
            P.op("vector", lambda e: e.tensor_tensor(out=yn_sb, in0=yn_sb, in1=ggb, op=ALU.add),
                 reads=[kb + "yn1", "ggb"], writes=[kb + "yn"])
            for k in range(8):
                P.op("tensor", lambda e, k=k: e.transpose(
                    ptB[:, 0:64], yn_sb[:, k * 128:(k + 1) * 128], ident[0:64, 0:64]),
                    reads=[kb + "yn", "ident"], writes=ptBK)
                P.op("vector", lambda e: e.tensor_copy(ynT_sb, ptB[:, 0:64]),
                     reads=ptBK, writes=[kb + "ynT"])
                for nh in range(2):
                    P.op("tensor", lambda e, k=k, nh=nh: e.matmul(
                        pvT[:, nh * 512:(nh + 1) * 512], lhsT=ynT_sb,
                        rhs=wvp_b[:, k * D + nh * 512: k * D + nh * 512 + 512],
                        start=(k == 0), stop=(k == 7), skip_group_check=True),
                        reads=[kb + "ynT", "wvpb"], writes=pvTK)
            P.op("sync", lambda e: e.dma_start(out=ggb, in_=bvp_r), writes=["ggb"], dma=True)
            P.op("vector", lambda e, b=b: e.tensor_tensor(
                out=sv_sb[b], in0=pvT, in1=ggb, op=ALU.add),
                reads=pvTK + ["ggb"], writes=[f"sv{b}"])
        sbp.pop()
        P.barrier()
        if "C" not in PHASES:
            if DBG_DUMP is not None:
                c0, c1 = DBG_DUMP
                dbg_t = dt("dbg", [128, c1 - c0], F32, kind="ExternalOutput").ap()
                P.barrier()
                P.op("sync", lambda e: e.dma_start(out=dbg_t, in_=sbp.raw(0, 128, c0, c1)),
                     dma=True, qsem="dbgdump")
            P.emit(blk_ctx, mksem, final_eng="sync")
            return nc

        sbp.push()
        xC = sbp.alloc(2 * D)
        lncolC = sbp.alloc(8)
        xnTC = sbp.alloc(8 * TB)
        wing = [sbp.alloc(8 * 128) for _ in range(2)]
        u_raw = sbp.alloc(NFC * SEG)
        uconv = sbp.alloc(NFC * TB)
        go_sl = [sbp.alloc(NFC * TB) for _ in range(2)]
        delta = sbp.alloc(NFC * TB)
        du_sl = sbp.alloc(NFC * TB)
        y_sl = sbp.alloc(NFC * TB)
        g_sb2 = sbp.alloc(TB, parts=96)
        _sio = si_sb2[0].offset
        _sio1 = si_sb2[1].offset
        brep = [sbp.raw(0, 128, _sio + i * TB, _sio + (i + 1) * TB) for i in range(2)]
        crep = [sbp.raw(0, 128, _sio + (2 + i) * TB, _sio + (3 + i) * TB) for i in range(2)]
        fbuf = [sbp.raw(0, 128, _sio1 + i * TB, _sio1 + (i + 1) * TB) for i in range(4)]
        dtmp = sbp.alloc(TB)
        gslab = sbp.alloc(NFC * TB)
        hslab = sbp.alloc(8 * TB)
        mslab = sbp.alloc(8 * TB)
        fin_d = mslab
        woutb = None
        wob = None
        aC = sbp.alloc(2 * NSLOT)
        f0T_sb = delta[0:128, 0:2048]
        aT_sb = sbp.alloc(2 * 128, parts=64)

        ps_uo = [R(0, 256), R(512, 768), R(256, 512), R(768, 1024)]
        ps_g = R(1024, 1280, p1=96)
        ps_d = [R(1280, 1536), R(1536, 1792), ps_uo[0], ps_uo[1]]
        ps_f0 = [R(1024, 1280), R(1536, 1792), R(1280, 1536), R(1792, 2048)]
        ps_tpx = [R(0, 128), R(512, 640)]
        ps_tpa = [R(1280, 1408, p1=64), R(1792, 1920, p1=64)]
        ps_fin = [[R(2048 + (sub * 2 + nh) * 512, 2048 + (sub * 2 + nh + 1) * 512)
                   for nh in range(2)] for sub in range(2)]

        def seg(slab, fc, w=TB):
            return slab[:, fc * w:(fc + 1) * w]

        u3d = u_raw.rearrange("p (c t) -> p c t", t=SEG)
        wslots = [(wing[0], "wing0"), (wing[1], "wing1"),
                  (xC[:, 0:D], "xCa"), (xC[:, D:2 * D], "xCb")]

        def stage_front(b, blk):
            t0 = blk * TB
            key = f"C{b}_{blk}"
            P.op("sync", lambda e: e.dma_start(
                out=xC.rearrange("p (s d) -> p s d", d=D),
                in_=x_in[b, t0:t0 + TB, :].rearrange("(s p) d -> p s d", p=128)),
                writes=["xCa", "xCb"], dma=True, tag=key + "xload")
            lnscr = u_raw[:, 2048:3072]
            for sub in range(2):
                xs = xC[:, sub * D:(sub + 1) * D]
                xn = u_raw[:, sub * D:(sub + 1) * D]
                layernorm(xs, xn, lncolC, lnscr, key + f"ln{sub}", "xCa" if sub == 0 else "xCb", "C", xtra=("u_raw",))
                for k in range(8):
                    tp, tpK = ps_tpx[k % 2]
                    P.op("tensor", lambda e, k=k, xn=xn, tp=tp: e.transpose(
                        tp, xn[:, k * 128:(k + 1) * 128], ident[:]),
                        reads=[key + f"ln{sub}xn", "u_raw", "ident"], writes=tpK)
                    P.op("vector", lambda e, k=k, sub=sub, tp=tp: e.tensor_copy(
                        xnTC[:, k * TB + sub * 128: k * TB + sub * 128 + 128], tp),
                        reads=tpK, writes=[f"CxnT_{k}_{sub}"])
            if blk == 0 and b == 0:
                P.op("vector", lambda e: e.memset(u3d[:, :, 0:3], 0.0),
                     reads=["u_raw"], writes=["u_raw"], tag=key + "hist0")
            elif blk == 0:
                P.op("vector", lambda e: e.memset(u3d[:, :, 0:3], 0.0),
                     reads=["u_raw"], writes=["u_raw"], tag=key + "hist0")
            else:
                P.op("vector", lambda e: e.tensor_copy(
                    u3d[:, :, 0:3], hist_t.rearrange("p (c t) -> p c t", t=3)),
                    reads=["u_raw", "hist_t"], writes=["u_raw"], tag=key + "hist")

        def stage_mid_dma(b, blk, grp):
            key = f"C{b}_{blk}"
            wb, wbK = wslots[grp % 4]
            P.op("sync", lambda e, grp=grp, wb=wb: e.dma_start(
                out=wb, in_=w_inT[grp]),
                writes=[wbK], dma=True, tag=key + "windma")

        def stage_mid_mm(b, blk, par, grp):
            key = f"C{b}_{blk}"
            xnT_keys = [f"CxnT_{k}_{s}" for k in range(8) for s in range(2)]
            wb, wbK = wslots[grp % 4]
            pm, pmK = ps_uo[grp % 4]
            for k in range(8):
                P.op("tensor", lambda e, k=k, wb=wb, pm=pm: e.matmul(
                    pm, lhsT=wb[:, k * 128:(k + 1) * 128],
                    rhs=xnTC[:, k * TB:(k + 1) * TB],
                    start=(k == 0), stop=(k == 7), skip_group_check=True),
                    reads=[wbK] + xnT_keys, writes=pmK,
                    tag=key + "uomm")
            fc = grp % 16
            if grp < 16:
                P.op("vector", lambda e, fc=fc, pm=pm: e.tensor_copy(
                    u3d[:, fc, 3:3 + TB], pm),
                    reads=pmK, writes=["u_raw"], tag=key + "uevac")
            else:
                P.op("scalar", lambda e, fc=fc, pm=pm, par=par: e.activation(
                    out=seg(go_sl[par], fc), in_=pm, func=AF.Copy),
                    reads=pmK, writes=[f"go_{par}_{fc}"], tag=key + "goevac")

        def stage_mid_grp(b, blk, par, grps):
            grps = list(grps)
            for j, grp in enumerate(grps):
                if j == 0:
                    stage_mid_dma(b, blk, grp)
                    if j + 1 < len(grps):
                        stage_mid_dma(b, blk, grps[j + 1])
                if j + 2 < len(grps):
                    stage_mid_dma(b, blk, grps[j + 2])
                stage_mid_mm(b, blk, par, grp)

        def stage_mid_post(b, blk):
            key = f"C{b}_{blk}"
            for q in range(NFC // 2):
                for j in range(4):
                    for fc in (2 * q, 2 * q + 1):
                        base = fc * SEG
                        srcv = u_raw[:, base + j:base + j + TB]
                        if j == 0:
                            P.op("vector", lambda e, srcv=srcv, fc=fc: e.tensor_scalar(
                                out=seg(uconv, fc), in0=srcv, scalar1=cwc[:, fc * 4:fc * 4 + 1],
                                scalar2=None, op0=ALU.mult),
                                reads=["u_raw", "cwc"], writes=[f"uc_{fc}"], tag=key + "conv")
                        else:
                            P.op("vector", lambda e, srcv=srcv, fc=fc, j=j: e.scalar_tensor_tensor(
                                out=seg(uconv, fc), in0=srcv, scalar=cwc[:, fc * 4 + j:fc * 4 + j + 1],
                                in1=seg(uconv, fc), op0=ALU.mult, op1=ALU.add),
                                reads=["u_raw", "cwc", f"uc_{fc}"], writes=[f"uc_{fc}"],
                                tag=key + "conv")
                for fc in (2 * q, 2 * q + 1):
                    P.op("scalar", lambda e, fc=fc: e.activation(
                        out=seg(uconv, fc), in_=seg(uconv, fc), func=AF.Silu,
                        bias=cbc[:, fc:fc + 1]),
                        reads=[f"uc_{fc}", "cbc"], writes=[f"uc_{fc}"], tag=key + "u2silu")
            P.op("vector", lambda e: e.tensor_copy(
                hist_t.rearrange("p (c t) -> p c t", t=3), u3d[:, :, TB:TB + 3]),
                reads=["u_raw"], writes=["hist_t"], tag=key + "tails")

        def gate_head(b, blk):
            key = f"C{b}_{blk}"
            pg, pgK = ps_g
            for k in range(NFC):
                P.op("tensor", lambda e, k=k, pg=pg: e.matmul(
                    pg, lhsT=wgT_b[:, k * 96:(k + 1) * 96], rhs=seg(uconv, k),
                    start=(k == 0), stop=(k == NFC - 1), skip_group_check=True),
                    reads=["wgT", f"uc_{k}"], writes=pgK, tag=key + "gmm")
            P.op("vector", lambda e, pg=pg: e.tensor_copy(g_sb2, pg),
                 reads=pgK, writes=["gsb"], tag=key + "gevac")

        def gate_dm(b, blk, m):
            key = f"C{b}_{blk}"
            pd, pdK = ps_d[m % 4]
            P.op("tensor", lambda e, m=m, pd=pd: e.matmul(
                pd, lhsT=wdT_b[:, m * 128:(m + 1) * 128], rhs=g_sb2[0:64, :],
                start=True, stop=True, skip_group_check=True),
                reads=["wdT", "gsb"], writes=pdK, tag=key + "dmm")
            P.op("scalar", lambda e, m=m, pd=pd: e.activation(
                out=dtmp, in_=pd, func=AF.Exp, bias=dbc[:, m:m + 1]),
                reads=pdK + ["dbc"], writes=["dtmp"], tag=key + "dexp")
            P.op("scalar", lambda e, m=m: e.activation(
                out=seg(delta, m), in_=dtmp, func=AF.Ln, bias=1.0),
                reads=["dtmp"], writes=[f"dl_{m}"], tag=key + "dln")

        def gate_du(b, blk):
            key = f"C{b}_{blk}"
            uc_keys = [f"uc_{fc}" for fc in range(NFC)]
            P.op("vector", lambda e: e.tensor_tensor(out=du_sl, in0=delta, in1=uconv,
                                                     op=ALU.mult),
                 reads=[f"dl_{m}" for m in range(NFC)] + uc_keys,
                 writes=["du"], tag=key + "du")

        def stage_gate(b, blk):
            gate_head(b, blk)
            for m in range(NFC):
                gate_dm(b, blk, m)
            gate_du(b, blk)

        def stage_scan(b, blk, par, midnxt, frontnxt, finprev):
            slots_for = {}
            if midnxt is not None:
                for g in range(32):
                    slots_for.setdefault(5 + (g * 27) // 32, []).append(g)
            t0s = blk * TB
            key = f"C{b}_{blk}"
            P.op("sync", lambda e, t0s=t0s, b=b: e.dma_start(
                out=aC.rearrange("p (s a) -> p s a", a=NSLOT),
                in_=ascr[b, t0s:t0s + TB, :].rearrange("(s p) a -> p s a", p=128)),
                writes=["aC"], dma=True, tag=key + "aload")

            def embrep(s):
                br, cr = brep[s % 2], crep[s % 2]
                P.op("sync", lambda e, s=s, br=br: e.dma_start(
                    out=br, in_=g_sb2[64 + s:65 + s, :].unsqueeze(1).broadcast_to((1, 128, TB))),
                    reads=["gsb"], writes=[f"brep{s % 2}"], dma=True, tag=key + "brep")
                P.op("sync", lambda e, s=s, cr=cr: e.dma_start(
                    out=cr, in_=g_sb2[80 + s:81 + s, :].unsqueeze(1).broadcast_to((1, 128, TB))),
                    reads=["gsb"], writes=[f"crep{s % 2}"], dma=True, tag=key + "crep")

            def emgbuild(s, gq, eng):
                br = brep[s % 2]
                gsl = gslab[:, gq * 4 * TB:(gq + 1) * 4 * TB]
                P.op(eng, lambda e, br=br, gsl=gsl: e.tensor_tensor(
                    out=gsl.rearrange("p (c t) -> p c t", t=TB),
                    in0=du_sl[:, gq * 4 * TB:(gq + 1) * 4 * TB].rearrange("p (c t) -> p c t", t=TB),
                    in1=br.unsqueeze(1).broadcast_to((128, 4, TB)), op=ALU.mult),
                    reads=["du", f"brep{s % 2}"], writes=[f"gsl{gq}"],
                    tag=key + "gbuild")

            embrep(0)
            for gq in range(4):
                emgbuild(0, gq, "vector")
            for s in range(NS):
                if s == 2 and frontnxt is not None:
                    stage_front(*frontnxt)
                if s == 8 and finprev is not None:
                    stage_fin(*finprev)
                if s + 1 < NS:
                    embrep(s + 1)
                for half in range(2):
                    for fc8 in range(8):
                        fc = half * 8 + fc8
                        fb = fbuf[fc % 4]
                        P.op("scalar", lambda e, s=s, fc=fc, fb=fb: e.activation(
                            out=fb, in_=seg(delta, fc), func=AF.Exp,
                            scale=acol[:, fc * NS + s:fc * NS + s + 1]),
                            reads=[f"dl_{fc}", "acol"], writes=[f"fb{fc % 4}"],
                            tag=key + "fexp")
                        P.op("vector", lambda e, s=s, fc=fc, fc8=fc8, fb=fb, b=b: e.tensor_tensor_scan(
                            out=seg(hslab, fc8), data0=fb, data1=seg(gslab, fc),
                            initial=carry[b][:, s * NFC + fc:s * NFC + fc + 1],
                            op0=ALU.mult, op1=ALU.add),
                            reads=[f"fb{fc % 4}", f"gsl{fc // 4}", f"ca_{b}_{s}_{fc}"],
                            writes=[f"h_{fc8}"], tag=key + "scan")
                        if s + 1 < NS and fc8 == 3:
                            emgbuild(s + 1, half * 2, "vector")
                        if s + 1 < NS and fc8 == 7:
                            emgbuild(s + 1, half * 2 + 1, "vector")
                    P.op("vector", lambda e, s=s, b=b, half=half: e.tensor_copy(
                        carry[b][:, s * NFC + half * 8:s * NFC + half * 8 + 8],
                        hslab.rearrange("p (c t) -> p c t", t=TB)[:, :, TB - 1]),
                        reads=[f"h_{j}" for j in range(8)],
                        writes=[f"ca_{b}_{s}_{half * 8 + j}" for j in range(8)],
                        tag=key + "carry")
                    sl = s * 2 + half
                    for g in slots_for.get(sl, ()):
                        if g == 0:
                            stage_mid_dma(midnxt[0], midnxt[1], 0)
                            stage_mid_dma(midnxt[0], midnxt[1], 1)
                        if g + 2 < 32:
                            stage_mid_dma(midnxt[0], midnxt[1], g + 2)
                        stage_mid_mm(midnxt[0], midnxt[1], midnxt[2], g)
                    for yq in range(2):
                        q0 = half * 8 * TB + yq * 4 * TB
                        ysl_q = y_sl[:, q0:q0 + 4 * TB]
                        h_q = hslab[:, yq * 4 * TB:(yq + 1) * 4 * TB]
                        m_q = mslab[:, yq * 4 * TB:(yq + 1) * 4 * TB]
                        hkeys = [f"h_{yq * 4 + j}" for j in range(4)]
                        segkeys = [f"y2_{half * 8 + yq * 4 + j}" for j in range(4)]
                        if s == 0:
                            P.op("vector", lambda e, cr=crep[s % 2], ysl_q=ysl_q, h_q=h_q: e.tensor_tensor(
                                out=ysl_q.rearrange("p (c t) -> p c t", t=TB),
                                in0=h_q.rearrange("p (c t) -> p c t", t=TB),
                                in1=cr.unsqueeze(1).broadcast_to((128, 4, TB)), op=ALU.mult),
                                reads=hkeys + [f"crep{s % 2}"],
                                writes=["y_sl"] + segkeys, tag=key + "ymul")
                        else:
                            P.op("vector", lambda e, cr=crep[s % 2], h_q=h_q, m_q=m_q: e.tensor_tensor(
                                out=m_q.rearrange("p (c t) -> p c t", t=TB),
                                in0=h_q.rearrange("p (c t) -> p c t", t=TB),
                                in1=cr.unsqueeze(1).broadcast_to((128, 4, TB)), op=ALU.mult),
                                reads=hkeys + [f"crep{s % 2}"],
                                writes=["mslab"], tag=key + "ymul")
                            P.op("vector", lambda e, ysl_q=ysl_q, m_q=m_q: e.tensor_tensor(
                                out=ysl_q, in0=ysl_q, in1=m_q, op=ALU.add),
                                reads=["y_sl", "mslab"], writes=["y_sl"] + segkeys, tag=key + "yadd")
            for fc in range(NFC):
                P.op("scalar", lambda e, fc=fc, par=par: e.activation(
                    out=seg(go_sl[par], fc), in_=seg(go_sl[par], fc), func=AF.Silu),
                    reads=[f"go_{par}_{fc}"], writes=[f"go_{par}_{fc}"], tag=key + "gosilu")
            for fc in range(NFC):
                P.op("vector", lambda e, fc=fc: e.scalar_tensor_tensor(
                    out=seg(mslab, fc % 8), in0=seg(uconv, fc), scalar=dcolv[:, fc:fc + 1],
                    in1=seg(y_sl, fc), op0=ALU.mult, op1=ALU.add),
                    reads=[f"uc_{fc}", "y_sl", "dcolv"], writes=[f"t2_{fc}", "mslab"],
                    tag=key + "du2")
                P.op("vector", lambda e, fc=fc, par=par: e.tensor_tensor(
                    out=seg(y_sl, fc), in0=seg(mslab, fc % 8), in1=seg(go_sl[par], fc), op=ALU.mult),
                    reads=[f"t2_{fc}", f"go_{par}_{fc}"], writes=["y_sl", f"y2_{fc}"],
                    tag=key + "y2")

        def stage_tail_a(b, blk):
            t0 = blk * TB
            key = f"C{b}_{blk}"
            for sub in range(2):
                for nh in range(2):
                    pf, pfK = ps_fin[sub][nh]
                    P.op("tensor", lambda e, sub=sub, nh=nh, pf=pf: e.matmul(
                        pf, lhsT=ones[64:65, 0:128],
                        rhs=bo_row[:, nh * 512:(nh + 1) * 512],
                        start=True, stop=False, skip_group_check=True),
                        reads=["ones", "bo_row"], writes=pfK, tag=key + "bomm")
            for sub in range(2):
                pa, paK = ps_tpa[sub]
                P.op("tensor", lambda e, sub=sub, pa=pa: e.transpose(
                    pa, aC[:, sub * NSLOT:(sub + 1) * NSLOT], ident[:]),
                    reads=["aC", "ident"], writes=paK)
                P.op("vector", lambda e, sub=sub, pa=pa: e.tensor_copy(
                    aT_sb[:, sub * 128:(sub + 1) * 128], pa),
                    reads=paK, writes=[f"aT{sub}"])
            for m in range(8):
                pf0, pf0K = ps_f0[m % 4]
                P.op("tensor", lambda e, m=m, b=b, pf0=pf0: e.matmul(
                    pf0, lhsT=sv_sb[b][:, m * 128:(m + 1) * 128], rhs=aT_sb,
                    start=True, stop=True, skip_group_check=True),
                    reads=[f"sv{b}", "aT0", "aT1"], writes=pf0K, tag=key + "f0mm")
                P.op("vector", lambda e, m=m, pf0=pf0: e.tensor_copy(
                    f0T_sb[:, m * TB:(m + 1) * TB], pf0),
                    reads=pf0K, writes=[f"dl_{m}"])
            for k in range(8):
                wb, wbK = wslots[k % 4]
                P.op("sync", lambda e, k=k, wb=wb: e.dma_start(
                    out=wb, in_=wo_d[k * 128:(k + 1) * 128, :]),
                    writes=[wbK], dma=True, tag=key + "wodma")
                for sub in range(2):
                    for nh in range(2):
                        pf, pfK = ps_fin[sub][nh]
                        P.op("tensor", lambda e, k=k, sub=sub, nh=nh, wb=wb, pf=pf: e.matmul(
                            pf, lhsT=f0T_sb[:, k * TB + sub * 128: k * TB + sub * 128 + 128],
                            rhs=wb[:, nh * 512:(nh + 1) * 512],
                            start=False, stop=False, skip_group_check=True),
                            reads=[wbK, f"dl_{k}"], writes=pfK,
                            tag=key + "womm")

        def stage_tail_b(b, blk, gn):
            t0 = blk * TB
            key = f"C{b}_{blk}"
            if gn is not None:
                gate_head(*gn)
                for m in range(NFC):
                    gate_dm(*gn, m)
            for c in range(NFC):
                wb, wbK2 = wslots[c % 4]
                P.op("sync", lambda e, c=c, wb=wb: e.dma_start(
                    out=wb, in_=w_outT[c * 128:(c + 1) * 128, :]),
                    writes=[wbK2], dma=True, tag=key + "woutdma")
                for sub in range(2):
                    for nh in range(2):
                        pf, pfK = ps_fin[sub][nh]
                        P.op("tensor", lambda e, c=c, sub=sub, nh=nh, wb=wb, pf=pf: e.matmul(
                            pf, lhsT=seg(y_sl, c)[:, sub * 128:sub * 128 + 128],
                            rhs=wb[:, nh * 512:(nh + 1) * 512],
                            start=False, stop=(c == NFC - 1), skip_group_check=True),
                            reads=[wbK2, f"y2_{c}", "y_sl"], writes=pfK,
                            tag=key + "woutmm")
            if gn is not None:
                gate_du(*gn)

        def stage_fin(b, blk):
            t0 = blk * TB
            key = f"C{b}_{blk}"
            for sub in range(2):
                for nh in range(2):
                    pf, pfK = ps_fin[sub][nh]
                    idx = sub * 2 + nh
                    dst = fin_d[:, idx * 512:(idx + 1) * 512]
                    P.op("vector", lambda e, pf=pf, dst=dst: e.tensor_copy(dst, pf),
                         reads=pfK, writes=["finP", "mslab"], tag=key + "finevac")
            P.op("gpsimd", lambda e, b=b, t0=t0: e.dma_start(
                out=out_hbm[b, t0:t0 + 128, :],
                in_=fin_d[:, 0:1024], accum_op=ALU.add),
                reads=["finP", "mslab"], dma=True, tag=key + "acc0", qsem="acc0")
            P.op("gpsimd", lambda e, b=b, t0=t0: e.dma_start(
                out=out_hbm[b, t0 + 128:t0 + 256, :],
                in_=fin_d[:, 1024:2048], accum_op=ALU.add),
                reads=["finP", "mslab"], dma=True, tag=key + "acc1", qsem="acc1")

        nblk_c = LIMIT_BLOCKS or NBLK
        for b in range(BPC):
            P.op("vector", lambda e, b=b: e.memset(carry[b][:], 0.0),
                 writes=[f"ca_{b}_{s}_{fc}" for s in range(NS) for fc in range(NFC)])
        seqC = [(b, blk) for b in range(BPC) for blk in range(nblk_c)]
        stage_front(*seqC[0])
        stage_mid_grp(*seqC[0], 0, range(32))
        stage_mid_post(*seqC[0])
        stage_gate(*seqC[0])
        prev = None
        for i, cur in enumerate(seqC):
            nxt = seqC[i + 1] if i + 1 < len(seqC) else None
            mn = (nxt + ((i + 1) % 2,)) if nxt else None
            stage_scan(*cur, i % 2, mn, nxt, prev)
            stage_tail_a(*cur)
            if nxt:
                stage_mid_post(*nxt)
            stage_tail_b(*cur, nxt)
            prev = cur
        stage_fin(*seqC[-1])
        sbp.pop()

        if DBG_DUMP is not None:
            c0, c1 = DBG_DUMP
            dbg_t = dt("dbg", [128, c1 - c0], F32, kind="ExternalOutput").ap()
            P.barrier()
            P.op("sync", lambda e: e.dma_start(out=dbg_t, in_=sbp.raw(0, 128, c0, c1)),
                 dma=True, qsem="dbgdump")
        P.emit(blk_ctx, mksem, final_eng="sync")
    return nc


_NC_CACHE = {}


def _get_nc():
    if "nc" not in _NC_CACHE:
        _NC_CACHE["nc"] = build_program()
    return _NC_CACHE["nc"]


def kernel(**inp):
    def c(a):
        return np.ascontiguousarray(np.asarray(a, dtype=np.float32))

    x = c(inp["L0_attn_x"])

    def cols(v, n):
        return np.ascontiguousarray(v.reshape(n, 128).T)

    W_in = c(inp["L0_attn_W_in"]); conv_w = c(inp["L0_attn_conv_w"])
    conv_b = c(inp["L0_attn_conv_b"]); W_gate = c(inp["L0_attn_W_gate"])
    W_delta = c(inp["L0_attn_W_delta"]); delta_bias = c(inp["L0_attn_delta_bias"])
    A_log = c(inp["L0_attn_A_log"]); Dv = c(inp["L0_attn_D"])
    W_out = c(inp["L0_attn_W_out"]); slot_init = c(inp["L0_ffn_slot_init"])
    w1 = c(inp["L0_ffn_w1"]); b1 = c(inp["L0_ffn_b1"])
    w2 = c(inp["L0_ffn_w2"]); b2 = c(inp["L0_ffn_b2"])
    tau = c(inp["L0_ffn_tau"]); Wih = c(inp["L0_ffn_Wih"]); Whh = c(inp["L0_ffn_Whh"])
    bih = c(inp["L0_ffn_bih"]); bhh = c(inp["L0_ffn_bhh"])
    wq = c(inp["L0_ffn_wq"]); bq = c(inp["L0_ffn_bq"])
    wk = c(inp["L0_ffn_wk"]); bk = c(inp["L0_ffn_bk"])
    wv = c(inp["L0_ffn_wv"]); bv = c(inp["L0_ffn_bv"])
    ln_g = c(inp["L0_ffn_ln_g"]); ln_b = c(inp["L0_ffn_ln_b"])
    wvp = c(inp["L0_ffn_wvp"]); bvp = c(inp["L0_ffn_bvp"])
    wo = c(inp["L0_ffn_wo"]); bo = c(inp["L0_ffn_bo"])

    common = {
        "w_inT": np.ascontiguousarray(W_in.T.reshape(8, 128, 32, 128).transpose(2, 1, 0, 3).reshape(32, 128, 1024)),
        "conv_w_c": np.ascontiguousarray(conv_w[:, 0, :].reshape(NFC, 128, 4)
                                         .transpose(1, 0, 2).reshape(128, NFC * 4)),
        "conv_b_c": cols(conv_b, NFC),
        "w_gateT": np.ascontiguousarray(W_gate.T),
        "w_deltaT": np.ascontiguousarray(W_delta.T),
        "dbias_c": cols(delta_bias, NFC),
        "alog_c": np.ascontiguousarray(A_log.reshape(NFC, 128, NS)
                                       .transpose(1, 0, 2).reshape(128, NFC * NS)),
        "dvec_c": cols(Dv, NFC),
        "w_outT": np.ascontiguousarray(W_out.T),
        "slot_init": slot_init,
        "slot_initT": np.ascontiguousarray(slot_init.T),
        "w1a": np.ascontiguousarray(w1[:D]),
        "w1b": np.ascontiguousarray(w1[D:]),
        "b1_c": cols(b1, 4),
        "w2": w2,
        "b2_c": np.ascontiguousarray(b2.reshape(NSLOT, 1)),
        "tau": np.ascontiguousarray(tau.reshape(1, 1)),
        "wihT": np.ascontiguousarray(Wih.T),
        "whhT": np.ascontiguousarray(Whh.T),
        "bih_c": cols(bih, 24),
        "bhh_c": cols(bhh, 24),
        "wq": wq, "wk": wk,
        "bqk_c": np.ascontiguousarray(np.concatenate([cols(bq, 2), cols(bk, 2)], axis=1)),
        "wv": wv,
        "bv_r": np.ascontiguousarray(np.broadcast_to(bv, (NSLOT, D))),
        "lng_r": np.ascontiguousarray(np.broadcast_to(ln_g, (NSLOT, D))),
        "lnb_r": np.ascontiguousarray(np.broadcast_to(ln_b, (NSLOT, D))),
        "wvp": wvp,
        "bvp_r": np.ascontiguousarray(np.broadcast_to(bvp, (NSLOT, D))),
        "wo": wo,
        "bo_r": np.ascontiguousarray(bo.reshape(1, D)),
    }

    nc = _get_nc()
    ncore = NCORE if N_CORES_OVERRIDE is None else N_CORES_OVERRIDE
    in_maps = []
    for core in range(ncore):
        m = dict(common)
        m["x"] = np.ascontiguousarray(x[core * BPC:(core + 1) * BPC])
        in_maps.append(m)
    res = run_bass_kernel_spmd(nc, in_maps, core_ids=list(range(ncore)))
    return np.concatenate([r["out"] for r in res.results], axis=0)


if __name__ == "__main__":
    build_program()
    print("built ok")
